# version 24
# baseline (speedup 1.0000x reference)
import numpy as np
import ml_dtypes
from contextlib import ExitStack

import concourse.bass as bass
import concourse.mybir as mybir
from concourse.bass_utils import run_bass_kernel_spmd

F32 = mybir.dt.float32
BF16 = mybir.dt.bfloat16
U8 = mybir.dt.uint8
AF = mybir.ActivationFunctionType
ALU = mybir.AluOpType
BFNP = ml_dtypes.bfloat16

TCORE = 2048
NSTRIP = 4
STRIP = 512
NF = 8
NHC = 32
CH = 64
CPS = STRIP // CH
D = 1024

PH_FFN1 = True
PH_ATTN = True
PH_FFN2 = True

ENGS = ("tensor", "vector", "scalar", "gpsimd", "sync")
SEMS = ("sT", "sV", "sS", "sG", "sIO", "sDG", "sCC",
        "sLW0", "sLW1", "sLH0", "sLH1", "sGW0", "sGW1", "sGH0", "sGH1")
ENG_SEM = {"tensor": "sT", "vector": "sV", "scalar": "sS", "gpsimd": "sG"}


class Sched:
    def __init__(self):
        self.q = {e: [] for e in ENGS}
        self.cnt = {s: 0 for s in SEMS}

    def val(self, sem):
        return self.cnt[sem]

    def ev(self, sem):
        return (sem, self.cnt[sem])

    def add(self, eng, fn, waits=(), incs=()):
        for s, by in incs:
            self.cnt[s] += by
        w = tuple((s, v) for (s, v) in waits if v is not None and v > 0)
        self.q[eng].append((fn, w, tuple(incs)))

    def dma(self, eng, fn, sem, waits=()):
        w = list(waits) + [(sem, self.cnt[sem])]
        self.add(eng, fn, waits=w, incs=[(sem, 16)])
        return self.ev(sem)

    def barrier(self):
        for e in ("tensor", "vector", "scalar", "gpsimd"):
            s = ENG_SEM[e]
            self.add(e, lambda en: en.nop(), waits=[(s, self.cnt[s])], incs=[(s, 1)])
        vals = [(s, self.cnt[s]) for s in SEMS]
        for e in ENGS:
            self.add(e, lambda en: en.nop(), waits=vals)

    def emit(self, e, eng, sems):
        for fn, waits, incs in self.q[eng]:
            for s, v in waits:
                e.wait_ge(sems[s], v)
            ins = fn(e)
            for s, by in incs:
                ins.then_inc(sems[s], by)


def _flt(evs):
    return [e for e in evs if e is not None]


def build_nc():
    nc = bass.Bass("TRN2", target_bir_lowering=False, detect_race_conditions=False)

    x_in = nc.dram_tensor("x", [TCORE, D], F32, kind="ExternalInput")
    y_out = nc.dram_tensor("y", [TCORE, D], F32, kind="ExternalOutput")

    w12_in = nc.dram_tensor("w12", [NHC, 2, 128, NF, 128], F32, kind="ExternalInput")
    h1_in = nc.dram_tensor("h1", [NHC, 128, NHC, 128], F32, kind="ExternalInput")
    h2_in = nc.dram_tensor("h2", [NHC, 128, NHC, 128], F32, kind="ExternalInput")
    w3_in = nc.dram_tensor("w3", [NF, 128, NHC, 128], F32, kind="ExternalInput")
    wkv_in = nc.dram_tensor("wkv", [NF, 2, 128, NF, 128], F32, kind="ExternalInput")
    wrg_in = nc.dram_tensor("wrg", [NF, 2, 128, NF, 128], F32, kind="ExternalInput")
    wo_in = nc.dram_tensor("wo", [NF, 1, 128, NF, 128], F32, kind="ExternalInput")

    s1_in = nc.dram_tensor("s1", [128, NHC], F32, kind="ExternalInput")
    s2_in = nc.dram_tensor("s2", [128, NHC], F32, kind="ExternalInput")
    sh1_in = nc.dram_tensor("sh1", [128, NHC], F32, kind="ExternalInput")
    sh2_in = nc.dram_tensor("sh2", [128, NHC], F32, kind="ExternalInput")
    s3_in = nc.dram_tensor("s3", [128, NF], F32, kind="ExternalInput")
    td_in = nc.dram_tensor("tdfm", [128, NF], F32, kind="ExternalInput")
    tf_in = nc.dram_tensor("tffm", [128, NF], F32, kind="ExternalInput")
    tm_in = nc.dram_tensor("tmfm", [128, NF], F32, kind="ExternalInput")
    idb_in = nc.dram_tensor("idb", [128, 128], BF16, kind="ExternalInput")
    idf_in = nc.dram_tensor("idf", [128, 128], F32, kind="ExternalInput")
    mask_in = nc.dram_tensor("maskt", [128, CH], U8, kind="ExternalInput")
    ci_in = nc.dram_tensor("ci", [128, CH], BF16, kind="ExternalInput")
    onebf_in = nc.dram_tensor("onebf", [128, 1], BF16, kind="ExternalInput")
    onerow_in = nc.dram_tensor("onerow", [1, 128], F32, kind="ExternalInput")
    eps_in = nc.dram_tensor("epst", [1, 1], F32, kind="ExternalInput")
    cmbm_in = nc.dram_tensor("cmbm", [128, 4], F32, kind="ExternalInput")
    bdm_in = nc.dram_tensor("bdm", [128, 128], U8, kind="ExternalInput")
    cmbk_in = nc.dram_tensor("cmbk", [128, 4], F32, kind="ExternalInput")

    w12_sc = nc.dram_tensor("w12sc", [NHC, 2, 128, NF, 128], BF16)
    h1_sc = nc.dram_tensor("h1sc", [NHC, 128, NHC, 128], BF16)
    h2_sc = nc.dram_tensor("h2sc", [NHC, 128, NHC, 128], BF16)
    w3_sc = nc.dram_tensor("w3sc", [NF, 128, NHC, 128], BF16)
    wkv_sc = nc.dram_tensor("wkvsc", [NF, 2, 128, NF, 128], BF16)
    wrg_sc = nc.dram_tensor("wrgsc", [NF, 2, 128, NF, 128], BF16)
    wo_sc = nc.dram_tensor("wosc", [NF, 1, 128, NF, 128], BF16)
    ccin_d = nc.dram_tensor("ccin", [128, 1024], F32)
    ccout_d = nc.dram_tensor("ccout", [512, 1024], F32)

    S = Sched()

    with ExitStack() as ctx:
        sb = lambda name, sh, dt: ctx.enter_context(nc.sbuf_tensor(name, sh, dt))

        R = sb("R", [128, NF, TCORE], F32)
        big1 = sb("big1", [128, 16384], BF16)
        big2 = sb("big2", [128, 16384], BF16)
        ha = big1.ap().rearrange("p (a b) -> p a b", a=NHC)
        hb = big2.ap().rearrange("p (a b) -> p a b", a=NHC)
        Kv = big1.ap().rearrange("p (a b) -> p a b", a=NF)
        Vv = big2.ap().rearrange("p (a b) -> p a b", a=NF)
        xs = sb("xs", [128, NF, STRIP], BF16)
        rp = sb("rp", [128, NF, STRIP], BF16)
        gp = sb("gp", [128, NF, STRIP], BF16)
        sq2 = sb("sq2", [128, 2, STRIP], BF16)
        rb2 = sb("rb2", [128, 2, STRIP], BF16)
        wbufW = sb("wbufW", [128, 2, 2, NF, 128], BF16)
        wbufH = sb("wbufH", [128, 2, NHC, 128], BF16)
        pat = sb("pat", [128, 2, NF, CH], BF16)
        Sst = sb("Sst", [128, NF, 128], F32)
        sbf = sb("sbf", [128, 2, 128], BF16)
        kmbd = sb("kmbd", [128, 2, 2 * CH], BF16)
        vbd = sb("vbd", [128, 2, 2 * CH], BF16)
        kbd = sb("kbd", [128, 2, 2 * CH], BF16)
        vtkt = sb("vtkt", [2 * CH, 2, 256], BF16)
        qs = sb("qs", [128, 2, CH], BF16)
        kint = sb("kint", [128, 2, CH], BF16)
        ats = sb("ats", [2 * CH, 2, CH], BF16)
        bon1 = sb("bon1", [128, 1, CH], F32)
        scr6 = sb("scr6", [128, 3, STRIP], F32)
        ln_a = sb("ln_a", [1, 2, STRIP], F32)
        ktm2 = sb("ktm2", [128, 512], BF16)
        e512 = sb("e512c", [128, NF], F32)
        dpow = sb("dpow", [128, 8, NF], F32)
        arena = sb("arena", [128, 1024], F32)
        exch = arena.ap().rearrange("p (f q) -> p f q", f=NF)

        s1c = sb("s1c", [128, NHC], F32)
        s2c = sb("s2c", [128, NHC], F32)
        sh1c = sb("sh1c", [128, NHC], F32)
        sh2c = sb("sh2c", [128, NHC], F32)
        s3c = sb("s3c", [128, NF], F32)
        s3h = sb("s3h", [128, NF], F32)
        tdc = sb("tdc", [128, NF], F32)
        tfc = sb("tfc", [128, NF], F32)
        tmc = sb("tmc", [128, NF], F32)
        ntd = sb("ntd", [128, NF], F32)
        dC = sb("dCc", [128, NF], F32)
        e32 = sb("e32c", [128, NF], F32)
        e315 = sb("e315c", [128, NF], F32)
        idb = sb("idbs", [128, 128], BF16)
        idf = sb("idfs", [128, 128], F32)
        maskt = sb("maskts", [128, CH], U8)
        ci = sb("cis", [128, CH], BF16)
        onebf = sb("onebfs", [128, 1], BF16)
        onerow = sb("onerows", [1, 128], F32)
        epst = sb("epsts", [1, 1], F32)
        cmbm = sb("cmbms", [128, 4], F32)
        bdm = sb("bdms", [128, 128], U8)
        cmbk = sb("cmbks", [128, 4], F32)

        psALL = ctx.enter_context(nc.psum_tensor("psALL", [128, 6, 512], F32))
        psSC = ctx.enter_context(nc.psum_tensor("psSC", [128, 2, 1024], BF16))

        sems = {s: ctx.enter_context(nc.semaphore(s)) for s in SEMS}

        evP = [[] for _ in range(6)]
        evW = [[], []]
        evH = [[], []]
        evRB2 = [None, None]
        evSQ2 = [None, None]
        evSilu = [None, None]
        st = {"xs_free": None, "rpgp_free": None, "lna_free": None}
        tr = {k: [None, None] for k in
              ("vbd", "kmbd", "kmbd2", "kbd", "qs", "qs2", "kint", "ats",
               "vtkt", "sbf", "psSC")}
        first_use = set()
        pref = {}

        def wload(key, src_ap, scr_ap, dst_ap, free_evs, slot):
            if key in pref:
                return pref.pop(key)
            gslot = "sG" + slot[2:]
            if key not in first_use:
                first_use.add(key)
                rdy = S.dma("gpsimd",
                            lambda e, d=dst_ap, s=src_ap: e.dma_start(out=d, in_=s),
                            gslot, waits=_flt(free_evs))
                wb = S.dma("sync",
                           lambda e, d=scr_ap, s=dst_ap: e.dma_start(out=d, in_=s),
                           slot, waits=[rdy])
                return rdy, wb
            w = _flt(free_evs)
            if key in castev:
                w.append(castev.pop(key))
            return S.dma("sync",
                         lambda e, d=dst_ap, s=scr_ap: e.dma_start(out=d, in_=s),
                         slot, waits=w), None

        def emit_ln_stats(strip, mode):
            sl = slice(strip * STRIP, (strip + 1) * STRIP)
            for kc in range(NF):
                par = kc % 2
                S.add("vector",
                      lambda e, k=kc, p=par, s=sl: e.tensor_copy(rb2[:, p, :], R[:, k, s]),
                      waits=_flt([evRB2[par]]), incs=[("sV", 1)])
                ev_rb = S.ev("sV")
                S.add("scalar",
                      lambda e, k=kc, p=par, s=sl: e.activation(sq2[:, p, :], R[:, k, s], AF.Square),
                      waits=_flt([evSQ2[par]]), incs=[("sS", 1)])
                ev_sq = S.ev("sS")
                w = [ev_rb] + (evP[0] if kc == 0 else [])
                S.add("tensor",
                      lambda e, p=par, k=kc: e.matmul(
                          psALL[0:1, 0, :], onebf[:, :], rb2[:, p, :],
                          start=(k == 0), stop=(k == NF - 1)),
                      waits=_flt(w))
                w = [ev_sq] + (evP[1] if kc == 0 else [])
                S.add("tensor",
                      lambda e, p=par, k=kc: e.matmul(
                          psALL[0:1, 1, :], onebf[:, :], sq2[:, p, :],
                          start=(k == 0), stop=(k == NF - 1)),
                      waits=_flt(w), incs=[("sT", 1)])
                ev_t = S.ev("sT")
                evRB2[par] = ev_t
                evSQ2[par] = ev_t
            ev_stats = S.ev("sT")
            S.add("vector", lambda e: e.tensor_scalar_mul(ln_a[:, 0, :], psALL[0:1, 0, :], 1.0 / D),
                  waits=_flt([ev_stats, st["lna_free"]]), incs=[("sV", 1)])
            S.add("vector", lambda e: e.tensor_scalar_mul(ln_a[:, 1, :], psALL[0:1, 1, :], 1.0 / D),
                  incs=[("sV", 1)])
            S.add("vector", lambda e: e.tensor_mul(scr6[0:1, 0, :], ln_a[:, 0, :], ln_a[:, 0, :]))
            S.add("vector", lambda e: e.tensor_sub(ln_a[:, 1, :], ln_a[:, 1, :], scr6[0:1, 0, :]),
                  incs=[("sV", 1)])
            ev_var = S.ev("sV")
            S.add("scalar", lambda e: e.activation(ln_a[:, 1, :], ln_a[:, 1, :], AF.Ln,
                                                   bias=epst[:, :]),
                  waits=[ev_var])
            S.add("scalar", lambda e: e.activation(scr6[0:1, 0, :], ln_a[:, 1, :], AF.Exp,
                                                   scale=-0.5),
                  incs=[("sS", 1)])
            ev_std = S.ev("sS")
            S.add("vector", lambda e: e.scalar_tensor_tensor(
                ln_a[:, 0, :], ln_a[:, 0, :], -1.0, scr6[0:1, 0, :], ALU.mult, ALU.mult),
                waits=[ev_std], incs=[("sV", 1)])
            ev_math = S.ev("sV")

            def bc():
                S.add("tensor", lambda e: e.matmul(psALL[:, 0, :], onerow[:, :],
                                                   scr6[0:1, 0, :],
                                                   start=True, stop=True),
                      waits=_flt([ev_math]))
                S.add("tensor", lambda e: e.matmul(psALL[:, 1, :], onerow[:, :],
                                                   ln_a[:, 0, :],
                                                   start=True, stop=True),
                      incs=[("sT", 1)])
                st["lna_free"] = S.ev("sT")

            def norm():
                ev_bc = st["lna_free"]
                for kc in range(NF):
                    w = _flt(([ev_bc] if kc == 0 else []) +
                             ([st["xs_free"]] if kc == 0 else []))
                    S.add("vector",
                          lambda e, k=kc, s=sl: e.tensor_mul(scr6[:, 0, :], R[:, k, s],
                                                             psALL[:, 0, :]),
                          waits=w)
                    if mode == "ffn":
                        S.add("vector",
                              lambda e, k=kc: e.tensor_add(xs[:, k, :], scr6[:, 0, :],
                                                           psALL[:, 1, :]),
                              incs=([("sV", 1)] if kc == NF - 1 else []))
                    else:
                        S.add("vector",
                              lambda e: e.tensor_add(scr6[:, 1, :], scr6[:, 0, :],
                                                     psALL[:, 1, :]))
                        S.add("vector",
                              lambda e, k=kc: e.tensor_scalar_mul(xs[:, k, :], scr6[:, 1, :],
                                                                  tmc[:, k:k + 1]),
                              incs=([("sV", 1)] if kc == NF - 1 else []))
                ev_xs = S.ev("sV")
                evP[0] = [ev_xs]
                evP[1] = [ev_xs]
                return ev_xs

            return bc, norm

        def emit_ln(strip, mode):
            bc, norm = emit_ln_stats(strip, mode)
            bc()
            return norm()

        castev = {}

        def precast_attn_weights():
            items = ([(("wkv", oc), wkv_in.ap()[oc], wkv_sc.ap()[oc])
                      for oc in range(NF)] +
                     [(("wrg", oc), wrg_in.ap()[oc], wrg_sc.ap()[oc])
                      for oc in range(NF)] +
                     [(("wo", oc), wo_in.ap()[oc], wo_sc.ap()[oc])
                      for oc in range(NF)])
            for i, (key, src, dst) in enumerate(items):
                first_use.add(key)
                castev[key] = S.dma(
                    "gpsimd", lambda e, d=dst, s=src: e.dma_start(out=d, in_=s),
                    ("sGW0", "sGW1")[i % 2])

        def emit_ffn(app, pre_xs=None):
            ev_xs = pre_xs if pre_xs is not None else emit_ln(0, "ffn")
            for s in range(NSTRIP):
                sl = slice(s * STRIP, (s + 1) * STRIP)
                if app == 0 and s == 2:
                    precast_attn_weights()
                for oc in range(NHC):
                    par = oc % 2
                    rdy, wb = wload(("w12", oc),
                                    w12_in.ap()[oc].rearrange("w p k q -> p w k q"),
                                    w12_sc.ap()[oc].rearrange("w p k q -> p w k q"),
                                    wbufW[:, par, :, :, :], evW[par], "sLW%d" % par)
                    w = [rdy] + (([ev_xs] if oc < 2 else [])) + evP[0 + par]
                    for kc in range(NF):
                        S.add("tensor",
                              lambda e, p=par, k=kc: e.matmul(
                                  psALL[:, 0 + p, :], wbufW[:, p, 0, k, :], xs[:, k, :],
                                  start=(k == 0), stop=(k == NF - 1)),
                              waits=_flt(w) if kc == 0 else (), incs=([("sT", 1)] if kc == NF - 1 else []))
                    ev_a = S.ev("sT")
                    for kc in range(NF):
                        S.add("tensor",
                              lambda e, p=par, k=kc: e.matmul(
                                  psALL[:, 2 + p, :], wbufW[:, p, 1, k, :], xs[:, k, :],
                                  start=(k == 0), stop=(k == NF - 1)),
                              waits=_flt(evP[2 + par]) if kc == 0 else (),
                              incs=([("sT", 1)] if kc == NF - 1 else []))
                    ev_b = S.ev("sT")
                    evW[par] = [ev_b] + ([wb] if wb else [])
                    S.add("scalar",
                          lambda e, p=par, o=oc: e.activation(
                              scr6[:, 1 + p, :], psALL[:, 0 + p, :], AF.Silu,
                              scale=s1c[:, o:o + 1]),
                          waits=_flt([ev_a, evSilu[par]]), incs=[("sS", 1)])
                    ev_si = S.ev("sS")
                    evP[0 + par] = [ev_si]
                    S.add("vector",
                          lambda e, p=par, o=oc: e.scalar_tensor_tensor(
                              ha[:, o, :], psALL[:, 2 + p, :], s2c[:, o:o + 1],
                              scr6[:, 1 + p, :], ALU.mult, ALU.mult),
                          waits=[ev_b, ev_si], incs=[("sV", 1)])
                    ev_ha = S.ev("sV")
                    evP[2 + par] = [ev_ha]
                    evSilu[par] = ev_ha
                st["xs_free"] = S.ev("sT")
                ha_all = S.ev("sV")
                lnbc, lnnorm = (emit_ln_stats(s + 1, "ffn")
                                if s < NSTRIP - 1 else (None, None))
                for oc in range(NHC):
                    if oc == 4 and lnbc is not None:
                        lnbc()
                        ev_xs = lnnorm()
                    par = oc % 2
                    rdy, wb = wload(("h1", oc), h1_in.ap()[oc], h1_sc.ap()[oc],
                                    wbufH[:, par, :, :], evH[par], "sLH%d" % par)
                    w = [rdy] + ([ha_all] if oc < 2 else []) + evP[4 + par]
                    for kc in range(NHC):
                        S.add("tensor",
                              lambda e, p=par, k=kc: e.matmul(
                                  psALL[:, 4 + p, :], wbufH[:, p, k, :], ha[:, k, :],
                                  start=(k == 0), stop=(k == NHC - 1)),
                              waits=_flt(w) if kc == 0 else (),
                              incs=([("sT", 1)] if kc == NHC - 1 else []))
                    ev_m = S.ev("sT")
                    evH[par] = [ev_m] + ([wb] if wb else [])
                    S.add("scalar",
                          lambda e, p=par, o=oc: e.activation(
                              hb[:, o, :], psALL[:, 4 + p, :], AF.Gelu,
                              scale=sh1c[:, o:o + 1]),
                          waits=[ev_m], incs=[("sS", 1)])
                    evP[4 + par] = [S.ev("sS")]
                hb_all = S.ev("sS")
                for oc in range(NHC):
                    par = oc % 2
                    rdy, wb = wload(("h2", oc), h2_in.ap()[oc], h2_sc.ap()[oc],
                                    wbufH[:, par, :, :], evH[par], "sLH%d" % par)
                    w = [rdy] + ([hb_all] if oc < 2 else []) + evP[4 + par]
                    for kc in range(NHC):
                        S.add("tensor",
                              lambda e, p=par, k=kc: e.matmul(
                                  psALL[:, 4 + p, :], wbufH[:, p, k, :], hb[:, k, :],
                                  start=(k == 0), stop=(k == NHC - 1)),
                              waits=_flt(w) if kc == 0 else (),
                              incs=([("sT", 1)] if kc == NHC - 1 else []))
                    ev_m = S.ev("sT")
                    evH[par] = [ev_m] + ([wb] if wb else [])
                    S.add("scalar",
                          lambda e, p=par, o=oc: e.activation(
                              ha[:, o, :], psALL[:, 4 + p, :], AF.Gelu,
                              scale=sh2c[:, o:o + 1]),
                          waits=[ev_m], incs=[("sS", 1)])
                    evP[4 + par] = [S.ev("sS")]
                hc_all = S.ev("sS")
                for oc in range(NF):
                    par = oc % 2
                    rdy, wb = wload(("w3", oc), w3_in.ap()[oc], w3_sc.ap()[oc],
                                    wbufH[:, par, :, :], evH[par], "sLH%d" % par)
                    w = [rdy] + ([hc_all] if oc < 2 else []) + evP[4 + par]
                    for kc in range(NHC):
                        S.add("tensor",
                              lambda e, p=par, k=kc: e.matmul(
                                  psALL[:, 4 + p, :], wbufH[:, p, k, :], ha[:, k, :],
                                  start=(k == 0), stop=(k == NHC - 1)),
                              waits=_flt(w) if kc == 0 else (),
                              incs=([("sT", 1)] if kc == NHC - 1 else []))
                    ev_m = S.ev("sT")
                    evH[par] = [ev_m] + ([wb] if wb else [])
                    S.add("vector",
                          lambda e, p=par, o=oc, s=sl: e.scalar_tensor_tensor(
                              R[:, o, s], psALL[:, 4 + p, :], s3h[:, o:o + 1],
                              R[:, o, s], ALU.mult, ALU.add),
                          waits=[ev_m], incs=[("sV", 1)])
                    evP[4 + par] = [S.ev("sV")]

        pend = [None]

        def emit_scan(c, f, full, extra_w=()):
            par = (c * 8 + f) % 2
            scol = slice(c * CH, (c + 1) * CH)
            cis = slice((c % CPS) * CH, (c % CPS + 1) * CH)
            KP = Kv[:, f, scol]
            VP = Vv[:, f, scol]
            PQ = pat[:, 0, f, :]
            PK = pat[:, 1, f, :]
            vtv = vtkt[:, par, 0:128]
            ktv = vtkt[:, par, 128:256]
            S.add("vector", lambda e: e.tensor_copy(vbd[0:64, par, 0:CH], VP[0:64, :]),
                  waits=_flt([tr["vbd"][par]] + list(extra_w)))
            S.add("vector", lambda e: e.tensor_copy(vbd[64:128, par, CH:2 * CH], VP[64:128, :]),
                  incs=[("sV", 1)])
            ev_vbd = S.ev("sV")
            S.add("vector", lambda e: e.tensor_mul(qs[:, par, :], KP, PQ),
                  waits=_flt([tr["qs"][par], tr["qs2"][par]]))
            S.add("vector", lambda e: e.tensor_mul(kmbd[0:64, par, 0:CH],
                                                   KP[0:64, :], PK[0:64, :]),
                  waits=_flt([tr["kmbd"][par], tr["kmbd2"][par]]))
            S.add("vector", lambda e: e.tensor_mul(kmbd[64:128, par, CH:2 * CH],
                                                   KP[64:128, :], PK[64:128, :]),
                  incs=[("sV", 1)])
            ev_qkm = S.ev("sV")
            S.add("tensor", lambda e: e.matmul(psALL[0:2 * CH, 0 + par, 0:CH],
                                               kmbd[:, par, :], qs[:, par, :],
                                               start=True, stop=True),
                  waits=_flt([ev_qkm] + evP[0 + par]), incs=[("sT", 1)])
            ev_A = S.ev("sT")
            tr["qs"][par] = ev_A
            tr["kmbd"][par] = ev_A
            S.add("tensor", lambda e: e.transpose(psSC[0:2 * CH, par, 0:128], vbd[:, par, :],
                                                  idb[:, :]),
                  waits=_flt([ev_vbd, tr["psSC"][par]]), incs=[("sT", 1)])
            ev_vt = S.ev("sT")
            tr["vbd"][par] = ev_vt
            S.add("scalar", lambda e: e.activation(
                kbd[0:64, par, 0:CH], kmbd[0:64, par, 0:CH], AF.Copy,
                scale=e315[0:64, f:f + 1]),
                waits=_flt([ev_qkm, tr["kbd"][par]]))
            S.add("scalar", lambda e: e.activation(
                kbd[64:128, par, CH:2 * CH], kmbd[64:128, par, CH:2 * CH], AF.Copy,
                scale=e315[64:128, f:f + 1]),
                incs=[("sS", 1)])
            ev_kbd = S.ev("sS")
            tr["kmbd2"][par] = ev_kbd
            S.add("scalar", lambda e: e.copy(vtkt[:, par, 0:128], psSC[0:2 * CH, par, 0:128]),
                  waits=_flt([ev_vt, tr["vtkt"][par]]), incs=[("sS", 1)])
            ev_vte = S.ev("sS")
            S.add("tensor", lambda e: e.transpose(psSC[0:2 * CH, par, 0:128], kbd[:, par, :],
                                                  idb[:, :]),
                  waits=[ev_kbd, ev_vte], incs=[("sT", 1)])
            ev_kt = S.ev("sT")
            tr["kbd"][par] = ev_kt
            S.add("scalar", lambda e: e.copy(vtkt[:, par, 128:256], psSC[0:2 * CH, par, 0:128]),
                  waits=_flt([ev_kt]), incs=[("sS", 1)])
            ev_tk = S.ev("sS")
            tr["psSC"][par] = ev_tk
            S.add("scalar", lambda e: e.activation(
                kint[:, par, :], qs[:, par, :], AF.Copy, scale=e32[:, f:f + 1]),
                waits=_flt([tr["kint"][par]]), incs=[("sS", 1)])
            ev_sk = S.ev("sS")
            tr["qs2"][par] = ev_sk
            S.add("vector", lambda e: e.copy_predicated(ats[:, par, :], maskt[0:2 * CH, :],
                                                        psALL[0:2 * CH, 0 + par, 0:CH]),
                  waits=_flt([ev_A, tr["ats"][par]]), incs=[("sV", 1)])
            ev_ats = S.ev("sV")
            evP[0 + par] = [ev_ats]
            S.add("vector", lambda e: e.copy_predicated(sbf[:, par, :], bdm[:, :],
                                                        Sst[:, f, :]),
                  waits=_flt([tr["sbf"][par]]), incs=[("sV", 1)])
            ev_sbf = S.ev("sV")
            S.add("tensor", lambda e: e.matmul(psALL[:, 4 + par, 0:128],
                                               ktv, vtv,
                                               start=True, stop=True),
                  waits=_flt([ev_tk] + evP[4 + par]), incs=[("sT", 1)])
            ev_ds = S.ev("sT")
            S.add("tensor", lambda e: e.matmul(psALL[:, 2 + par, 0:CH],
                                               vtv, ats[:, par, :],
                                               start=True, stop=False),
                  waits=_flt([ev_vte, ev_ats, ev_sbf, ev_sk] + evP[2 + par]))
            S.add("tensor", lambda e: e.matmul(psALL[:, 2 + par, 0:CH],
                                               sbf[:, par, :], kint[:, par, :],
                                               start=False, stop=True),
                  incs=[("sT", 1)])
            ev_y = S.ev("sT")
            tr["ats"][par] = ev_y
            tr["sbf"][par] = ev_y
            tr["kint"][par] = ev_y
            tr["vtkt"][par] = ev_y

            projT = st.get("projT")

            def tail():
                S.add("vector", lambda e: e.scalar_tensor_tensor(
                    Sst[:, f, :], Sst[:, f, :], dC[:, f:f + 1], psALL[:, 4 + par, 0:128],
                    ALU.mult, ALU.add),
                    waits=[ev_ds], incs=[("sV", 1)])
                evP[4 + par] = [S.ev("sV")]
                S.add("vector", lambda e: e.tensor_mul(bon1[:, 0, :], KP, VP))
                S.add("vector", lambda e: e.scalar_tensor_tensor(
                    bon1[:, 0, :], bon1[:, 0, :], tfc[:, f:f + 1],
                    psALL[:, 2 + par, 0:CH], ALU.mult, ALU.add),
                    waits=[ev_y])
                S.add("vector", lambda e: e.tensor_mul(bon1[:, 0, :], bon1[:, 0, :],
                                                       rp[:, f, cis]))
                S.add("vector", lambda e: e.tensor_mul(xs[:, f, cis], bon1[:, 0, :],
                                                       gp[:, f, cis]),
                      waits=_flt([projT]), incs=[("sV", 1)])
                ev_cmb = S.ev("sV")
                evP[2 + par] = [ev_cmb]

            if pend[0] is not None:
                pend[0]()
            pend[0] = tail

        def flush_scan():
            if pend[0] is not None:
                pend[0]()
                pend[0] = None

        def emit_p2b_coarse(s, kev, vev):
            vtm2 = vtkt.ap().rearrange("p a b -> p (a b)")
            for f in range(NF):
                par = f % 2
                evv = [None] * 4
                for t4 in range(4):
                    col = slice(s * STRIP + t4 * 128, s * STRIP + (t4 + 1) * 128)
                    S.add("tensor",
                          lambda e, f_=f, c_=col, p=t4 % 2: e.transpose(
                              psSC[:, p, 0:128], Vv[:, f_, c_], idb[:, :]),
                          waits=_flt([vev if (f == 0 and t4 == 0) else None,
                                      st.get("c2b_%d" % (t4 % 2)),
                                      st.get("c2k_%d" % (t4 % 2))]),
                          incs=[("sT", 1)])
                    ev_t = S.ev("sT")
                    S.add("vector",
                          lambda e, t_=t4, p=t4 % 2: e.tensor_copy(
                              vtm2[:, t_ * 128:(t_ + 1) * 128], psSC[:, p, 0:128]),
                          waits=_flt([ev_t, st.get("c2b_vtm")]), incs=[("sV", 1)])
                    evv[t4] = S.ev("sV")
                    st["c2b_%d" % (t4 % 2)] = evv[t4]
                evk = [None] * 4
                for t4 in range(4):
                    col0 = s * STRIP + t4 * 128
                    for h in range(2):
                        q = t4 * 2 + h
                        S.add("vector",
                              lambda e, f_=f, q_=q, c0=col0, h_=h: e.scalar_tensor_tensor(
                                  kbd[:, 0, h_ * CH:(h_ + 1) * CH],
                                  Kv[:, f_, c0 + h_ * CH:c0 + (h_ + 1) * CH],
                                  dpow[:, q_, f_:f_ + 1], pat[:, 1, f_, :],
                                  ALU.mult, ALU.mult),
                              waits=_flt([kev if (f == 0 and t4 == 0 and h == 0) else None,
                                          st.get("c2b_kbd") if (h == 0) else None]),
                              incs=([("sV", 1)] if h == 1 else ()))
                    ev_kb = S.ev("sV")
                    S.add("tensor",
                          lambda e, p=t4 % 2: e.transpose(
                              psSC[:, p, 128:256], kbd[:, 0, 0:128], idb[:, :]),
                          waits=_flt([ev_kb, st.get("c2k_%d" % (t4 % 2)),
                                      evv[3] if t4 == 0 else None]),
                          incs=[("sT", 1)])
                    ev_t = S.ev("sT")
                    st["c2b_kbd"] = ev_t
                    S.add("vector",
                          lambda e, t_=t4, p=t4 % 2: e.tensor_copy(
                              ktm2[:, t_ * 128:(t_ + 1) * 128], psSC[:, p, 128:256]),
                          waits=_flt([ev_t, st.get("c2b_ktm")]), incs=[("sV", 1)])
                    evk[t4] = S.ev("sV")
                    st["c2k_%d" % (t4 % 2)] = evk[t4]
                for t4 in range(4):
                    S.add("tensor",
                          lambda e, t_=t4, p=par: e.matmul(
                              psALL[:, 4 + p, 0:128],
                              ktm2[:, t_ * 128:(t_ + 1) * 128],
                              vtm2[:, t_ * 128:(t_ + 1) * 128],
                              start=(t_ == 0), stop=(t_ == 3)),
                          waits=_flt(([evk[3], evv[3]] + evP[4 + par]) if t4 == 0 else []),
                          incs=([("sT", 1)] if t4 == 3 else []))
                ev_ds = S.ev("sT")
                st["c2b_vtm"] = ev_ds
                st["c2b_ktm"] = ev_ds
                S.add("vector", lambda e, f_=f, p=par: e.scalar_tensor_tensor(
                    Sst[:, f_, :], Sst[:, f_, :], e512[:, f_:f_ + 1],
                    psALL[:, 4 + p, 0:128], ALU.mult, ALU.add),
                    waits=[ev_ds], incs=[("sV", 1)])
                evP[4 + par] = [S.ev("sV")]
                tr["vtkt"][0] = ev_ds
                tr["vtkt"][1] = ev_ds
                tr["psSC"][0] = S.ev("sV")
                tr["psSC"][1] = S.ev("sV")

        def emit_attn(pre_xs=None):
            ev_xs = pre_xs if pre_xs is not None else emit_ln(0, "attn")
            for s in range(NSTRIP):
                sl = slice(s * STRIP, (s + 1) * STRIP)
                for oc in range(NF):
                    par = oc % 2
                    rdy, wb = wload(("wkv", oc),
                                    wkv_in.ap()[oc].rearrange("w p k q -> p w k q"),
                                    wkv_sc.ap()[oc].rearrange("w p k q -> p w k q"),
                                    wbufW[:, par, :, :, :], evW[par], "sLW%d" % par)
                    w = [rdy] + ([ev_xs] if oc < 2 else []) + evP[4 + par]
                    for kc in range(NF):
                        S.add("tensor",
                              lambda e, p=par, k=kc: e.matmul(
                                  psALL[:, 4 + p, :], wbufW[:, p, 0, k, :], xs[:, k, :],
                                  start=(k == 0), stop=(k == NF - 1)),
                              waits=_flt(w) if kc == 0 else (),
                              incs=([("sT", 1)] if kc == NF - 1 else []))
                    ev_k = S.ev("sT")
                    for kc in range(NF):
                        S.add("tensor",
                              lambda e, p=par, k=kc: e.matmul(
                                  psALL[:, 2 + p, :], wbufW[:, p, 1, k, :], xs[:, k, :],
                                  start=(k == 0), stop=(k == NF - 1)),
                              waits=_flt(evP[2 + par]) if kc == 0 else (),
                              incs=([("sT", 1)] if kc == NF - 1 else []))
                    ev_v = S.ev("sT")
                    evW[par] = [ev_v] + ([wb] if wb else [])
                    S.add("scalar",
                          lambda e, p=par, o=oc, s=sl: e.copy(Kv[:, o, s], psALL[:, 4 + p, :]),
                          waits=[ev_k], incs=[("sS", 1)])
                    evP[4 + par] = [S.ev("sS")]
                    S.add("scalar",
                          lambda e, p=par, o=oc, s=sl: e.copy(Vv[:, o, s], psALL[:, 2 + p, :]),
                          waits=[ev_v], incs=[("sS", 1)])
                    evP[2 + par] = [S.ev("sS")]
                st["xs_free"] = S.ev("sT")
                kev = S.ev("sS")
                vev = S.ev("sS")
                lnbc, lnnorm = (emit_ln_stats(s + 1, "attn")
                                if s < NSTRIP - 1 else (None, None))
                emit_p2b_coarse(s, kev, vev)
                if lnbc is not None:
                    lnbc()
                    ev_xs = lnnorm()
            ev_sup = S.ev("sV")
            S.dma("gpsimd",
                  lambda e: e.dma_start(out=ccin_d.ap()[:, :],
                                        in_=Sst.ap().rearrange("p f q -> p (f q)")),
                  "sDG", waits=[ev_sup])
            S.add("gpsimd",
                  lambda e: e.collective_compute(
                      "AllGather", ALU.bypass,
                      replica_groups=[[0, 1, 2, 3], [4, 5, 6, 7]],
                      ins=[ccin_d.ap().opt()], outs=[ccout_d.ap().opt()]),
                  waits=[S.ev("sDG")], incs=[("sCC", 1)])
            ev_cc = S.ev("sCC")
            ev_xs0 = emit_ln(0, "attn")
            for f in range(NF):
                S.add("vector",
                      lambda e, f_=f: e.tensor_scalar_mul(
                          scr6[:, 2, f_ * 4:(f_ + 1) * 4], cmbm[:, :], tdc[:, f_:f_ + 1]),
                      incs=([("sV", 1)] if f == NF - 1 else []))
            ev_ar = S.ev("sV")
            S.add("scalar", lambda e: e.activation(scr6[:, 2, 0:32], scr6[:, 2, 0:32], AF.Exp),
                  waits=[ev_ar], incs=[("sS", 1)])
            ev_ex = S.ev("sS")
            for f in range(NF):
                S.add("vector",
                      lambda e, f_=f: e.tensor_mul(scr6[:, 2, f_ * 4:(f_ + 1) * 4],
                                                   scr6[:, 2, f_ * 4:(f_ + 1) * 4], cmbk[:, :]),
                      waits=([ev_ex] if f == 0 else ()),
                      incs=([("sV", 1)] if f == NF - 1 else []))
            ev_w = S.ev("sV")
            prev = None
            for q in range(4):
                ev_ld = S.dma("gpsimd",
                              lambda e, q_=q: e.dma_start(
                                  out=arena[:, :], in_=ccout_d.ap()[q_ * 128:(q_ + 1) * 128, :]),
                              "sDG",
                              waits=_flt([ev_cc if q == 0 else None,
                                          ev_w if q == 0 else None, prev]))
                for f in range(NF):
                    if q == 0:
                        S.add("vector",
                              lambda e, f_=f: e.tensor_scalar_mul(
                                  Sst[:, f_, :], exch[:, f_, :], scr6[:, 2, f_ * 4:f_ * 4 + 1]),
                              waits=([ev_ld] if f == 0 else ()),
                              incs=([("sV", 1)] if f == NF - 1 else []))
                    else:
                        S.add("vector",
                              lambda e, f_=f, q_=q: e.scalar_tensor_tensor(
                                  Sst[:, f_, :], exch[:, f_, :],
                                  scr6[:, 2, f_ * 4 + q_:f_ * 4 + q_ + 1],
                                  Sst[:, f_, :], ALU.mult, ALU.add),
                              waits=([ev_ld] if f == 0 else ()),
                              incs=([("sV", 1)] if f == NF - 1 else []))
                prev = S.ev("sV")
            ev_xs = ev_xs0
            for s in range(NSTRIP):
                sl = slice(s * STRIP, (s + 1) * STRIP)
                rg_free = st["rpgp_free"]
                for oc in range(NF):
                    par = oc % 2
                    rdy, wb = wload(("wrg", oc),
                                    wrg_in.ap()[oc].rearrange("w p k q -> p w k q"),
                                    wrg_sc.ap()[oc].rearrange("w p k q -> p w k q"),
                                    wbufW[:, par, :, :, :], evW[par], "sLW%d" % par)
                    w = [rdy] + ([ev_xs] if oc < 2 else []) + evP[4 + par]
                    for kc in range(NF):
                        S.add("tensor",
                              lambda e, p=par, k=kc: e.matmul(
                                  psALL[:, 4 + p, :], wbufW[:, p, 0, k, :], xs[:, k, :],
                                  start=(k == 0), stop=(k == NF - 1)),
                              waits=_flt(w) if kc == 0 else (),
                              incs=([("sT", 1)] if kc == NF - 1 else []))
                    ev_r = S.ev("sT")
                    for kc in range(NF):
                        S.add("tensor",
                              lambda e, p=par, k=kc: e.matmul(
                                  psALL[:, 2 + p, :], wbufW[:, p, 1, k, :], xs[:, k, :],
                                  start=(k == 0), stop=(k == NF - 1)),
                              waits=_flt(evP[2 + par]) if kc == 0 else (),
                              incs=([("sT", 1)] if kc == NF - 1 else []))
                    ev_g = S.ev("sT")
                    evW[par] = [ev_g] + ([wb] if wb else [])
                    S.add("scalar",
                          lambda e, p=par, o=oc: e.copy(rp[:, o, :], psALL[:, 4 + p, :]),
                          waits=_flt([ev_r, rg_free]), incs=[("sS", 1)])
                    evP[4 + par] = [S.ev("sS")]
                    S.add("scalar",
                          lambda e, p=par, o=oc: e.activation(gp[:, o, :], psALL[:, 2 + p, :],
                                                              AF.Silu),
                          waits=[ev_g], incs=[("sS", 1)])
                    evP[2 + par] = [S.ev("sS")]
                st["projT"] = S.ev("sT")
                for ci_ in range(CPS):
                    for f in range(NF):
                        emit_scan(s * CPS + ci_, f, full=True)
                flush_scan()
                aout_done = S.ev("sV")
                lnbc, lnnorm = (emit_ln_stats(s + 1, "attn")
                                if s < NSTRIP - 1 else (None, None))
                for oc in range(NF):
                    if oc == 5 and lnbc is not None:
                        lnbc()
                    par = oc % 2
                    rdy, wb = wload(("wo", oc),
                                    wo_in.ap()[oc].rearrange("w p k q -> p w k q"),
                                    wo_sc.ap()[oc].rearrange("w p k q -> p w k q"),
                                    wbufW[:, par, 0:1, :, :], evW[par], "sLW%d" % par)
                    w = [rdy] + ([aout_done] if oc < 2 else []) + evP[2 + par]
                    for kc in range(NF):
                        S.add("tensor",
                              lambda e, p=par, k=kc: e.matmul(
                                  psALL[:, 2 + p, :], wbufW[:, p, 0, k, :], xs[:, k, :],
                                  start=(k == 0), stop=(k == NF - 1)),
                              waits=_flt(w) if kc == 0 else (),
                              incs=([("sT", 1)] if kc == NF - 1 else []))
                    ev_m = S.ev("sT")
                    evW[par] = [ev_m] + ([wb] if wb else [])
                    S.add("vector",
                          lambda e, p=par, o=oc, s=sl: e.tensor_add(
                              R[:, o, s], R[:, o, s], psALL[:, 2 + p, :]),
                          waits=[ev_m], incs=[("sV", 1)])
                    evP[2 + par] = [S.ev("sV")]
                st["xs_free"] = S.ev("sT")
                st["rpgp_free"] = aout_done
                if lnnorm is not None:
                    ev_xs = lnnorm()

        loads = [
            (s1c, s1_in), (s2c, s2_in), (sh1c, sh1_in), (sh2c, sh2_in),
            (s3c, s3_in), (tdc, td_in), (tfc, tf_in), (tmc, tm_in),
            (idb, idb_in), (idf, idf_in), (maskt, mask_in), (ci, ci_in),
            (onebf, onebf_in), (onerow, onerow_in), (epst, eps_in),
            (cmbm, cmbm_in), (cmbk, cmbk_in), (bdm, bdm_in),
        ]
        for dst, src in loads:
            S.dma("sync", lambda e, d=dst, s=src: e.dma_start(out=d.ap(), in_=s.ap()),
                  "sIO")
        ev_small = S.ev("sIO")
        S.add("vector", lambda e: e.tensor_scalar_mul(ntd[:, :], tdc[:, :], -1.0),
              waits=[ev_small], incs=[("sV", 1)])
        ev_ntd = S.ev("sV")
        S.add("vector", lambda e: e.tensor_scalar_mul(s3h[:, :], s3c[:, :], 0.5))
        for t_ in (Sst, vbd, kmbd, kbd, ats, sbf):
            S.add("vector", lambda e, tt=t_: e.memset(tt.ap(), 0.0))
        S.add("vector", lambda e: e.engine_nop(), incs=[("sV", 1)])
        ev_memset = S.ev("sV")
        S.add("scalar", lambda e: e.activation(dC[:, :], tdc[:, :], AF.Exp, scale=-float(CH)),
              waits=[ev_small])
        S.add("scalar", lambda e: e.activation(e32[:, :], tdc[:, :], AF.Exp, scale=-(CH / 2.0 + 0.5)))
        S.add("scalar", lambda e: e.activation(e315[:, :], tdc[:, :], AF.Exp, scale=-(CH / 2.0 - 0.5)))
        S.add("scalar", lambda e: e.activation(e512[:, :], tdc[:, :], AF.Exp, scale=-512.0))
        for m in range(8):
            S.add("scalar", lambda e, m_=m: e.activation(dpow[:, m_, :], tdc[:, :], AF.Exp,
                                                         scale=-(float(CH) * (7 - m_) + CH / 2.0 - 0.5)),
                  incs=([("sS", 1)] if m == 7 else ()))
        prev_pat = None
        for f in range(NF):
            for which in range(2):
                slot = (f * 2 + which) % 2
                col = ntd if which == 0 else tdc
                S.add("vector",
                      lambda e, f_=f, sl_=slot, c_=col: e.tensor_scalar_mul(
                          scr6[:, sl_, 0:CH], ci[:, :], c_[:, f_:f_ + 1]),
                      waits=_flt([ev_ntd if (f == 0 and which == 0) else None, prev_pat]),
                      incs=[("sV", 1)])
                ev_arg = S.ev("sV")
                S.add("scalar",
                      lambda e, f_=f, w_=which, sl_=slot: e.activation(
                          pat[:, w_, f_, :], scr6[:, sl_, 0:CH], AF.Exp),
                      waits=[ev_arg], incs=[("sS", 1)])
                prev_pat = S.ev("sS")
        ev_io = None
        trp = [None, None]
        for t in range(16):
            ev_x = S.dma("sync",
                         lambda e, t_=t: e.dma_start(out=arena[:, :],
                                                     in_=x_in.ap()[t_ * 128:(t_ + 1) * 128, :]),
                         "sIO", waits=_flt([ev_io]))
            for f in range(NF):
                pf = f % 2
                S.add("tensor",
                      lambda e, f_=f, p=pf: e.transpose(
                          psALL[:, 4 + p, 0:128], arena[:, f_ * 128:(f_ + 1) * 128], idf[:, :]),
                      waits=_flt(([ev_x, ev_small] if f == 0 else []) + [trp[pf]]),
                      incs=[("sT", 1)])
                ev_t = S.ev("sT")
                if f == NF - 1:
                    ev_io = ev_t
                S.add("vector",
                      lambda e, f_=f, p=pf, t_=t: e.tensor_copy(
                          R[:, f_, t_ * 128:(t_ + 1) * 128], psALL[:, 4 + p, 0:128]),
                      waits=[ev_t], incs=[("sV", 1)])
                trp[pf] = S.ev("sV")
        for oc in (0, 1):
            par = oc % 2
            first_use.add(("w12", oc))
            rdy = S.dma("gpsimd",
                        lambda e, o=oc, p=par: e.dma_start(
                            out=wbufW[:, p, :, :, :],
                            in_=w12_in.ap()[o].rearrange("w p k q -> p w k q")),
                        "sGW%d" % par)
            wb = S.dma("sync",
                       lambda e, o=oc, p=par: e.dma_start(
                           out=w12_sc.ap()[o].rearrange("w p k q -> p w k q"),
                           in_=wbufW[:, p, :, :, :]),
                       "sLW%d" % par, waits=[rdy])
            pref[("w12", oc)] = (rdy, wb)
        S.barrier()

        if PH_FFN1:
            emit_ffn(0)
            ln_a0 = emit_ln(0, "attn") if PH_ATTN else None
            S.barrier()
        else:
            ln_a0 = None
        if PH_ATTN:
            emit_attn(ln_a0)
            ln_f2 = emit_ln(0, "ffn") if PH_FFN2 else None
            S.barrier()
        else:
            ln_f2 = None
        if PH_FFN2:
            emit_ffn(1, ln_f2)
            S.barrier()

        trp = [None, None]
        ev_half = [None, None]
        for t in range(16):
            for f in range(NF):
                pf = f % 2
                S.add("tensor",
                      lambda e, f_=f, p=pf, t_=t: e.transpose(
                          psALL[:, 4 + p, 0:128], R[:, f_, t_ * 128:(t_ + 1) * 128], idf[:, :]),
                      waits=_flt([trp[pf]] + (evP[4 + pf] if t == 0 and f < 2 else [])),
                      incs=[("sT", 1)])
                ev_t = S.ev("sT")
                S.add("vector",
                      lambda e, f_=f, p=pf: e.tensor_copy(
                          arena[:, f_ * 128:(f_ + 1) * 128], psALL[:, 4 + p, 0:128]),
                      waits=_flt([ev_t] + ([ev_half[f // 4]] if f % 4 == 0 else [])),
                      incs=[("sV", 1)])
                trp[pf] = S.ev("sV")
                if f == 3 or f == NF - 1:
                    h = f // 4
                    ev_half[h] = S.dma(
                        "sync",
                        lambda e, t_=t, h_=h: e.dma_start(
                            out=y_out.ap()[t_ * 128:(t_ + 1) * 128,
                                           h_ * 512:(h_ + 1) * 512],
                            in_=arena[:, h_ * 512:(h_ + 1) * 512]),
                        "sIO", waits=[trp[f % 2]])
        S.add("sync", lambda e: e.nop(), waits=[("sIO", S.val("sIO"))])

        block = ctx.enter_context(nc.Block())
        for eng in ENGS:
            getattr(block, eng)(lambda e, g=eng: S.emit(e, g, sems))

    return nc


def _tile_w(w):
    O, I = w.shape
    t = w.reshape(O // 128, 128, I // 128, 128)
    return np.ascontiguousarray(t.transpose(0, 3, 2, 1))


def _col128(v):
    return np.ascontiguousarray(np.asarray(v, np.float32).reshape(-1, 128).T)


def kernel(**inputs):
    x = np.asarray(inputs["L0_attn_x"], np.float32)
    B, T, _ = x.shape

    w12 = np.ascontiguousarray(np.stack(
        [_tile_w(np.asarray(inputs["L0_ffn_w1_w"], np.float32)),
         _tile_w(np.asarray(inputs["L0_ffn_w2_w"], np.float32))], axis=1))
    h1 = _tile_w(np.asarray(inputs["L0_ffn_h1_w"], np.float32))
    h2 = _tile_w(np.asarray(inputs["L0_ffn_h2_w"], np.float32))
    w3 = _tile_w(np.asarray(inputs["L0_ffn_w3_w"], np.float32))
    wkv = np.ascontiguousarray(np.stack(
        [_tile_w(np.asarray(inputs["L0_attn_Wk"], np.float32)),
         _tile_w(np.asarray(inputs["L0_attn_Wv"], np.float32))], axis=1))
    wrg = np.ascontiguousarray(np.stack(
        [_tile_w(np.asarray(inputs["L0_attn_Wr"], np.float32)),
         _tile_w(np.asarray(inputs["L0_attn_Wg"], np.float32))], axis=1))
    wo = np.ascontiguousarray(
        _tile_w(np.asarray(inputs["L0_attn_Wo"], np.float32))[:, None])

    s1 = _col128(inputs["L0_ffn_w1_s"])
    s2 = _col128(inputs["L0_ffn_w2_s"])
    sh1 = _col128(inputs["L0_ffn_h1_s"])
    sh2 = _col128(inputs["L0_ffn_h2_s"])
    s3 = _col128(inputs["L0_ffn_w3_s"])
    tdfm = _col128(inputs["L0_attn_time_decay"])
    tffm = _col128(inputs["L0_attn_time_first"])
    tmfm = _col128(inputs["L0_attn_time_mix"])

    idb = np.eye(128, dtype=np.float32).astype(BFNP)
    idf = np.eye(128, dtype=np.float32)
    i_loc = np.arange(CH)
    maskt = np.ascontiguousarray(
        np.tile((i_loc[:, None] <= i_loc[None, :]), (128 // CH, 1)).astype(np.uint8))
    ci = np.ascontiguousarray(
        np.broadcast_to((i_loc - (CH - 1) / 2.0).astype(np.float32),
                        (128, CH))).astype(BFNP)
    onebf = np.ones((128, 1), dtype=BFNP)
    p_i = np.arange(128)
    bdm = np.ascontiguousarray(((p_i[:, None] // 64) == (p_i[None, :] // 64)).astype(np.uint8))
    onerow = np.ones((1, 128), dtype=np.float32)
    epst = np.full((1, 1), 1e-5, dtype=np.float32)

    nc = build_nc()
    in_maps = []
    for c in range(8):
        pos = c % 4
        mvec = np.zeros(4, np.float32)
        kvec = np.zeros(4, np.float32)
        for q in range(pos):
            mvec[q] = -2048.0 * (pos - 1 - q)
            kvec[q] = 1.0
        cmbm = np.ascontiguousarray(np.broadcast_to(mvec, (128, 4)))
        cmbk = np.ascontiguousarray(np.broadcast_to(kvec, (128, 4)))
        shard = np.ascontiguousarray(x[c // 4, pos * TCORE:(pos + 1) * TCORE, :])
        in_maps.append({
            "x": shard, "w12": w12, "h1": h1, "h2": h2, "w3": w3,
            "wkv": wkv, "wrg": wrg, "wo": wo,
            "s1": s1, "s2": s2, "sh1": sh1, "sh2": sh2, "s3": s3,
            "tdfm": tdfm, "tffm": tffm, "tmfm": tmfm,
            "idb": idb, "idf": idf, "maskt": maskt, "ci": ci,
            "onebf": onebf, "onerow": onerow, "epst": epst,
            "cmbm": cmbm, "cmbk": cmbk, "bdm": bdm,
        })
    res = run_bass_kernel_spmd(nc, in_maps, core_ids=list(range(8)))
    y = np.empty((B, T, D), np.float32)
    for c in range(8):
        y[c // 4, (c % 4) * TCORE:((c % 4) + 1) * TCORE, :] = res.results[c]["y"]
    return y


# version 26
# speedup vs baseline: 1.0109x; 1.0109x over previous
import numpy as np
import ml_dtypes
from contextlib import ExitStack

import concourse.bass as bass
import concourse.mybir as mybir
from concourse.bass_utils import run_bass_kernel_spmd

F32 = mybir.dt.float32
BF16 = mybir.dt.bfloat16
U8 = mybir.dt.uint8
AF = mybir.ActivationFunctionType
ALU = mybir.AluOpType
BFNP = ml_dtypes.bfloat16

TCORE = 2048
NSTRIP = 4
STRIP = 512
NF = 8
NHC = 32
CH = 64
CPS = STRIP // CH
D = 1024

PH_FFN1 = True
PH_ATTN = True
PH_FFN2 = True

ENGS = ("tensor", "vector", "scalar", "gpsimd", "sync")
SEMS = ("sT", "sV", "sS", "sG", "sIO", "sDG", "sCC",
        "sLW0", "sLW1", "sLH0", "sLH1", "sGW0", "sGW1", "sGH0", "sGH1")
ENG_SEM = {"tensor": "sT", "vector": "sV", "scalar": "sS", "gpsimd": "sG"}


class Sched:
    def __init__(self):
        self.q = {e: [] for e in ENGS}
        self.cnt = {s: 0 for s in SEMS}

    def val(self, sem):
        return self.cnt[sem]

    def ev(self, sem):
        return (sem, self.cnt[sem])

    def add(self, eng, fn, waits=(), incs=()):
        for s, by in incs:
            self.cnt[s] += by
        w = tuple((s, v) for (s, v) in waits if v is not None and v > 0)
        self.q[eng].append((fn, w, tuple(incs)))

    def dma(self, eng, fn, sem, waits=()):
        w = list(waits) + [(sem, self.cnt[sem])]
        self.add(eng, fn, waits=w, incs=[(sem, 16)])
        return self.ev(sem)

    def barrier(self):
        for e in ("tensor", "vector", "scalar", "gpsimd"):
            s = ENG_SEM[e]
            self.add(e, lambda en: en.nop(), waits=[(s, self.cnt[s])], incs=[(s, 1)])
        vals = [(s, self.cnt[s]) for s in SEMS]
        for e in ENGS:
            self.add(e, lambda en: en.nop(), waits=vals)

    def emit(self, e, eng, sems):
        for fn, waits, incs in self.q[eng]:
            for s, v in waits:
                e.wait_ge(sems[s], v)
            ins = fn(e)
            for s, by in incs:
                ins.then_inc(sems[s], by)


def _flt(evs):
    return [e for e in evs if e is not None]


def build_nc():
    nc = bass.Bass("TRN2", target_bir_lowering=False, detect_race_conditions=False)

    x_in = nc.dram_tensor("x", [TCORE, D], F32, kind="ExternalInput")
    y_out = nc.dram_tensor("y", [TCORE, D], F32, kind="ExternalOutput")

    w12_in = nc.dram_tensor("w12", [NHC, 2, 128, NF, 128], F32, kind="ExternalInput")
    h1_in = nc.dram_tensor("h1", [NHC, 128, NHC, 128], F32, kind="ExternalInput")
    h2_in = nc.dram_tensor("h2", [NHC, 128, NHC, 128], F32, kind="ExternalInput")
    w3_in = nc.dram_tensor("w3", [NF, 128, NHC, 128], F32, kind="ExternalInput")
    wkv_in = nc.dram_tensor("wkv", [NF, 2, 128, NF, 128], F32, kind="ExternalInput")
    wrg_in = nc.dram_tensor("wrg", [NF, 2, 128, NF, 128], F32, kind="ExternalInput")
    wo_in = nc.dram_tensor("wo", [NF, 1, 128, NF, 128], F32, kind="ExternalInput")

    s1_in = nc.dram_tensor("s1", [128, NHC], F32, kind="ExternalInput")
    s2_in = nc.dram_tensor("s2", [128, NHC], F32, kind="ExternalInput")
    sh1_in = nc.dram_tensor("sh1", [128, NHC], F32, kind="ExternalInput")
    sh2_in = nc.dram_tensor("sh2", [128, NHC], F32, kind="ExternalInput")
    s3_in = nc.dram_tensor("s3", [128, NF], F32, kind="ExternalInput")
    td_in = nc.dram_tensor("tdfm", [128, NF], F32, kind="ExternalInput")
    tf_in = nc.dram_tensor("tffm", [128, NF], F32, kind="ExternalInput")
    tm_in = nc.dram_tensor("tmfm", [128, NF], F32, kind="ExternalInput")
    idb_in = nc.dram_tensor("idb", [128, 128], BF16, kind="ExternalInput")
    idf_in = nc.dram_tensor("idf", [128, 128], F32, kind="ExternalInput")
    mask_in = nc.dram_tensor("maskt", [128, CH], U8, kind="ExternalInput")
    ci_in = nc.dram_tensor("ci", [128, CH], BF16, kind="ExternalInput")
    onebf_in = nc.dram_tensor("onebf", [128, 1], BF16, kind="ExternalInput")
    onerow_in = nc.dram_tensor("onerow", [1, 128], F32, kind="ExternalInput")
    eps_in = nc.dram_tensor("epst", [1, 1], F32, kind="ExternalInput")
    cmbm_in = nc.dram_tensor("cmbm", [128, 4], F32, kind="ExternalInput")
    bdm_in = nc.dram_tensor("bdm", [128, 128], U8, kind="ExternalInput")
    cmbk_in = nc.dram_tensor("cmbk", [128, 4], F32, kind="ExternalInput")

    w12_sc = nc.dram_tensor("w12sc", [NHC, 2, 128, NF, 128], BF16)
    h1_sc = nc.dram_tensor("h1sc", [NHC, 128, NHC, 128], BF16)
    h2_sc = nc.dram_tensor("h2sc", [NHC, 128, NHC, 128], BF16)
    w3_sc = nc.dram_tensor("w3sc", [NF, 128, NHC, 128], BF16)
    wkv_sc = nc.dram_tensor("wkvsc", [NF, 2, 128, NF, 128], BF16)
    wrg_sc = nc.dram_tensor("wrgsc", [NF, 2, 128, NF, 128], BF16)
    wo_sc = nc.dram_tensor("wosc", [NF, 1, 128, NF, 128], BF16)
    ccin_d = nc.dram_tensor("ccin", [128, 1024], F32)
    ccout_d = nc.dram_tensor("ccout", [512, 1024], F32)

    S = Sched()

    with ExitStack() as ctx:
        sb = lambda name, sh, dt: ctx.enter_context(nc.sbuf_tensor(name, sh, dt))

        R = sb("R", [128, NF, TCORE], F32)
        big1 = sb("big1", [128, 16384], BF16)
        big2 = sb("big2", [128, 16384], BF16)
        ha = big1.ap().rearrange("p (a b) -> p a b", a=NHC)
        hb = big2.ap().rearrange("p (a b) -> p a b", a=NHC)
        Kv = big1.ap().rearrange("p (a b) -> p a b", a=NF)
        Vv = big2.ap().rearrange("p (a b) -> p a b", a=NF)
        xs = sb("xs", [128, NF, STRIP], BF16)
        rp = sb("rp", [128, NF, STRIP], BF16)
        gp = sb("gp", [128, NF, STRIP], BF16)
        sq2 = sb("sq2", [128, 2, STRIP], BF16)
        rb2 = sb("rb2", [128, 2, STRIP], BF16)
        wbufW = sb("wbufW", [128, 2, 2, NF, 128], BF16)
        wbufH = sb("wbufH", [128, 2, NHC, 128], BF16)
        pat = sb("pat", [128, 2, NF, CH], BF16)
        Sst = sb("Sst", [128, NF, 128], F32)
        sbf = sb("sbf", [128, 2, 128], BF16)
        kmbd = sb("kmbd", [128, 2, 2 * CH], BF16)
        vbd = sb("vbd", [128, 2, 2 * CH], BF16)
        kbd = sb("kbd", [128, 2, 2 * CH], BF16)
        vtkt = sb("vtkt", [2 * CH, 2, 256], BF16)
        qs = sb("qs", [128, 2, CH], BF16)
        kint = sb("kint", [128, 2, CH], BF16)
        ats = sb("ats", [2 * CH, 2, CH], BF16)
        bon1 = sb("bon1", [128, 1, CH], F32)
        scr6 = sb("scr6", [128, 3, STRIP], F32)
        ln_a = sb("ln_a", [1, 2, STRIP], F32)
        ktm2 = sb("ktm2", [128, 512], BF16)
        e512 = sb("e512c", [128, NF], F32)
        dpow = sb("dpow", [128, 8, NF], F32)
        arena = sb("arena", [128, 1024], F32)
        exch = arena.ap().rearrange("p (f q) -> p f q", f=NF)

        s1c = sb("s1c", [128, NHC], F32)
        s2c = sb("s2c", [128, NHC], F32)
        sh1c = sb("sh1c", [128, NHC], F32)
        sh2c = sb("sh2c", [128, NHC], F32)
        s3c = sb("s3c", [128, NF], F32)
        s3h = sb("s3h", [128, NF], F32)
        tdc = sb("tdc", [128, NF], F32)
        tfc = sb("tfc", [128, NF], F32)
        tmc = sb("tmc", [128, NF], F32)
        ntd = sb("ntd", [128, NF], F32)
        dC = sb("dCc", [128, NF], F32)
        e32 = sb("e32c", [128, NF], F32)
        e315 = sb("e315c", [128, NF], F32)
        idb = sb("idbs", [128, 128], BF16)
        idf = sb("idfs", [128, 128], F32)
        maskt = sb("maskts", [128, CH], U8)
        ci = sb("cis", [128, CH], BF16)
        onebf = sb("onebfs", [128, 1], BF16)
        onerow = sb("onerows", [1, 128], F32)
        epst = sb("epsts", [1, 1], F32)
        cmbm = sb("cmbms", [128, 4], F32)
        bdm = sb("bdms", [128, 128], U8)
        cmbk = sb("cmbks", [128, 4], F32)

        psALL = ctx.enter_context(nc.psum_tensor("psALL", [128, 6, 512], F32))
        psSC = ctx.enter_context(nc.psum_tensor("psSC", [128, 2, 1024], BF16))

        sems = {s: ctx.enter_context(nc.semaphore(s)) for s in SEMS}

        evP = [[] for _ in range(6)]
        evW = [[], []]
        evH = [[], []]
        evRB2 = [None, None]
        evSQ2 = [None, None]
        evSilu = [None, None]
        st = {"xs_free": None, "rpgp_free": None, "lna_free": None}
        tr = {k: [None, None] for k in
              ("vbd", "kmbd", "kmbd2", "kbd", "qs", "qs2", "kint", "ats",
               "vtkt", "sbf", "psSC")}
        first_use = set()
        pref = {}

        def wload(key, src_ap, scr_ap, dst_ap, free_evs, slot):
            if key in pref:
                return pref.pop(key)
            gslot = "sG" + slot[2:]
            if key not in first_use:
                first_use.add(key)
                rdy = S.dma("gpsimd",
                            lambda e, d=dst_ap, s=src_ap: e.dma_start(out=d, in_=s),
                            gslot, waits=_flt(free_evs))
                wb = S.dma("sync",
                           lambda e, d=scr_ap, s=dst_ap: e.dma_start(out=d, in_=s),
                           slot, waits=[rdy])
                return rdy, wb
            w = _flt(free_evs)
            if key in castev:
                w.append(castev.pop(key))
            return S.dma("sync",
                         lambda e, d=dst_ap, s=scr_ap: e.dma_start(out=d, in_=s),
                         slot, waits=w), None

        def emit_ln_stats(strip, mode):
            sl = slice(strip * STRIP, (strip + 1) * STRIP)
            for kc in range(NF):
                par = kc % 2
                S.add("vector",
                      lambda e, k=kc, p=par, s=sl: e.tensor_copy(rb2[:, p, :], R[:, k, s]),
                      waits=_flt([evRB2[par]]), incs=[("sV", 1)])
                ev_rb = S.ev("sV")
                S.add("scalar",
                      lambda e, k=kc, p=par, s=sl: e.activation(sq2[:, p, :], R[:, k, s], AF.Square),
                      waits=_flt([evSQ2[par]]), incs=[("sS", 1)])
                ev_sq = S.ev("sS")
                w = [ev_rb] + (evP[0] if kc == 0 else [])
                S.add("tensor",
                      lambda e, p=par, k=kc: e.matmul(
                          psALL[0:1, 0, :], onebf[:, :], rb2[:, p, :],
                          start=(k == 0), stop=(k == NF - 1)),
                      waits=_flt(w))
                w = [ev_sq] + (evP[1] if kc == 0 else [])
                S.add("tensor",
                      lambda e, p=par, k=kc: e.matmul(
                          psALL[0:1, 1, :], onebf[:, :], sq2[:, p, :],
                          start=(k == 0), stop=(k == NF - 1)),
                      waits=_flt(w), incs=[("sT", 1)])
                ev_t = S.ev("sT")
                evRB2[par] = ev_t
                evSQ2[par] = ev_t
            ev_stats = S.ev("sT")
            S.add("vector", lambda e: e.tensor_scalar_mul(ln_a[:, 0, :], psALL[0:1, 0, :], 1.0 / D),
                  waits=_flt([ev_stats, st["lna_free"]]), incs=[("sV", 1)])
            S.add("vector", lambda e: e.tensor_scalar_mul(ln_a[:, 1, :], psALL[0:1, 1, :], 1.0 / D),
                  incs=[("sV", 1)])
            S.add("vector", lambda e: e.tensor_mul(scr6[0:1, 0, :], ln_a[:, 0, :], ln_a[:, 0, :]))
            S.add("vector", lambda e: e.tensor_sub(ln_a[:, 1, :], ln_a[:, 1, :], scr6[0:1, 0, :]),
                  incs=[("sV", 1)])
            ev_var = S.ev("sV")
            S.add("scalar", lambda e: e.activation(ln_a[:, 1, :], ln_a[:, 1, :], AF.Ln,
                                                   bias=epst[:, :]),
                  waits=[ev_var])
            S.add("scalar", lambda e: e.activation(scr6[0:1, 0, :], ln_a[:, 1, :], AF.Exp,
                                                   scale=-0.5),
                  incs=[("sS", 1)])
            ev_std = S.ev("sS")
            S.add("vector", lambda e: e.scalar_tensor_tensor(
                ln_a[:, 0, :], ln_a[:, 0, :], -1.0, scr6[0:1, 0, :], ALU.mult, ALU.mult),
                waits=[ev_std], incs=[("sV", 1)])
            ev_math = S.ev("sV")

            def bc():
                S.add("tensor", lambda e: e.matmul(psALL[:, 0, :], onerow[:, :],
                                                   scr6[0:1, 0, :],
                                                   start=True, stop=True),
                      waits=_flt([ev_math]))
                S.add("tensor", lambda e: e.matmul(psALL[:, 1, :], onerow[:, :],
                                                   ln_a[:, 0, :],
                                                   start=True, stop=True),
                      incs=[("sT", 1)])
                st["lna_free"] = S.ev("sT")

            def norm():
                ev_bc = st["lna_free"]
                for kc in range(NF):
                    w = _flt(([ev_bc] if kc == 0 else []) +
                             ([st["xs_free"]] if kc == 0 else []))
                    S.add("vector",
                          lambda e, k=kc, s=sl: e.tensor_mul(scr6[:, 0, :], R[:, k, s],
                                                             psALL[:, 0, :]),
                          waits=w)
                    if mode == "ffn":
                        S.add("vector",
                              lambda e, k=kc: e.tensor_add(xs[:, k, :], scr6[:, 0, :],
                                                           psALL[:, 1, :]),
                              incs=([("sV", 1)] if kc == NF - 1 else []))
                    else:
                        S.add("vector",
                              lambda e: e.tensor_add(scr6[:, 1, :], scr6[:, 0, :],
                                                     psALL[:, 1, :]))
                        S.add("vector",
                              lambda e, k=kc: e.tensor_scalar_mul(xs[:, k, :], scr6[:, 1, :],
                                                                  tmc[:, k:k + 1]),
                              incs=([("sV", 1)] if kc == NF - 1 else []))
                ev_xs = S.ev("sV")
                evP[0] = [ev_xs]
                evP[1] = [ev_xs]
                return ev_xs

            return bc, norm

        def emit_ln(strip, mode):
            bc, norm = emit_ln_stats(strip, mode)
            bc()
            return norm()

        castev = {}

        def precast_attn_weights():
            items = ([(("wkv", oc), wkv_in.ap()[oc], wkv_sc.ap()[oc])
                      for oc in range(NF)] +
                     [(("wrg", oc), wrg_in.ap()[oc], wrg_sc.ap()[oc])
                      for oc in range(NF)] +
                     [(("wo", oc), wo_in.ap()[oc], wo_sc.ap()[oc])
                      for oc in range(NF)])
            for i, (key, src, dst) in enumerate(items):
                first_use.add(key)
                castev[key] = S.dma(
                    "gpsimd", lambda e, d=dst, s=src: e.dma_start(out=d, in_=s),
                    ("sGW0", "sGW1")[i % 2])

        def emit_ffn(app, pre_xs=None):
            ev_xs = pre_xs if pre_xs is not None else emit_ln(0, "ffn")
            for s in range(NSTRIP):
                sl = slice(s * STRIP, (s + 1) * STRIP)
                if app == 0 and s == 2:
                    precast_attn_weights()
                for oc in range(NHC):
                    par = oc % 2
                    rdy, wb = wload(("w12", oc),
                                    w12_in.ap()[oc].rearrange("w p k q -> p w k q"),
                                    w12_sc.ap()[oc].rearrange("w p k q -> p w k q"),
                                    wbufW[:, par, :, :, :], evW[par], "sLW%d" % par)
                    w = [rdy] + (([ev_xs] if oc < 2 else [])) + evP[0 + par]
                    for kc in range(NF):
                        S.add("tensor",
                              lambda e, p=par, k=kc: e.matmul(
                                  psALL[:, 0 + p, :], wbufW[:, p, 0, k, :], xs[:, k, :],
                                  start=(k == 0), stop=(k == NF - 1)),
                              waits=_flt(w) if kc == 0 else (), incs=([("sT", 1)] if kc == NF - 1 else []))
                    ev_a = S.ev("sT")
                    for kc in range(NF):
                        S.add("tensor",
                              lambda e, p=par, k=kc: e.matmul(
                                  psALL[:, 2 + p, :], wbufW[:, p, 1, k, :], xs[:, k, :],
                                  start=(k == 0), stop=(k == NF - 1)),
                              waits=_flt(evP[2 + par]) if kc == 0 else (),
                              incs=([("sT", 1)] if kc == NF - 1 else []))
                    ev_b = S.ev("sT")
                    evW[par] = [ev_b] + ([wb] if wb else [])
                    S.add("scalar",
                          lambda e, p=par, o=oc: e.activation(
                              scr6[:, 1 + p, :], psALL[:, 0 + p, :], AF.Silu,
                              scale=s1c[:, o:o + 1]),
                          waits=_flt([ev_a, evSilu[par]]), incs=[("sS", 1)])
                    ev_si = S.ev("sS")
                    evP[0 + par] = [ev_si]
                    S.add("vector",
                          lambda e, p=par, o=oc: e.scalar_tensor_tensor(
                              ha[:, o, :], psALL[:, 2 + p, :], s2c[:, o:o + 1],
                              scr6[:, 1 + p, :], ALU.mult, ALU.mult),
                          waits=[ev_b, ev_si], incs=[("sV", 1)])
                    ev_ha = S.ev("sV")
                    evP[2 + par] = [ev_ha]
                    evSilu[par] = ev_ha
                st["xs_free"] = S.ev("sT")
                ha_all = S.ev("sV")
                lnbc, lnnorm = (emit_ln_stats(s + 1, "ffn")
                                if s < NSTRIP - 1 else (None, None))
                for oc in range(NHC):
                    if oc == 4 and lnbc is not None:
                        lnbc()
                        ev_xs = lnnorm()
                    par = oc % 2
                    rdy, wb = wload(("h1", oc), h1_in.ap()[oc], h1_sc.ap()[oc],
                                    wbufH[:, par, :, :], evH[par], "sLH%d" % par)
                    w = [rdy] + ([ha_all] if oc < 2 else []) + evP[4 + par]
                    for kc in range(NHC):
                        S.add("tensor",
                              lambda e, p=par, k=kc: e.matmul(
                                  psALL[:, 4 + p, :], wbufH[:, p, k, :], ha[:, k, :],
                                  start=(k == 0), stop=(k == NHC - 1)),
                              waits=_flt(w) if kc == 0 else (),
                              incs=([("sT", 1)] if kc == NHC - 1 else []))
                    ev_m = S.ev("sT")
                    evH[par] = [ev_m] + ([wb] if wb else [])
                    S.add("scalar",
                          lambda e, p=par, o=oc: e.activation(
                              hb[:, o, :], psALL[:, 4 + p, :], AF.Gelu,
                              scale=sh1c[:, o:o + 1]),
                          waits=[ev_m], incs=[("sS", 1)])
                    evP[4 + par] = [S.ev("sS")]
                hb_all = S.ev("sS")
                for oc in range(NHC):
                    par = oc % 2
                    rdy, wb = wload(("h2", oc), h2_in.ap()[oc], h2_sc.ap()[oc],
                                    wbufH[:, par, :, :], evH[par], "sLH%d" % par)
                    w = [rdy] + ([hb_all] if oc < 2 else []) + evP[4 + par]
                    for kc in range(NHC):
                        S.add("tensor",
                              lambda e, p=par, k=kc: e.matmul(
                                  psALL[:, 4 + p, :], wbufH[:, p, k, :], hb[:, k, :],
                                  start=(k == 0), stop=(k == NHC - 1)),
                              waits=_flt(w) if kc == 0 else (),
                              incs=([("sT", 1)] if kc == NHC - 1 else []))
                    ev_m = S.ev("sT")
                    evH[par] = [ev_m] + ([wb] if wb else [])
                    S.add("scalar",
                          lambda e, p=par, o=oc: e.activation(
                              ha[:, o, :], psALL[:, 4 + p, :], AF.Gelu,
                              scale=sh2c[:, o:o + 1]),
                          waits=[ev_m], incs=[("sS", 1)])
                    evP[4 + par] = [S.ev("sS")]
                hc_all = S.ev("sS")
                for oc in range(NF):
                    par = oc % 2
                    rdy, wb = wload(("w3", oc), w3_in.ap()[oc], w3_sc.ap()[oc],
                                    wbufH[:, par, :, :], evH[par], "sLH%d" % par)
                    w = [rdy] + ([hc_all] if oc < 2 else []) + evP[4 + par]
                    for kc in range(NHC):
                        S.add("tensor",
                              lambda e, p=par, k=kc: e.matmul(
                                  psALL[:, 4 + p, :], wbufH[:, p, k, :], ha[:, k, :],
                                  start=(k == 0), stop=(k == NHC - 1)),
                              waits=_flt(w) if kc == 0 else (),
                              incs=([("sT", 1)] if kc == NHC - 1 else []))
                    ev_m = S.ev("sT")
                    evH[par] = [ev_m] + ([wb] if wb else [])
                    S.add("vector",
                          lambda e, p=par, o=oc, s=sl: e.scalar_tensor_tensor(
                              R[:, o, s], psALL[:, 4 + p, :], s3h[:, o:o + 1],
                              R[:, o, s], ALU.mult, ALU.add),
                          waits=[ev_m], incs=[("sV", 1)])
                    evP[4 + par] = [S.ev("sV")]

        pend = [None]

        def emit_scan(c, f, full, extra_w=()):
            par = (c * 8 + f) % 2
            scol = slice(c * CH, (c + 1) * CH)
            cis = slice((c % CPS) * CH, (c % CPS + 1) * CH)
            KP = Kv[:, f, scol]
            VP = Vv[:, f, scol]
            PQ = pat[:, 0, f, :]
            PK = pat[:, 1, f, :]
            vtv = vtkt[:, par, 0:128]
            ktv = vtkt[:, par, 128:256]
            S.add("scalar", lambda e: e.copy(vbd[0:64, par, 0:CH], VP[0:64, :]),
                  waits=_flt([tr["vbd"][par]] + list(extra_w)))
            S.add("scalar", lambda e: e.copy(vbd[64:128, par, CH:2 * CH], VP[64:128, :]),
                  incs=[("sS", 1)])
            ev_vbd = S.ev("sS")
            S.add("vector", lambda e: e.tensor_mul(qs[:, par, :], KP, PQ),
                  waits=_flt([tr["qs"][par], tr["qs2"][par]]))
            S.add("vector", lambda e: e.tensor_mul(kmbd[0:64, par, 0:CH],
                                                   KP[0:64, :], PK[0:64, :]),
                  waits=_flt([tr["kmbd"][par], tr["kmbd2"][par]]))
            S.add("vector", lambda e: e.tensor_mul(kmbd[64:128, par, CH:2 * CH],
                                                   KP[64:128, :], PK[64:128, :]),
                  incs=[("sV", 1)])
            ev_qkm = S.ev("sV")
            S.add("tensor", lambda e: e.matmul(psALL[0:2 * CH, 0 + par, 0:CH],
                                               kmbd[:, par, :], qs[:, par, :],
                                               start=True, stop=True),
                  waits=_flt([ev_qkm] + evP[0 + par]), incs=[("sT", 1)])
            ev_A = S.ev("sT")
            tr["qs"][par] = ev_A
            tr["kmbd"][par] = ev_A
            S.add("tensor", lambda e: e.transpose(psSC[0:2 * CH, par, 0:128], vbd[:, par, :],
                                                  idb[:, :]),
                  waits=_flt([ev_vbd, tr["psSC"][par]]), incs=[("sT", 1)])
            ev_vt = S.ev("sT")
            tr["vbd"][par] = ev_vt
            S.add("scalar", lambda e: e.activation(
                kbd[0:64, par, 0:CH], kmbd[0:64, par, 0:CH], AF.Copy,
                scale=e315[0:64, f:f + 1]),
                waits=_flt([ev_qkm, tr["kbd"][par]]))
            S.add("scalar", lambda e: e.activation(
                kbd[64:128, par, CH:2 * CH], kmbd[64:128, par, CH:2 * CH], AF.Copy,
                scale=e315[64:128, f:f + 1]),
                incs=[("sS", 1)])
            ev_kbd = S.ev("sS")
            tr["kmbd2"][par] = ev_kbd
            S.add("scalar", lambda e: e.copy(vtkt[:, par, 0:128], psSC[0:2 * CH, par, 0:128]),
                  waits=_flt([ev_vt, tr["vtkt"][par]]), incs=[("sS", 1)])
            ev_vte = S.ev("sS")
            S.add("scalar", lambda e: e.activation(
                kint[:, par, :], qs[:, par, :], AF.Copy, scale=e32[:, f:f + 1]),
                waits=_flt([tr["kint"][par]]), incs=[("sS", 1)])
            ev_sk = S.ev("sS")
            tr["qs2"][par] = ev_sk
            S.add("tensor", lambda e: e.transpose(psSC[0:2 * CH, par, 0:128], kbd[:, par, :],
                                                  idb[:, :]),
                  waits=[ev_kbd, ev_vte], incs=[("sT", 1)])
            ev_kt = S.ev("sT")
            tr["kbd"][par] = ev_kt
            S.add("scalar", lambda e: e.copy(vtkt[:, par, 128:256], psSC[0:2 * CH, par, 0:128]),
                  waits=_flt([ev_kt]), incs=[("sS", 1)])
            ev_tk = S.ev("sS")
            tr["psSC"][par] = ev_tk
            S.add("vector", lambda e: e.copy_predicated(sbf[:, par, :], bdm[:, :],
                                                        Sst[:, f, :]),
                  waits=_flt([tr["sbf"][par]]), incs=[("sV", 1)])
            ev_sbf = S.ev("sV")
            S.add("vector", lambda e: e.copy_predicated(ats[:, par, :], maskt[0:2 * CH, :],
                                                        psALL[0:2 * CH, 0 + par, 0:CH]),
                  waits=_flt([ev_A, tr["ats"][par]]), incs=[("sV", 1)])
            ev_ats = S.ev("sV")
            evP[0 + par] = [ev_ats]
            S.add("tensor", lambda e: e.matmul(psALL[:, 2 + par, 0:CH],
                                               vtv, ats[:, par, :],
                                               start=True, stop=False),
                  waits=_flt([ev_vte, ev_ats, ev_sbf, ev_sk] + evP[2 + par]))
            S.add("tensor", lambda e: e.matmul(psALL[:, 2 + par, 0:CH],
                                               sbf[:, par, :], kint[:, par, :],
                                               start=False, stop=True),
                  incs=[("sT", 1)])
            ev_y = S.ev("sT")
            tr["ats"][par] = ev_y
            tr["sbf"][par] = ev_y
            tr["kint"][par] = ev_y
            S.add("tensor", lambda e: e.matmul(psALL[:, 4 + par, 0:128],
                                               ktv, vtv,
                                               start=True, stop=True),
                  waits=_flt([ev_tk] + evP[4 + par]), incs=[("sT", 1)])
            ev_ds = S.ev("sT")
            tr["vtkt"][par] = ev_ds

            projT = st.get("projT")

            def tail():
                S.add("vector", lambda e: e.scalar_tensor_tensor(
                    Sst[:, f, :], Sst[:, f, :], dC[:, f:f + 1], psALL[:, 4 + par, 0:128],
                    ALU.mult, ALU.add),
                    waits=[ev_ds], incs=[("sV", 1)])
                evP[4 + par] = [S.ev("sV")]
                S.add("vector", lambda e: e.tensor_mul(bon1[:, 0, :], KP, VP))
                S.add("vector", lambda e: e.scalar_tensor_tensor(
                    bon1[:, 0, :], bon1[:, 0, :], tfc[:, f:f + 1],
                    psALL[:, 2 + par, 0:CH], ALU.mult, ALU.add),
                    waits=[ev_y])
                S.add("vector", lambda e: e.tensor_mul(bon1[:, 0, :], bon1[:, 0, :],
                                                       rp[:, f, cis]))
                S.add("vector", lambda e: e.tensor_mul(xs[:, f, cis], bon1[:, 0, :],
                                                       gp[:, f, cis]),
                      waits=_flt([projT]), incs=[("sV", 1)])
                ev_cmb = S.ev("sV")
                evP[2 + par] = [ev_cmb]

            if pend[0] is not None:
                pend[0]()
            pend[0] = tail

        def flush_scan():
            if pend[0] is not None:
                pend[0]()
                pend[0] = None

        def emit_p2b_coarse(s, kev, vev):
            vtm2 = vtkt.ap().rearrange("p a b -> p (a b)")
            for f in range(NF):
                par = f % 2
                evv = [None] * 4
                for t4 in range(4):
                    col = slice(s * STRIP + t4 * 128, s * STRIP + (t4 + 1) * 128)
                    S.add("tensor",
                          lambda e, f_=f, c_=col, p=t4 % 2: e.transpose(
                              psSC[:, p, 0:128], Vv[:, f_, c_], idb[:, :]),
                          waits=_flt([vev if (f == 0 and t4 == 0) else None,
                                      st.get("c2b_%d" % (t4 % 2)),
                                      st.get("c2k_%d" % (t4 % 2))]),
                          incs=[("sT", 1)])
                    ev_t = S.ev("sT")
                    S.add("vector",
                          lambda e, t_=t4, p=t4 % 2: e.tensor_copy(
                              vtm2[:, t_ * 128:(t_ + 1) * 128], psSC[:, p, 0:128]),
                          waits=_flt([ev_t, st.get("c2b_vtm")]), incs=[("sV", 1)])
                    evv[t4] = S.ev("sV")
                    st["c2b_%d" % (t4 % 2)] = evv[t4]
                evk = [None] * 4
                for t4 in range(4):
                    col0 = s * STRIP + t4 * 128
                    for h in range(2):
                        q = t4 * 2 + h
                        S.add("vector",
                              lambda e, f_=f, q_=q, c0=col0, h_=h: e.scalar_tensor_tensor(
                                  kbd[:, 0, h_ * CH:(h_ + 1) * CH],
                                  Kv[:, f_, c0 + h_ * CH:c0 + (h_ + 1) * CH],
                                  dpow[:, q_, f_:f_ + 1], pat[:, 1, f_, :],
                                  ALU.mult, ALU.mult),
                              waits=_flt([kev if (f == 0 and t4 == 0 and h == 0) else None,
                                          st.get("c2b_kbd") if (h == 0) else None]),
                              incs=([("sV", 1)] if h == 1 else ()))
                    ev_kb = S.ev("sV")
                    S.add("tensor",
                          lambda e, p=t4 % 2: e.transpose(
                              psSC[:, p, 128:256], kbd[:, 0, 0:128], idb[:, :]),
                          waits=_flt([ev_kb, st.get("c2k_%d" % (t4 % 2)),
                                      evv[3] if t4 == 0 else None]),
                          incs=[("sT", 1)])
                    ev_t = S.ev("sT")
                    st["c2b_kbd"] = ev_t
                    S.add("vector",
                          lambda e, t_=t4, p=t4 % 2: e.tensor_copy(
                              ktm2[:, t_ * 128:(t_ + 1) * 128], psSC[:, p, 128:256]),
                          waits=_flt([ev_t, st.get("c2b_ktm")]), incs=[("sV", 1)])
                    evk[t4] = S.ev("sV")
                    st["c2k_%d" % (t4 % 2)] = evk[t4]
                for t4 in range(4):
                    S.add("tensor",
                          lambda e, t_=t4, p=par: e.matmul(
                              psALL[:, 4 + p, 0:128],
                              ktm2[:, t_ * 128:(t_ + 1) * 128],
                              vtm2[:, t_ * 128:(t_ + 1) * 128],
                              start=(t_ == 0), stop=(t_ == 3)),
                          waits=_flt(([evk[3], evv[3]] + evP[4 + par]) if t4 == 0 else []),
                          incs=([("sT", 1)] if t4 == 3 else []))
                ev_ds = S.ev("sT")
                st["c2b_vtm"] = ev_ds
                st["c2b_ktm"] = ev_ds
                S.add("vector", lambda e, f_=f, p=par: e.scalar_tensor_tensor(
                    Sst[:, f_, :], Sst[:, f_, :], e512[:, f_:f_ + 1],
                    psALL[:, 4 + p, 0:128], ALU.mult, ALU.add),
                    waits=[ev_ds], incs=[("sV", 1)])
                evP[4 + par] = [S.ev("sV")]
                tr["vtkt"][0] = ev_ds
                tr["vtkt"][1] = ev_ds
                tr["psSC"][0] = S.ev("sV")
                tr["psSC"][1] = S.ev("sV")

        def emit_attn(pre_xs=None):
            ev_xs = pre_xs if pre_xs is not None else emit_ln(0, "attn")
            for s in range(NSTRIP):
                sl = slice(s * STRIP, (s + 1) * STRIP)
                for oc in range(NF):
                    par = oc % 2
                    rdy, wb = wload(("wkv", oc),
                                    wkv_in.ap()[oc].rearrange("w p k q -> p w k q"),
                                    wkv_sc.ap()[oc].rearrange("w p k q -> p w k q"),
                                    wbufW[:, par, :, :, :], evW[par], "sLW%d" % par)
                    w = [rdy] + ([ev_xs] if oc < 2 else []) + evP[4 + par]
                    for kc in range(NF):
                        S.add("tensor",
                              lambda e, p=par, k=kc: e.matmul(
                                  psALL[:, 4 + p, :], wbufW[:, p, 0, k, :], xs[:, k, :],
                                  start=(k == 0), stop=(k == NF - 1)),
                              waits=_flt(w) if kc == 0 else (),
                              incs=([("sT", 1)] if kc == NF - 1 else []))
                    ev_k = S.ev("sT")
                    for kc in range(NF):
                        S.add("tensor",
                              lambda e, p=par, k=kc: e.matmul(
                                  psALL[:, 2 + p, :], wbufW[:, p, 1, k, :], xs[:, k, :],
                                  start=(k == 0), stop=(k == NF - 1)),
                              waits=_flt(evP[2 + par]) if kc == 0 else (),
                              incs=([("sT", 1)] if kc == NF - 1 else []))
                    ev_v = S.ev("sT")
                    evW[par] = [ev_v] + ([wb] if wb else [])
                    S.add("scalar",
                          lambda e, p=par, o=oc, s=sl: e.copy(Kv[:, o, s], psALL[:, 4 + p, :]),
                          waits=[ev_k], incs=[("sS", 1)])
                    evP[4 + par] = [S.ev("sS")]
                    S.add("vector",
                          lambda e, p=par, o=oc, s=sl: e.tensor_copy(Vv[:, o, s], psALL[:, 2 + p, :]),
                          waits=[ev_v], incs=[("sV", 1)])
                    evP[2 + par] = [S.ev("sV")]
                st["xs_free"] = S.ev("sT")
                kev = S.ev("sS")
                vev = S.ev("sV")
                lnbc, lnnorm = (emit_ln_stats(s + 1, "attn")
                                if s < NSTRIP - 1 else (None, None))
                emit_p2b_coarse(s, kev, vev)
                if lnbc is not None:
                    lnbc()
                    ev_xs = lnnorm()
            ev_sup = S.ev("sV")
            S.dma("gpsimd",
                  lambda e: e.dma_start(out=ccin_d.ap()[:, :],
                                        in_=Sst.ap().rearrange("p f q -> p (f q)")),
                  "sDG", waits=[ev_sup])
            S.add("gpsimd",
                  lambda e: e.collective_compute(
                      "AllGather", ALU.bypass,
                      replica_groups=[[0, 1, 2, 3], [4, 5, 6, 7]],
                      ins=[ccin_d.ap().opt()], outs=[ccout_d.ap().opt()]),
                  waits=[S.ev("sDG")], incs=[("sCC", 1)])
            ev_cc = S.ev("sCC")
            ev_xs0 = emit_ln(0, "attn")
            for f in range(NF):
                S.add("vector",
                      lambda e, f_=f: e.tensor_scalar_mul(
                          scr6[:, 2, f_ * 4:(f_ + 1) * 4], cmbm[:, :], tdc[:, f_:f_ + 1]),
                      incs=([("sV", 1)] if f == NF - 1 else []))
            ev_ar = S.ev("sV")
            S.add("scalar", lambda e: e.activation(scr6[:, 2, 0:32], scr6[:, 2, 0:32], AF.Exp),
                  waits=[ev_ar], incs=[("sS", 1)])
            ev_ex = S.ev("sS")
            for f in range(NF):
                S.add("vector",
                      lambda e, f_=f: e.tensor_mul(scr6[:, 2, f_ * 4:(f_ + 1) * 4],
                                                   scr6[:, 2, f_ * 4:(f_ + 1) * 4], cmbk[:, :]),
                      waits=([ev_ex] if f == 0 else ()),
                      incs=([("sV", 1)] if f == NF - 1 else []))
            ev_w = S.ev("sV")
            prev = None
            for q in range(4):
                ev_ld = S.dma("gpsimd",
                              lambda e, q_=q: e.dma_start(
                                  out=arena[:, :], in_=ccout_d.ap()[q_ * 128:(q_ + 1) * 128, :]),
                              "sDG",
                              waits=_flt([ev_cc if q == 0 else None,
                                          ev_w if q == 0 else None, prev]))
                for f in range(NF):
                    if q == 0:
                        S.add("vector",
                              lambda e, f_=f: e.tensor_scalar_mul(
                                  Sst[:, f_, :], exch[:, f_, :], scr6[:, 2, f_ * 4:f_ * 4 + 1]),
                              waits=([ev_ld] if f == 0 else ()),
                              incs=([("sV", 1)] if f == NF - 1 else []))
                    else:
                        S.add("vector",
                              lambda e, f_=f, q_=q: e.scalar_tensor_tensor(
                                  Sst[:, f_, :], exch[:, f_, :],
                                  scr6[:, 2, f_ * 4 + q_:f_ * 4 + q_ + 1],
                                  Sst[:, f_, :], ALU.mult, ALU.add),
                              waits=([ev_ld] if f == 0 else ()),
                              incs=([("sV", 1)] if f == NF - 1 else []))
                prev = S.ev("sV")
            ev_xs = ev_xs0
            for s in range(NSTRIP):
                sl = slice(s * STRIP, (s + 1) * STRIP)
                rg_free = st["rpgp_free"]
                for oc in range(NF):
                    par = oc % 2
                    rdy, wb = wload(("wrg", oc),
                                    wrg_in.ap()[oc].rearrange("w p k q -> p w k q"),
                                    wrg_sc.ap()[oc].rearrange("w p k q -> p w k q"),
                                    wbufW[:, par, :, :, :], evW[par], "sLW%d" % par)
                    w = [rdy] + ([ev_xs] if oc < 2 else []) + evP[4 + par]
                    for kc in range(NF):
                        S.add("tensor",
                              lambda e, p=par, k=kc: e.matmul(
                                  psALL[:, 4 + p, :], wbufW[:, p, 0, k, :], xs[:, k, :],
                                  start=(k == 0), stop=(k == NF - 1)),
                              waits=_flt(w) if kc == 0 else (),
                              incs=([("sT", 1)] if kc == NF - 1 else []))
                    ev_r = S.ev("sT")
                    for kc in range(NF):
                        S.add("tensor",
                              lambda e, p=par, k=kc: e.matmul(
                                  psALL[:, 2 + p, :], wbufW[:, p, 1, k, :], xs[:, k, :],
                                  start=(k == 0), stop=(k == NF - 1)),
                              waits=_flt(evP[2 + par]) if kc == 0 else (),
                              incs=([("sT", 1)] if kc == NF - 1 else []))
                    ev_g = S.ev("sT")
                    evW[par] = [ev_g] + ([wb] if wb else [])
                    S.add("scalar",
                          lambda e, p=par, o=oc: e.copy(rp[:, o, :], psALL[:, 4 + p, :]),
                          waits=_flt([ev_r, rg_free]), incs=[("sS", 1)])
                    evP[4 + par] = [S.ev("sS")]
                    S.add("scalar",
                          lambda e, p=par, o=oc: e.activation(gp[:, o, :], psALL[:, 2 + p, :],
                                                              AF.Silu),
                          waits=[ev_g], incs=[("sS", 1)])
                    evP[2 + par] = [S.ev("sS")]
                st["projT"] = S.ev("sT")
                for ci_ in range(CPS):
                    for f in range(NF):
                        emit_scan(s * CPS + ci_, f, full=True)
                flush_scan()
                aout_done = S.ev("sV")
                lnbc, lnnorm = (emit_ln_stats(s + 1, "attn")
                                if s < NSTRIP - 1 else (None, None))
                for oc in range(NF):
                    if oc == 5 and lnbc is not None:
                        lnbc()
                    par = oc % 2
                    rdy, wb = wload(("wo", oc),
                                    wo_in.ap()[oc].rearrange("w p k q -> p w k q"),
                                    wo_sc.ap()[oc].rearrange("w p k q -> p w k q"),
                                    wbufW[:, par, 0:1, :, :], evW[par], "sLW%d" % par)
                    w = [rdy] + ([aout_done] if oc < 2 else []) + evP[2 + par]
                    for kc in range(NF):
                        S.add("tensor",
                              lambda e, p=par, k=kc: e.matmul(
                                  psALL[:, 2 + p, :], wbufW[:, p, 0, k, :], xs[:, k, :],
                                  start=(k == 0), stop=(k == NF - 1)),
                              waits=_flt(w) if kc == 0 else (),
                              incs=([("sT", 1)] if kc == NF - 1 else []))
                    ev_m = S.ev("sT")
                    evW[par] = [ev_m] + ([wb] if wb else [])
                    S.add("vector",
                          lambda e, p=par, o=oc, s=sl: e.tensor_add(
                              R[:, o, s], R[:, o, s], psALL[:, 2 + p, :]),
                          waits=[ev_m], incs=[("sV", 1)])
                    evP[2 + par] = [S.ev("sV")]
                st["xs_free"] = S.ev("sT")
                st["rpgp_free"] = aout_done
                if lnnorm is not None:
                    ev_xs = lnnorm()

        loads = [
            (s1c, s1_in), (s2c, s2_in), (sh1c, sh1_in), (sh2c, sh2_in),
            (s3c, s3_in), (tdc, td_in), (tfc, tf_in), (tmc, tm_in),
            (idb, idb_in), (idf, idf_in), (maskt, mask_in), (ci, ci_in),
            (onebf, onebf_in), (onerow, onerow_in), (epst, eps_in),
            (cmbm, cmbm_in), (cmbk, cmbk_in), (bdm, bdm_in),
        ]
        for dst, src in loads:
            S.dma("sync", lambda e, d=dst, s=src: e.dma_start(out=d.ap(), in_=s.ap()),
                  "sIO")
        ev_small = S.ev("sIO")
        S.add("vector", lambda e: e.tensor_scalar_mul(ntd[:, :], tdc[:, :], -1.0),
              waits=[ev_small], incs=[("sV", 1)])
        ev_ntd = S.ev("sV")
        S.add("vector", lambda e: e.tensor_scalar_mul(s3h[:, :], s3c[:, :], 0.5))
        for t_ in (Sst, vbd, kmbd, kbd, ats, sbf):
            S.add("vector", lambda e, tt=t_: e.memset(tt.ap(), 0.0))
        S.add("vector", lambda e: e.engine_nop(), incs=[("sV", 1)])
        ev_memset = S.ev("sV")
        S.add("scalar", lambda e: e.activation(dC[:, :], tdc[:, :], AF.Exp, scale=-float(CH)),
              waits=[ev_small])
        S.add("scalar", lambda e: e.activation(e32[:, :], tdc[:, :], AF.Exp, scale=-(CH / 2.0 + 0.5)))
        S.add("scalar", lambda e: e.activation(e315[:, :], tdc[:, :], AF.Exp, scale=-(CH / 2.0 - 0.5)))
        S.add("scalar", lambda e: e.activation(e512[:, :], tdc[:, :], AF.Exp, scale=-512.0))
        for m in range(8):
            S.add("scalar", lambda e, m_=m: e.activation(dpow[:, m_, :], tdc[:, :], AF.Exp,
                                                         scale=-(float(CH) * (7 - m_) + CH / 2.0 - 0.5)),
                  incs=([("sS", 1)] if m == 7 else ()))
        prev_pat = None
        for f in range(NF):
            for which in range(2):
                slot = (f * 2 + which) % 2
                col = ntd if which == 0 else tdc
                S.add("vector",
                      lambda e, f_=f, sl_=slot, c_=col: e.tensor_scalar_mul(
                          scr6[:, sl_, 0:CH], ci[:, :], c_[:, f_:f_ + 1]),
                      waits=_flt([ev_ntd if (f == 0 and which == 0) else None, prev_pat]),
                      incs=[("sV", 1)])
                ev_arg = S.ev("sV")
                S.add("scalar",
                      lambda e, f_=f, w_=which, sl_=slot: e.activation(
                          pat[:, w_, f_, :], scr6[:, sl_, 0:CH], AF.Exp),
                      waits=[ev_arg], incs=[("sS", 1)])
                prev_pat = S.ev("sS")
        ev_ioA = None
        ev_ioB = None
        trp = [None, None]
        for t in range(16):
            ev_xA = S.dma("sync",
                          lambda e, t_=t: e.dma_start(
                              out=arena[:, 0:512],
                              in_=x_in.ap()[t_ * 128:(t_ + 1) * 128, 0:512]),
                          "sIO", waits=_flt([ev_ioA]))
            ev_xB = S.dma("sync",
                          lambda e, t_=t: e.dma_start(
                              out=arena[:, 512:1024],
                              in_=x_in.ap()[t_ * 128:(t_ + 1) * 128, 512:1024]),
                          "sIO", waits=_flt([ev_ioB]))
            for f in range(NF):
                pf = f % 2
                w = ([ev_xA, ev_small] if f == 0 else []) + ([ev_xB] if f == 4 else [])
                S.add("tensor",
                      lambda e, f_=f, p=pf: e.transpose(
                          psALL[:, 4 + p, 0:128], arena[:, f_ * 128:(f_ + 1) * 128], idf[:, :]),
                      waits=_flt(w + [trp[pf]]),
                      incs=[("sT", 1)])
                ev_t = S.ev("sT")
                if f == 3:
                    ev_ioA = ev_t
                if f == NF - 1:
                    ev_ioB = ev_t
                S.add("vector",
                      lambda e, f_=f, p=pf, t_=t: e.tensor_copy(
                          R[:, f_, t_ * 128:(t_ + 1) * 128], psALL[:, 4 + p, 0:128]),
                      waits=[ev_t], incs=[("sV", 1)])
                trp[pf] = S.ev("sV")
        for oc in (0, 1):
            par = oc % 2
            first_use.add(("w12", oc))
            rdy = S.dma("gpsimd",
                        lambda e, o=oc, p=par: e.dma_start(
                            out=wbufW[:, p, :, :, :],
                            in_=w12_in.ap()[o].rearrange("w p k q -> p w k q")),
                        "sGW%d" % par)
            wb = S.dma("sync",
                       lambda e, o=oc, p=par: e.dma_start(
                           out=w12_sc.ap()[o].rearrange("w p k q -> p w k q"),
                           in_=wbufW[:, p, :, :, :]),
                       "sLW%d" % par, waits=[rdy])
            pref[("w12", oc)] = (rdy, wb)
        S.barrier()

        if PH_FFN1:
            emit_ffn(0)
            ln_a0 = emit_ln(0, "attn") if PH_ATTN else None
            S.barrier()
        else:
            ln_a0 = None
        if PH_ATTN:
            emit_attn(ln_a0)
            ln_f2 = emit_ln(0, "ffn") if PH_FFN2 else None
            S.barrier()
        else:
            ln_f2 = None
        if PH_FFN2:
            emit_ffn(1, ln_f2)
            S.barrier()

        trp = [None, None]
        ev_half = [None, None]
        for t in range(16):
            for f in range(NF):
                pf = f % 2
                S.add("tensor",
                      lambda e, f_=f, p=pf, t_=t: e.transpose(
                          psALL[:, 4 + p, 0:128], R[:, f_, t_ * 128:(t_ + 1) * 128], idf[:, :]),
                      waits=_flt([trp[pf]] + (evP[4 + pf] if t == 0 and f < 2 else [])),
                      incs=[("sT", 1)])
                ev_t = S.ev("sT")
                S.add("vector",
                      lambda e, f_=f, p=pf: e.tensor_copy(
                          arena[:, f_ * 128:(f_ + 1) * 128], psALL[:, 4 + p, 0:128]),
                      waits=_flt([ev_t] + ([ev_half[f // 4]] if f % 4 == 0 else [])),
                      incs=[("sV", 1)])
                trp[pf] = S.ev("sV")
                if f == 3 or f == NF - 1:
                    h = f // 4
                    ev_half[h] = S.dma(
                        "sync",
                        lambda e, t_=t, h_=h: e.dma_start(
                            out=y_out.ap()[t_ * 128:(t_ + 1) * 128,
                                           h_ * 512:(h_ + 1) * 512],
                            in_=arena[:, h_ * 512:(h_ + 1) * 512]),
                        "sIO", waits=[trp[f % 2]])
        S.add("sync", lambda e: e.nop(), waits=[("sIO", S.val("sIO"))])

        block = ctx.enter_context(nc.Block())
        for eng in ENGS:
            getattr(block, eng)(lambda e, g=eng: S.emit(e, g, sems))

    return nc


def _tile_w(w):
    O, I = w.shape
    t = w.reshape(O // 128, 128, I // 128, 128)
    return np.ascontiguousarray(t.transpose(0, 3, 2, 1))


def _col128(v):
    return np.ascontiguousarray(np.asarray(v, np.float32).reshape(-1, 128).T)


def kernel(**inputs):
    x = np.asarray(inputs["L0_attn_x"], np.float32)
    B, T, _ = x.shape

    w12 = np.ascontiguousarray(np.stack(
        [_tile_w(np.asarray(inputs["L0_ffn_w1_w"], np.float32)),
         _tile_w(np.asarray(inputs["L0_ffn_w2_w"], np.float32))], axis=1))
    h1 = _tile_w(np.asarray(inputs["L0_ffn_h1_w"], np.float32))
    h2 = _tile_w(np.asarray(inputs["L0_ffn_h2_w"], np.float32))
    w3 = _tile_w(np.asarray(inputs["L0_ffn_w3_w"], np.float32))
    wkv = np.ascontiguousarray(np.stack(
        [_tile_w(np.asarray(inputs["L0_attn_Wk"], np.float32)),
         _tile_w(np.asarray(inputs["L0_attn_Wv"], np.float32))], axis=1))
    wrg = np.ascontiguousarray(np.stack(
        [_tile_w(np.asarray(inputs["L0_attn_Wr"], np.float32)),
         _tile_w(np.asarray(inputs["L0_attn_Wg"], np.float32))], axis=1))
    wo = np.ascontiguousarray(
        _tile_w(np.asarray(inputs["L0_attn_Wo"], np.float32))[:, None])

    s1 = _col128(inputs["L0_ffn_w1_s"])
    s2 = _col128(inputs["L0_ffn_w2_s"])
    sh1 = _col128(inputs["L0_ffn_h1_s"])
    sh2 = _col128(inputs["L0_ffn_h2_s"])
    s3 = _col128(inputs["L0_ffn_w3_s"])
    tdfm = _col128(inputs["L0_attn_time_decay"])
    tffm = _col128(inputs["L0_attn_time_first"])
    tmfm = _col128(inputs["L0_attn_time_mix"])

    idb = np.eye(128, dtype=np.float32).astype(BFNP)
    idf = np.eye(128, dtype=np.float32)
    i_loc = np.arange(CH)
    maskt = np.ascontiguousarray(
        np.tile((i_loc[:, None] <= i_loc[None, :]), (128 // CH, 1)).astype(np.uint8))
    ci = np.ascontiguousarray(
        np.broadcast_to((i_loc - (CH - 1) / 2.0).astype(np.float32),
                        (128, CH))).astype(BFNP)
    onebf = np.ones((128, 1), dtype=BFNP)
    p_i = np.arange(128)
    bdm = np.ascontiguousarray(((p_i[:, None] // 64) == (p_i[None, :] // 64)).astype(np.uint8))
    onerow = np.ones((1, 128), dtype=np.float32)
    epst = np.full((1, 1), 1e-5, dtype=np.float32)

    nc = build_nc()
    in_maps = []
    for c in range(8):
        pos = c % 4
        mvec = np.zeros(4, np.float32)
        kvec = np.zeros(4, np.float32)
        for q in range(pos):
            mvec[q] = -2048.0 * (pos - 1 - q)
            kvec[q] = 1.0
        cmbm = np.ascontiguousarray(np.broadcast_to(mvec, (128, 4)))
        cmbk = np.ascontiguousarray(np.broadcast_to(kvec, (128, 4)))
        shard = np.ascontiguousarray(x[c // 4, pos * TCORE:(pos + 1) * TCORE, :])
        in_maps.append({
            "x": shard, "w12": w12, "h1": h1, "h2": h2, "w3": w3,
            "wkv": wkv, "wrg": wrg, "wo": wo,
            "s1": s1, "s2": s2, "sh1": sh1, "sh2": sh2, "s3": s3,
            "tdfm": tdfm, "tffm": tffm, "tmfm": tmfm,
            "idb": idb, "idf": idf, "maskt": maskt, "ci": ci,
            "onebf": onebf, "onerow": onerow, "epst": epst,
            "cmbm": cmbm, "cmbk": cmbk, "bdm": bdm,
        })
    res = run_bass_kernel_spmd(nc, in_maps, core_ids=list(range(8)))
    y = np.empty((B, T, D), np.float32)
    for c in range(8):
        y[c // 4, (c % 4) * TCORE:((c % 4) + 1) * TCORE, :] = res.results[c]["y"]
    return y


# version 27 (final)
# speedup vs baseline: 1.0228x; 1.0118x over previous
import numpy as np
import ml_dtypes
from contextlib import ExitStack

import concourse.bass as bass
import concourse.mybir as mybir
from concourse.bass_utils import run_bass_kernel_spmd

F32 = mybir.dt.float32
BF16 = mybir.dt.bfloat16
U8 = mybir.dt.uint8
AF = mybir.ActivationFunctionType
ALU = mybir.AluOpType
BFNP = ml_dtypes.bfloat16

TCORE = 2048
NSTRIP = 4
STRIP = 512
NF = 8
NHC = 32
CH = 64
CPS = STRIP // CH
D = 1024

PH_FFN1 = True
PH_ATTN = True
PH_FFN2 = True

ENGS = ("tensor", "vector", "scalar", "gpsimd", "sync")
SEMS = ("sT", "sV", "sS", "sG", "sIO", "sDG", "sCC",
        "sLW0", "sLW1", "sLH0", "sLH1", "sGW0", "sGW1", "sGH0", "sGH1")
ENG_SEM = {"tensor": "sT", "vector": "sV", "scalar": "sS", "gpsimd": "sG"}


class Sched:
    def __init__(self):
        self.q = {e: [] for e in ENGS}
        self.cnt = {s: 0 for s in SEMS}

    def val(self, sem):
        return self.cnt[sem]

    def ev(self, sem):
        return (sem, self.cnt[sem])

    def add(self, eng, fn, waits=(), incs=()):
        for s, by in incs:
            self.cnt[s] += by
        w = tuple((s, v) for (s, v) in waits if v is not None and v > 0)
        self.q[eng].append((fn, w, tuple(incs)))

    def dma(self, eng, fn, sem, waits=()):
        w = list(waits) + [(sem, self.cnt[sem])]
        self.add(eng, fn, waits=w, incs=[(sem, 16)])
        return self.ev(sem)

    def barrier(self):
        for e in ("tensor", "vector", "scalar", "gpsimd"):
            s = ENG_SEM[e]
            self.add(e, lambda en: en.nop(), waits=[(s, self.cnt[s])], incs=[(s, 1)])
        vals = [(s, self.cnt[s]) for s in SEMS]
        for e in ENGS:
            self.add(e, lambda en: en.nop(), waits=vals)

    def emit(self, e, eng, sems):
        for fn, waits, incs in self.q[eng]:
            for s, v in waits:
                e.wait_ge(sems[s], v)
            ins = fn(e)
            for s, by in incs:
                ins.then_inc(sems[s], by)


def _flt(evs):
    return [e for e in evs if e is not None]


def build_nc():
    nc = bass.Bass("TRN2", target_bir_lowering=False, detect_race_conditions=False)

    x_in = nc.dram_tensor("x", [TCORE, D], F32, kind="ExternalInput")
    y_out = nc.dram_tensor("y", [TCORE, D], F32, kind="ExternalOutput")

    w12_in = nc.dram_tensor("w12", [NHC, 2, 128, NF, 128], F32, kind="ExternalInput")
    h1_in = nc.dram_tensor("h1", [NHC, 128, NHC, 128], F32, kind="ExternalInput")
    h2_in = nc.dram_tensor("h2", [NHC, 128, NHC, 128], F32, kind="ExternalInput")
    w3_in = nc.dram_tensor("w3", [NF, 128, NHC, 128], F32, kind="ExternalInput")
    wkv_in = nc.dram_tensor("wkv", [NF, 2, 128, NF, 128], F32, kind="ExternalInput")
    wrg_in = nc.dram_tensor("wrg", [NF, 2, 128, NF, 128], F32, kind="ExternalInput")
    wo_in = nc.dram_tensor("wo", [NF, 1, 128, NF, 128], F32, kind="ExternalInput")

    s1_in = nc.dram_tensor("s1", [128, NHC], F32, kind="ExternalInput")
    s2_in = nc.dram_tensor("s2", [128, NHC], F32, kind="ExternalInput")
    sh1_in = nc.dram_tensor("sh1", [128, NHC], F32, kind="ExternalInput")
    sh2_in = nc.dram_tensor("sh2", [128, NHC], F32, kind="ExternalInput")
    s3_in = nc.dram_tensor("s3", [128, NF], F32, kind="ExternalInput")
    td_in = nc.dram_tensor("tdfm", [128, NF], F32, kind="ExternalInput")
    tf_in = nc.dram_tensor("tffm", [128, NF], F32, kind="ExternalInput")
    tm_in = nc.dram_tensor("tmfm", [128, NF], F32, kind="ExternalInput")
    idb_in = nc.dram_tensor("idb", [128, 128], BF16, kind="ExternalInput")
    idf_in = nc.dram_tensor("idf", [128, 128], F32, kind="ExternalInput")
    mask_in = nc.dram_tensor("maskt", [128, CH], U8, kind="ExternalInput")
    ci_in = nc.dram_tensor("ci", [128, CH], BF16, kind="ExternalInput")
    onebf_in = nc.dram_tensor("onebf", [128, 1], BF16, kind="ExternalInput")
    onerow_in = nc.dram_tensor("onerow", [1, 128], F32, kind="ExternalInput")
    eps_in = nc.dram_tensor("epst", [1, 1], F32, kind="ExternalInput")
    cmbm_in = nc.dram_tensor("cmbm", [128, 4], F32, kind="ExternalInput")
    bdm_in = nc.dram_tensor("bdm", [128, 128], U8, kind="ExternalInput")
    cmbk_in = nc.dram_tensor("cmbk", [128, 4], F32, kind="ExternalInput")

    w12_sc = nc.dram_tensor("w12sc", [NHC, 2, 128, NF, 128], BF16)
    h1_sc = nc.dram_tensor("h1sc", [NHC, 128, NHC, 128], BF16)
    h2_sc = nc.dram_tensor("h2sc", [NHC, 128, NHC, 128], BF16)
    w3_sc = nc.dram_tensor("w3sc", [NF, 128, NHC, 128], BF16)
    wkv_sc = nc.dram_tensor("wkvsc", [NF, 2, 128, NF, 128], BF16)
    wrg_sc = nc.dram_tensor("wrgsc", [NF, 2, 128, NF, 128], BF16)
    wo_sc = nc.dram_tensor("wosc", [NF, 1, 128, NF, 128], BF16)
    ccin_d = nc.dram_tensor("ccin", [128, 1024], F32)
    ccout_d = nc.dram_tensor("ccout", [512, 1024], F32)

    S = Sched()

    with ExitStack() as ctx:
        sb = lambda name, sh, dt: ctx.enter_context(nc.sbuf_tensor(name, sh, dt))

        R = sb("R", [128, NF, TCORE], F32)
        big1 = sb("big1", [128, 16384], BF16)
        big2 = sb("big2", [128, 16384], BF16)
        ha = big1.ap().rearrange("p (a b) -> p a b", a=NHC)
        hb = big2.ap().rearrange("p (a b) -> p a b", a=NHC)
        Kv = big1.ap().rearrange("p (a b) -> p a b", a=NF)
        Vv = big2.ap().rearrange("p (a b) -> p a b", a=NF)
        xs = sb("xs", [128, NF, STRIP], BF16)
        rp = sb("rp", [128, NF, STRIP], BF16)
        gp = sb("gp", [128, NF, STRIP], BF16)
        sq2 = sb("sq2", [128, 2, STRIP], BF16)
        rb2 = sb("rb2", [128, 2, STRIP], BF16)
        wbufW = sb("wbufW", [128, 2, 2, NF, 128], BF16)
        wbufH = sb("wbufH", [128, 2, NHC, 128], BF16)
        pat = sb("pat", [128, 2, NF, CH], BF16)
        Sst = sb("Sst", [128, NF, 128], F32)
        sbf = sb("sbf", [128, 2, 128], BF16)
        kmbd = sb("kmbd", [128, 2, 2 * CH], BF16)
        vbd = sb("vbd", [128, 2, 2 * CH], BF16)
        kbd = sb("kbd", [128, 2, 2 * CH], BF16)
        vtkt = sb("vtkt", [2 * CH, 2, 256], BF16)
        qs = sb("qs", [128, 2, CH], BF16)
        kint = sb("kint", [128, 2, CH], BF16)
        ats = sb("ats", [2 * CH, 2, CH], BF16)
        bon1 = sb("bon1", [128, 1, CH], F32)
        scr6 = sb("scr6", [128, 3, STRIP], F32)
        ln_a = sb("ln_a", [1, 2, STRIP], F32)
        ktm2 = sb("ktm2", [128, 512], BF16)
        e512 = sb("e512c", [128, NF], F32)
        dpow = sb("dpow", [128, 8, NF], F32)
        arena = sb("arena", [128, 1024], F32)
        exch = arena.ap().rearrange("p (f q) -> p f q", f=NF)

        s1c = sb("s1c", [128, NHC], F32)
        s2c = sb("s2c", [128, NHC], F32)
        sh1c = sb("sh1c", [128, NHC], F32)
        sh2c = sb("sh2c", [128, NHC], F32)
        s3c = sb("s3c", [128, NF], F32)
        s3h = sb("s3h", [128, NF], F32)
        tdc = sb("tdc", [128, NF], F32)
        tfc = sb("tfc", [128, NF], F32)
        tmc = sb("tmc", [128, NF], F32)
        ntd = sb("ntd", [128, NF], F32)
        dC = sb("dCc", [128, NF], F32)
        e32 = sb("e32c", [128, NF], F32)
        e315 = sb("e315c", [128, NF], F32)
        idb = sb("idbs", [128, 128], BF16)
        idf = sb("idfs", [128, 128], F32)
        maskt = sb("maskts", [128, CH], U8)
        ci = sb("cis", [128, CH], BF16)
        onebf = sb("onebfs", [128, 1], BF16)
        onerow = sb("onerows", [1, 128], F32)
        epst = sb("epsts", [1, 1], F32)
        cmbm = sb("cmbms", [128, 4], F32)
        bdm = sb("bdms", [128, 128], U8)
        cmbk = sb("cmbks", [128, 4], F32)

        psALL = ctx.enter_context(nc.psum_tensor("psALL", [128, 6, 512], F32))
        psSC = ctx.enter_context(nc.psum_tensor("psSC", [128, 2, 1024], BF16))

        sems = {s: ctx.enter_context(nc.semaphore(s)) for s in SEMS}

        evP = [[] for _ in range(6)]
        evW = [[], []]
        evH = [[], []]
        evRB2 = [None, None]
        evSQ2 = [None, None]
        evSilu = [None, None]
        st = {"xs_free": None, "rpgp_free": None, "lna_free": None}
        tr = {k: [None, None] for k in
              ("vbd", "kmbd", "kmbd2", "kbd", "qs", "qs2", "kint", "ats",
               "vtkt", "sbf", "psSC")}
        first_use = set()
        pref = {}

        def wload(key, src_ap, scr_ap, dst_ap, free_evs, slot):
            if key in pref:
                return pref.pop(key)
            gslot = "sG" + slot[2:]
            if key not in first_use:
                first_use.add(key)
                rdy = S.dma("gpsimd",
                            lambda e, d=dst_ap, s=src_ap: e.dma_start(out=d, in_=s),
                            gslot, waits=_flt(free_evs))
                wb = S.dma("sync",
                           lambda e, d=scr_ap, s=dst_ap: e.dma_start(out=d, in_=s),
                           slot, waits=[rdy])
                return rdy, wb
            w = _flt(free_evs)
            if key in castev:
                w.append(castev.pop(key))
            return S.dma("sync",
                         lambda e, d=dst_ap, s=scr_ap: e.dma_start(out=d, in_=s),
                         slot, waits=w), None

        def emit_ln_stats(strip, mode):
            sl = slice(strip * STRIP, (strip + 1) * STRIP)
            for kc in range(NF):
                par = kc % 2
                S.add("vector",
                      lambda e, k=kc, p=par, s=sl: e.tensor_copy(rb2[:, p, :], R[:, k, s]),
                      waits=_flt([evRB2[par]]), incs=[("sV", 1)])
                ev_rb = S.ev("sV")
                S.add("scalar",
                      lambda e, k=kc, p=par, s=sl: e.activation(sq2[:, p, :], R[:, k, s], AF.Square),
                      waits=_flt([evSQ2[par]]), incs=[("sS", 1)])
                ev_sq = S.ev("sS")
                w = [ev_rb] + (evP[0] if kc == 0 else [])
                S.add("tensor",
                      lambda e, p=par, k=kc: e.matmul(
                          psALL[0:1, 0, :], onebf[:, :], rb2[:, p, :],
                          start=(k == 0), stop=(k == NF - 1)),
                      waits=_flt(w))
                w = [ev_sq] + (evP[1] if kc == 0 else [])
                S.add("tensor",
                      lambda e, p=par, k=kc: e.matmul(
                          psALL[0:1, 1, :], onebf[:, :], sq2[:, p, :],
                          start=(k == 0), stop=(k == NF - 1)),
                      waits=_flt(w), incs=[("sT", 1)])
                ev_t = S.ev("sT")
                evRB2[par] = ev_t
                evSQ2[par] = ev_t
            ev_stats = S.ev("sT")
            S.add("vector", lambda e: e.tensor_scalar_mul(ln_a[:, 0, :], psALL[0:1, 0, :], 1.0 / D),
                  waits=_flt([ev_stats, st["lna_free"]]), incs=[("sV", 1)])
            S.add("vector", lambda e: e.tensor_scalar_mul(ln_a[:, 1, :], psALL[0:1, 1, :], 1.0 / D),
                  incs=[("sV", 1)])
            S.add("vector", lambda e: e.tensor_mul(scr6[0:1, 0, :], ln_a[:, 0, :], ln_a[:, 0, :]))
            S.add("vector", lambda e: e.tensor_sub(ln_a[:, 1, :], ln_a[:, 1, :], scr6[0:1, 0, :]),
                  incs=[("sV", 1)])
            ev_var = S.ev("sV")
            S.add("scalar", lambda e: e.activation(ln_a[:, 1, :], ln_a[:, 1, :], AF.Ln,
                                                   bias=epst[:, :]),
                  waits=[ev_var])
            S.add("scalar", lambda e: e.activation(scr6[0:1, 0, :], ln_a[:, 1, :], AF.Exp,
                                                   scale=-0.5),
                  incs=[("sS", 1)])
            ev_std = S.ev("sS")
            S.add("vector", lambda e: e.scalar_tensor_tensor(
                ln_a[:, 0, :], ln_a[:, 0, :], -1.0, scr6[0:1, 0, :], ALU.mult, ALU.mult),
                waits=[ev_std], incs=[("sV", 1)])
            ev_math = S.ev("sV")

            def bc():
                S.add("tensor", lambda e: e.matmul(psALL[:, 0, :], onerow[:, :],
                                                   scr6[0:1, 0, :],
                                                   start=True, stop=True),
                      waits=_flt([ev_math]))
                S.add("tensor", lambda e: e.matmul(psALL[:, 1, :], onerow[:, :],
                                                   ln_a[:, 0, :],
                                                   start=True, stop=True),
                      incs=[("sT", 1)])
                st["lna_free"] = S.ev("sT")

            def norm():
                ev_bc = st["lna_free"]
                for kc in range(NF):
                    w = _flt(([ev_bc] if kc == 0 else []) +
                             ([st["xs_free"]] if kc == 0 else []))
                    S.add("vector",
                          lambda e, k=kc, s=sl: e.tensor_mul(scr6[:, 0, :], R[:, k, s],
                                                             psALL[:, 0, :]),
                          waits=w)
                    if mode == "ffn":
                        S.add("vector",
                              lambda e, k=kc: e.tensor_add(xs[:, k, :], scr6[:, 0, :],
                                                           psALL[:, 1, :]),
                              incs=([("sV", 1)] if kc == NF - 1 else []))
                    else:
                        S.add("vector",
                              lambda e: e.tensor_add(scr6[:, 1, :], scr6[:, 0, :],
                                                     psALL[:, 1, :]))
                        S.add("vector",
                              lambda e, k=kc: e.tensor_scalar_mul(xs[:, k, :], scr6[:, 1, :],
                                                                  tmc[:, k:k + 1]),
                              incs=([("sV", 1)] if kc == NF - 1 else []))
                ev_xs = S.ev("sV")
                evP[0] = [ev_xs]
                evP[1] = [ev_xs]
                return ev_xs

            return bc, norm

        def emit_ln(strip, mode):
            bc, norm = emit_ln_stats(strip, mode)
            bc()
            return norm()

        castev = {}

        def precast_attn_weights():
            items = ([(("wkv", oc), wkv_in.ap()[oc], wkv_sc.ap()[oc])
                      for oc in range(NF)] +
                     [(("wrg", oc), wrg_in.ap()[oc], wrg_sc.ap()[oc])
                      for oc in range(NF)] +
                     [(("wo", oc), wo_in.ap()[oc], wo_sc.ap()[oc])
                      for oc in range(NF)])
            for i, (key, src, dst) in enumerate(items):
                first_use.add(key)
                castev[key] = S.dma(
                    "gpsimd", lambda e, d=dst, s=src: e.dma_start(out=d, in_=s),
                    ("sGW0", "sGW1")[i % 2])

        def emit_ffn(app, pre_xs=None):
            ev_xs = pre_xs if pre_xs is not None else emit_ln(0, "ffn")
            for s in range(NSTRIP):
                sl = slice(s * STRIP, (s + 1) * STRIP)
                if app == 0 and s == 2:
                    precast_attn_weights()
                for oc in range(NHC):
                    par = oc % 2
                    rdy, wb = wload(("w12", oc),
                                    w12_in.ap()[oc].rearrange("w p k q -> p w k q"),
                                    w12_sc.ap()[oc].rearrange("w p k q -> p w k q"),
                                    wbufW[:, par, :, :, :], evW[par], "sLW%d" % par)
                    w = [rdy] + (([ev_xs] if oc < 2 else [])) + evP[0 + par]
                    for kc in range(NF):
                        S.add("tensor",
                              lambda e, p=par, k=kc: e.matmul(
                                  psALL[:, 0 + p, :], wbufW[:, p, 0, k, :], xs[:, k, :],
                                  start=(k == 0), stop=(k == NF - 1)),
                              waits=_flt(w) if kc == 0 else (), incs=([("sT", 1)] if kc == NF - 1 else []))
                    ev_a = S.ev("sT")
                    for kc in range(NF):
                        S.add("tensor",
                              lambda e, p=par, k=kc: e.matmul(
                                  psALL[:, 2 + p, :], wbufW[:, p, 1, k, :], xs[:, k, :],
                                  start=(k == 0), stop=(k == NF - 1)),
                              waits=_flt(evP[2 + par]) if kc == 0 else (),
                              incs=([("sT", 1)] if kc == NF - 1 else []))
                    ev_b = S.ev("sT")
                    evW[par] = [ev_b] + ([wb] if wb else [])
                    S.add("scalar",
                          lambda e, p=par, o=oc: e.activation(
                              scr6[:, 1 + p, :], psALL[:, 0 + p, :], AF.Silu,
                              scale=s1c[:, o:o + 1]),
                          waits=_flt([ev_a, evSilu[par]]), incs=[("sS", 1)])
                    ev_si = S.ev("sS")
                    evP[0 + par] = [ev_si]
                    S.add("vector",
                          lambda e, p=par, o=oc: e.scalar_tensor_tensor(
                              ha[:, o, :], psALL[:, 2 + p, :], s2c[:, o:o + 1],
                              scr6[:, 1 + p, :], ALU.mult, ALU.mult),
                          waits=[ev_b, ev_si], incs=[("sV", 1)])
                    ev_ha = S.ev("sV")
                    evP[2 + par] = [ev_ha]
                    evSilu[par] = ev_ha
                st["xs_free"] = S.ev("sT")
                ha_all = S.ev("sV")
                lnbc, lnnorm = (emit_ln_stats(s + 1, "ffn")
                                if s < NSTRIP - 1 else (None, None))
                for oc in range(NHC):
                    if oc == 4 and lnbc is not None:
                        lnbc()
                        ev_xs = lnnorm()
                    par = oc % 2
                    rdy, wb = wload(("h1", oc), h1_in.ap()[oc], h1_sc.ap()[oc],
                                    wbufH[:, par, :, :], evH[par], "sLH%d" % par)
                    w = [rdy] + ([ha_all] if oc < 2 else []) + evP[4 + par]
                    for kc in range(NHC):
                        S.add("tensor",
                              lambda e, p=par, k=kc: e.matmul(
                                  psALL[:, 4 + p, :], wbufH[:, p, k, :], ha[:, k, :],
                                  start=(k == 0), stop=(k == NHC - 1)),
                              waits=_flt(w) if kc == 0 else (),
                              incs=([("sT", 1)] if kc == NHC - 1 else []))
                    ev_m = S.ev("sT")
                    evH[par] = [ev_m] + ([wb] if wb else [])
                    S.add("scalar",
                          lambda e, p=par, o=oc: e.activation(
                              hb[:, o, :], psALL[:, 4 + p, :], AF.Gelu,
                              scale=sh1c[:, o:o + 1]),
                          waits=[ev_m], incs=[("sS", 1)])
                    evP[4 + par] = [S.ev("sS")]
                hb_all = S.ev("sS")
                for oc in range(NHC):
                    par = oc % 2
                    rdy, wb = wload(("h2", oc), h2_in.ap()[oc], h2_sc.ap()[oc],
                                    wbufH[:, par, :, :], evH[par], "sLH%d" % par)
                    w = [rdy] + ([hb_all] if oc < 2 else []) + evP[4 + par]
                    for kc in range(NHC):
                        S.add("tensor",
                              lambda e, p=par, k=kc: e.matmul(
                                  psALL[:, 4 + p, :], wbufH[:, p, k, :], hb[:, k, :],
                                  start=(k == 0), stop=(k == NHC - 1)),
                              waits=_flt(w) if kc == 0 else (),
                              incs=([("sT", 1)] if kc == NHC - 1 else []))
                    ev_m = S.ev("sT")
                    evH[par] = [ev_m] + ([wb] if wb else [])
                    S.add("scalar",
                          lambda e, p=par, o=oc: e.activation(
                              ha[:, o, :], psALL[:, 4 + p, :], AF.Gelu,
                              scale=sh2c[:, o:o + 1]),
                          waits=[ev_m], incs=[("sS", 1)])
                    evP[4 + par] = [S.ev("sS")]
                hc_all = S.ev("sS")
                for oc in range(NF):
                    par = oc % 2
                    rdy, wb = wload(("w3", oc), w3_in.ap()[oc], w3_sc.ap()[oc],
                                    wbufH[:, par, :, :], evH[par], "sLH%d" % par)
                    w = [rdy] + ([hc_all] if oc < 2 else []) + evP[4 + par]
                    for kc in range(NHC):
                        S.add("tensor",
                              lambda e, p=par, k=kc: e.matmul(
                                  psALL[:, 4 + p, :], wbufH[:, p, k, :], ha[:, k, :],
                                  start=(k == 0), stop=(k == NHC - 1)),
                              waits=_flt(w) if kc == 0 else (),
                              incs=([("sT", 1)] if kc == NHC - 1 else []))
                    ev_m = S.ev("sT")
                    evH[par] = [ev_m] + ([wb] if wb else [])
                    S.add("vector",
                          lambda e, p=par, o=oc, s=sl: e.scalar_tensor_tensor(
                              R[:, o, s], psALL[:, 4 + p, :], s3h[:, o:o + 1],
                              R[:, o, s], ALU.mult, ALU.add),
                          waits=[ev_m], incs=[("sV", 1)])
                    evP[4 + par] = [S.ev("sV")]

        pend = [None]

        def emit_scan(c, f, full, extra_w=()):
            par = (c * 8 + f) % 2
            scol = slice(c * CH, (c + 1) * CH)
            cis = slice((c % CPS) * CH, (c % CPS + 1) * CH)
            KP = Kv[:, f, scol]
            VP = Vv[:, f, scol]
            PQ = pat[:, 0, f, :]
            PK = pat[:, 1, f, :]
            vtv = vtkt[:, par, 0:128]
            ktv = vtkt[:, par, 128:256]
            S.add("vector", lambda e: e.tensor_copy(vbd[0:64, par, 0:CH], VP[0:64, :]),
                  waits=_flt([tr["vbd"][par]] + list(extra_w)))
            S.add("vector", lambda e: e.tensor_copy(vbd[64:128, par, CH:2 * CH], VP[64:128, :]),
                  incs=[("sV", 1)])
            ev_vbd = S.ev("sV")
            S.add("vector", lambda e: e.tensor_mul(qs[:, par, :], KP, PQ),
                  waits=_flt([tr["qs"][par], tr["qs2"][par]]))
            S.add("vector", lambda e: e.tensor_mul(kmbd[0:64, par, 0:CH],
                                                   KP[0:64, :], PK[0:64, :]),
                  waits=_flt([tr["kmbd"][par], tr["kmbd2"][par]]))
            S.add("vector", lambda e: e.tensor_mul(kmbd[64:128, par, CH:2 * CH],
                                                   KP[64:128, :], PK[64:128, :]),
                  incs=[("sV", 1)])
            ev_qkm = S.ev("sV")
            S.add("tensor", lambda e: e.matmul(psALL[0:2 * CH, 0 + par, 0:CH],
                                               kmbd[:, par, :], qs[:, par, :],
                                               start=True, stop=True),
                  waits=_flt([ev_qkm] + evP[0 + par]), incs=[("sT", 1)])
            ev_A = S.ev("sT")
            tr["qs"][par] = ev_A
            tr["kmbd"][par] = ev_A
            S.add("tensor", lambda e: e.transpose(psSC[0:2 * CH, par, 0:128], vbd[:, par, :],
                                                  idb[:, :]),
                  waits=_flt([ev_vbd, tr["psSC"][par]]), incs=[("sT", 1)])
            ev_vt = S.ev("sT")
            tr["vbd"][par] = ev_vt
            S.add("scalar", lambda e: e.activation(
                kbd[0:64, par, 0:CH], kmbd[0:64, par, 0:CH], AF.Copy,
                scale=e315[0:64, f:f + 1]),
                waits=_flt([ev_qkm, tr["kbd"][par]]))
            S.add("scalar", lambda e: e.activation(
                kbd[64:128, par, CH:2 * CH], kmbd[64:128, par, CH:2 * CH], AF.Copy,
                scale=e315[64:128, f:f + 1]),
                incs=[("sS", 1)])
            ev_kbd = S.ev("sS")
            tr["kmbd2"][par] = ev_kbd
            S.add("scalar", lambda e: e.copy(vtkt[:, par, 0:128], psSC[0:2 * CH, par, 0:128]),
                  waits=_flt([ev_vt, tr["vtkt"][par]]), incs=[("sS", 1)])
            ev_vte = S.ev("sS")
            S.add("scalar", lambda e: e.activation(
                kint[:, par, :], qs[:, par, :], AF.Copy, scale=e32[:, f:f + 1]),
                waits=_flt([tr["kint"][par]]), incs=[("sS", 1)])
            ev_sk = S.ev("sS")
            tr["qs2"][par] = ev_sk
            S.add("tensor", lambda e: e.transpose(psSC[0:2 * CH, par, 0:128], kbd[:, par, :],
                                                  idb[:, :]),
                  waits=[ev_kbd, ev_vte], incs=[("sT", 1)])
            ev_kt = S.ev("sT")
            tr["kbd"][par] = ev_kt
            S.add("scalar", lambda e: e.copy(vtkt[:, par, 128:256], psSC[0:2 * CH, par, 0:128]),
                  waits=_flt([ev_kt]), incs=[("sS", 1)])
            ev_tk = S.ev("sS")
            tr["psSC"][par] = ev_tk
            S.add("vector", lambda e: e.copy_predicated(sbf[:, par, :], bdm[:, :],
                                                        Sst[:, f, :]),
                  waits=_flt([tr["sbf"][par]]), incs=[("sV", 1)])
            ev_sbf = S.ev("sV")
            S.add("vector", lambda e: e.copy_predicated(ats[:, par, :], maskt[0:2 * CH, :],
                                                        psALL[0:2 * CH, 0 + par, 0:CH]),
                  waits=_flt([ev_A, tr["ats"][par]]), incs=[("sV", 1)])
            ev_ats = S.ev("sV")
            evP[0 + par] = [ev_ats]
            S.add("tensor", lambda e: e.matmul(psALL[:, 2 + par, 0:CH],
                                               vtv, ats[:, par, :],
                                               start=True, stop=False),
                  waits=_flt([ev_vte, ev_ats, ev_sbf, ev_sk] + evP[2 + par]))
            S.add("tensor", lambda e: e.matmul(psALL[:, 2 + par, 0:CH],
                                               sbf[:, par, :], kint[:, par, :],
                                               start=False, stop=True),
                  incs=[("sT", 1)])
            ev_y = S.ev("sT")
            tr["ats"][par] = ev_y
            tr["sbf"][par] = ev_y
            tr["kint"][par] = ev_y
            S.add("tensor", lambda e: e.matmul(psALL[:, 4 + par, 0:128],
                                               ktv, vtv,
                                               start=True, stop=True),
                  waits=_flt([ev_tk] + evP[4 + par]), incs=[("sT", 1)])
            ev_ds = S.ev("sT")
            tr["vtkt"][par] = ev_ds

            projT = st.get("projT")

            def tail():
                S.add("vector", lambda e: e.scalar_tensor_tensor(
                    Sst[:, f, :], Sst[:, f, :], dC[:, f:f + 1], psALL[:, 4 + par, 0:128],
                    ALU.mult, ALU.add),
                    waits=[ev_ds], incs=[("sV", 1)])
                evP[4 + par] = [S.ev("sV")]
                S.add("vector", lambda e: e.tensor_mul(bon1[:, 0, :], KP, VP))
                S.add("vector", lambda e: e.scalar_tensor_tensor(
                    bon1[:, 0, :], bon1[:, 0, :], tfc[:, f:f + 1],
                    psALL[:, 2 + par, 0:CH], ALU.mult, ALU.add),
                    waits=[ev_y])
                S.add("vector", lambda e: e.tensor_mul(bon1[:, 0, :], bon1[:, 0, :],
                                                       rp[:, f, cis]))
                S.add("vector", lambda e: e.tensor_mul(xs[:, f, cis], bon1[:, 0, :],
                                                       gp[:, f, cis]),
                      waits=_flt([projT]), incs=[("sV", 1)])
                ev_cmb = S.ev("sV")
                evP[2 + par] = [ev_cmb]

            if pend[0] is not None:
                pend[0]()
            pend[0] = tail

        def flush_scan():
            if pend[0] is not None:
                pend[0]()
                pend[0] = None

        def emit_p2b_coarse(s, kev, vev):
            vtm2 = vtkt.ap().rearrange("p a b -> p (a b)")
            for f in range(NF):
                par = f % 2
                evv = [None] * 4
                for t4 in range(4):
                    col = slice(s * STRIP + t4 * 128, s * STRIP + (t4 + 1) * 128)
                    S.add("tensor",
                          lambda e, f_=f, c_=col, p=t4 % 2: e.transpose(
                              psSC[:, p, 0:128], Vv[:, f_, c_], idb[:, :]),
                          waits=_flt([vev if (f == 0 and t4 == 0) else None,
                                      st.get("c2b_%d" % (t4 % 2)),
                                      st.get("c2k_%d" % (t4 % 2))]),
                          incs=[("sT", 1)])
                    ev_t = S.ev("sT")
                    S.add("scalar",
                          lambda e, t_=t4, p=t4 % 2: e.copy(
                              vtm2[:, t_ * 128:(t_ + 1) * 128], psSC[:, p, 0:128]),
                          waits=_flt([ev_t, st.get("c2b_vtm")]), incs=[("sS", 1)])
                    evv[t4] = S.ev("sS")
                    st["c2b_%d" % (t4 % 2)] = evv[t4]
                evk = [None] * 4
                for t4 in range(4):
                    col0 = s * STRIP + t4 * 128
                    for h in range(2):
                        q = t4 * 2 + h
                        S.add("vector",
                              lambda e, f_=f, q_=q, c0=col0, h_=h: e.scalar_tensor_tensor(
                                  kbd[:, 0, h_ * CH:(h_ + 1) * CH],
                                  Kv[:, f_, c0 + h_ * CH:c0 + (h_ + 1) * CH],
                                  dpow[:, q_, f_:f_ + 1], pat[:, 1, f_, :],
                                  ALU.mult, ALU.mult),
                              waits=_flt([kev if (f == 0 and t4 == 0 and h == 0) else None,
                                          st.get("c2b_kbd") if (h == 0) else None]),
                              incs=([("sV", 1)] if h == 1 else ()))
                    ev_kb = S.ev("sV")
                    S.add("tensor",
                          lambda e, p=t4 % 2: e.transpose(
                              psSC[:, p, 128:256], kbd[:, 0, 0:128], idb[:, :]),
                          waits=_flt([ev_kb, st.get("c2k_%d" % (t4 % 2)),
                                      evv[3] if t4 == 0 else None]),
                          incs=[("sT", 1)])
                    ev_t = S.ev("sT")
                    st["c2b_kbd"] = ev_t
                    S.add("scalar",
                          lambda e, t_=t4, p=t4 % 2: e.copy(
                              ktm2[:, t_ * 128:(t_ + 1) * 128], psSC[:, p, 128:256]),
                          waits=_flt([ev_t, st.get("c2b_ktm")]), incs=[("sS", 1)])
                    evk[t4] = S.ev("sS")
                    st["c2k_%d" % (t4 % 2)] = evk[t4]
                for t4 in range(4):
                    S.add("tensor",
                          lambda e, t_=t4, p=par: e.matmul(
                              psALL[:, 4 + p, 0:128],
                              ktm2[:, t_ * 128:(t_ + 1) * 128],
                              vtm2[:, t_ * 128:(t_ + 1) * 128],
                              start=(t_ == 0), stop=(t_ == 3)),
                          waits=_flt(([evk[3], evv[3]] + evP[4 + par]) if t4 == 0 else []),
                          incs=([("sT", 1)] if t4 == 3 else []))
                ev_ds = S.ev("sT")
                st["c2b_vtm"] = ev_ds
                st["c2b_ktm"] = ev_ds
                S.add("vector", lambda e, f_=f, p=par: e.scalar_tensor_tensor(
                    Sst[:, f_, :], Sst[:, f_, :], e512[:, f_:f_ + 1],
                    psALL[:, 4 + p, 0:128], ALU.mult, ALU.add),
                    waits=[ev_ds], incs=[("sV", 1)])
                evP[4 + par] = [S.ev("sV")]
                tr["vtkt"][0] = ev_ds
                tr["vtkt"][1] = ev_ds
                tr["psSC"][0] = S.ev("sV")
                tr["psSC"][1] = S.ev("sV")

        def emit_attn(pre_xs=None):
            ev_xs = pre_xs if pre_xs is not None else emit_ln(0, "attn")
            for s in range(NSTRIP):
                sl = slice(s * STRIP, (s + 1) * STRIP)
                for oc in range(NF):
                    par = oc % 2
                    rdy, wb = wload(("wkv", oc),
                                    wkv_in.ap()[oc].rearrange("w p k q -> p w k q"),
                                    wkv_sc.ap()[oc].rearrange("w p k q -> p w k q"),
                                    wbufW[:, par, :, :, :], evW[par], "sLW%d" % par)
                    w = [rdy] + ([ev_xs] if oc < 2 else []) + evP[4 + par]
                    for kc in range(NF):
                        S.add("tensor",
                              lambda e, p=par, k=kc: e.matmul(
                                  psALL[:, 4 + p, :], wbufW[:, p, 0, k, :], xs[:, k, :],
                                  start=(k == 0), stop=(k == NF - 1)),
                              waits=_flt(w) if kc == 0 else (),
                              incs=([("sT", 1)] if kc == NF - 1 else []))
                    ev_k = S.ev("sT")
                    for kc in range(NF):
                        S.add("tensor",
                              lambda e, p=par, k=kc: e.matmul(
                                  psALL[:, 2 + p, :], wbufW[:, p, 1, k, :], xs[:, k, :],
                                  start=(k == 0), stop=(k == NF - 1)),
                              waits=_flt(evP[2 + par]) if kc == 0 else (),
                              incs=([("sT", 1)] if kc == NF - 1 else []))
                    ev_v = S.ev("sT")
                    evW[par] = [ev_v] + ([wb] if wb else [])
                    S.add("scalar",
                          lambda e, p=par, o=oc, s=sl: e.copy(Kv[:, o, s], psALL[:, 4 + p, :]),
                          waits=[ev_k], incs=[("sS", 1)])
                    evP[4 + par] = [S.ev("sS")]
                    S.add("vector",
                          lambda e, p=par, o=oc, s=sl: e.tensor_copy(Vv[:, o, s], psALL[:, 2 + p, :]),
                          waits=[ev_v], incs=[("sV", 1)])
                    evP[2 + par] = [S.ev("sV")]
                st["xs_free"] = S.ev("sT")
                kev = S.ev("sS")
                vev = S.ev("sV")
                lnbc, lnnorm = (emit_ln_stats(s + 1, "attn")
                                if s < NSTRIP - 1 else (None, None))
                emit_p2b_coarse(s, kev, vev)
                if lnbc is not None:
                    lnbc()
                    ev_xs = lnnorm()
            ev_sup = S.ev("sV")
            S.dma("gpsimd",
                  lambda e: e.dma_start(out=ccin_d.ap()[:, :],
                                        in_=Sst.ap().rearrange("p f q -> p (f q)")),
                  "sDG", waits=[ev_sup])
            S.add("gpsimd",
                  lambda e: e.collective_compute(
                      "AllGather", ALU.bypass,
                      replica_groups=[[0, 1, 2, 3], [4, 5, 6, 7]],
                      ins=[ccin_d.ap().opt()], outs=[ccout_d.ap().opt()]),
                  waits=[S.ev("sDG")], incs=[("sCC", 1)])
            ev_cc = S.ev("sCC")
            ev_xs0 = emit_ln(0, "attn")
            for f in range(NF):
                S.add("vector",
                      lambda e, f_=f: e.tensor_scalar_mul(
                          scr6[:, 2, f_ * 4:(f_ + 1) * 4], cmbm[:, :], tdc[:, f_:f_ + 1]),
                      incs=([("sV", 1)] if f == NF - 1 else []))
            ev_ar = S.ev("sV")
            S.add("scalar", lambda e: e.activation(scr6[:, 2, 0:32], scr6[:, 2, 0:32], AF.Exp),
                  waits=[ev_ar], incs=[("sS", 1)])
            ev_ex = S.ev("sS")
            for f in range(NF):
                S.add("vector",
                      lambda e, f_=f: e.tensor_mul(scr6[:, 2, f_ * 4:(f_ + 1) * 4],
                                                   scr6[:, 2, f_ * 4:(f_ + 1) * 4], cmbk[:, :]),
                      waits=([ev_ex] if f == 0 else ()),
                      incs=([("sV", 1)] if f == NF - 1 else []))
            ev_w = S.ev("sV")
            prev = None
            for q in range(4):
                ev_ld = S.dma("gpsimd",
                              lambda e, q_=q: e.dma_start(
                                  out=arena[:, :], in_=ccout_d.ap()[q_ * 128:(q_ + 1) * 128, :]),
                              "sDG",
                              waits=_flt([ev_cc if q == 0 else None,
                                          ev_w if q == 0 else None, prev]))
                for f in range(NF):
                    if q == 0:
                        S.add("vector",
                              lambda e, f_=f: e.tensor_scalar_mul(
                                  Sst[:, f_, :], exch[:, f_, :], scr6[:, 2, f_ * 4:f_ * 4 + 1]),
                              waits=([ev_ld] if f == 0 else ()),
                              incs=([("sV", 1)] if f == NF - 1 else []))
                    else:
                        S.add("vector",
                              lambda e, f_=f, q_=q: e.scalar_tensor_tensor(
                                  Sst[:, f_, :], exch[:, f_, :],
                                  scr6[:, 2, f_ * 4 + q_:f_ * 4 + q_ + 1],
                                  Sst[:, f_, :], ALU.mult, ALU.add),
                              waits=([ev_ld] if f == 0 else ()),
                              incs=([("sV", 1)] if f == NF - 1 else []))
                prev = S.ev("sV")
            ev_xs = ev_xs0
            for s in range(NSTRIP):
                sl = slice(s * STRIP, (s + 1) * STRIP)
                rg_free = st["rpgp_free"]
                for oc in range(NF):
                    par = oc % 2
                    rdy, wb = wload(("wrg", oc),
                                    wrg_in.ap()[oc].rearrange("w p k q -> p w k q"),
                                    wrg_sc.ap()[oc].rearrange("w p k q -> p w k q"),
                                    wbufW[:, par, :, :, :], evW[par], "sLW%d" % par)
                    w = [rdy] + ([ev_xs] if oc < 2 else []) + evP[4 + par]
                    for kc in range(NF):
                        S.add("tensor",
                              lambda e, p=par, k=kc: e.matmul(
                                  psALL[:, 4 + p, :], wbufW[:, p, 0, k, :], xs[:, k, :],
                                  start=(k == 0), stop=(k == NF - 1)),
                              waits=_flt(w) if kc == 0 else (),
                              incs=([("sT", 1)] if kc == NF - 1 else []))
                    ev_r = S.ev("sT")
                    for kc in range(NF):
                        S.add("tensor",
                              lambda e, p=par, k=kc: e.matmul(
                                  psALL[:, 2 + p, :], wbufW[:, p, 1, k, :], xs[:, k, :],
                                  start=(k == 0), stop=(k == NF - 1)),
                              waits=_flt(evP[2 + par]) if kc == 0 else (),
                              incs=([("sT", 1)] if kc == NF - 1 else []))
                    ev_g = S.ev("sT")
                    evW[par] = [ev_g] + ([wb] if wb else [])
                    S.add("scalar",
                          lambda e, p=par, o=oc: e.copy(rp[:, o, :], psALL[:, 4 + p, :]),
                          waits=_flt([ev_r, rg_free]), incs=[("sS", 1)])
                    evP[4 + par] = [S.ev("sS")]
                    S.add("scalar",
                          lambda e, p=par, o=oc: e.activation(gp[:, o, :], psALL[:, 2 + p, :],
                                                              AF.Silu),
                          waits=[ev_g], incs=[("sS", 1)])
                    evP[2 + par] = [S.ev("sS")]
                st["projT"] = S.ev("sT")
                for ci_ in range(CPS):
                    for f in range(NF):
                        emit_scan(s * CPS + ci_, f, full=True)
                flush_scan()
                aout_done = S.ev("sV")
                lnbc, lnnorm = (emit_ln_stats(s + 1, "attn")
                                if s < NSTRIP - 1 else (None, None))
                for oc in range(NF):
                    if oc == 5 and lnbc is not None:
                        lnbc()
                    par = oc % 2
                    rdy, wb = wload(("wo", oc),
                                    wo_in.ap()[oc].rearrange("w p k q -> p w k q"),
                                    wo_sc.ap()[oc].rearrange("w p k q -> p w k q"),
                                    wbufW[:, par, 0:1, :, :], evW[par], "sLW%d" % par)
                    w = [rdy] + ([aout_done] if oc < 2 else []) + evP[2 + par]
                    for kc in range(NF):
                        S.add("tensor",
                              lambda e, p=par, k=kc: e.matmul(
                                  psALL[:, 2 + p, :], wbufW[:, p, 0, k, :], xs[:, k, :],
                                  start=(k == 0), stop=(k == NF - 1)),
                              waits=_flt(w) if kc == 0 else (),
                              incs=([("sT", 1)] if kc == NF - 1 else []))
                    ev_m = S.ev("sT")
                    evW[par] = [ev_m] + ([wb] if wb else [])
                    S.add("vector",
                          lambda e, p=par, o=oc, s=sl: e.tensor_add(
                              R[:, o, s], R[:, o, s], psALL[:, 2 + p, :]),
                          waits=[ev_m], incs=[("sV", 1)])
                    evP[2 + par] = [S.ev("sV")]
                st["xs_free"] = S.ev("sT")
                st["rpgp_free"] = aout_done
                if lnnorm is not None:
                    ev_xs = lnnorm()

        loads = [
            (s1c, s1_in), (s2c, s2_in), (sh1c, sh1_in), (sh2c, sh2_in),
            (s3c, s3_in), (tdc, td_in), (tfc, tf_in), (tmc, tm_in),
            (idb, idb_in), (idf, idf_in), (maskt, mask_in), (ci, ci_in),
            (onebf, onebf_in), (onerow, onerow_in), (epst, eps_in),
            (cmbm, cmbm_in), (cmbk, cmbk_in), (bdm, bdm_in),
        ]
        for dst, src in loads:
            S.dma("sync", lambda e, d=dst, s=src: e.dma_start(out=d.ap(), in_=s.ap()),
                  "sIO")
        ev_small = S.ev("sIO")
        S.add("vector", lambda e: e.tensor_scalar_mul(ntd[:, :], tdc[:, :], -1.0),
              waits=[ev_small], incs=[("sV", 1)])
        ev_ntd = S.ev("sV")
        S.add("vector", lambda e: e.tensor_scalar_mul(s3h[:, :], s3c[:, :], 0.5))
        for t_ in (Sst, vbd, kmbd, kbd, ats, sbf):
            S.add("vector", lambda e, tt=t_: e.memset(tt.ap(), 0.0))
        S.add("vector", lambda e: e.engine_nop(), incs=[("sV", 1)])
        ev_memset = S.ev("sV")
        S.add("scalar", lambda e: e.activation(dC[:, :], tdc[:, :], AF.Exp, scale=-float(CH)),
              waits=[ev_small])
        S.add("scalar", lambda e: e.activation(e32[:, :], tdc[:, :], AF.Exp, scale=-(CH / 2.0 + 0.5)))
        S.add("scalar", lambda e: e.activation(e315[:, :], tdc[:, :], AF.Exp, scale=-(CH / 2.0 - 0.5)))
        S.add("scalar", lambda e: e.activation(e512[:, :], tdc[:, :], AF.Exp, scale=-512.0))
        for m in range(8):
            S.add("scalar", lambda e, m_=m: e.activation(dpow[:, m_, :], tdc[:, :], AF.Exp,
                                                         scale=-(float(CH) * (7 - m_) + CH / 2.0 - 0.5)),
                  incs=([("sS", 1)] if m == 7 else ()))
        prev_pat = None
        for f in range(NF):
            for which in range(2):
                slot = (f * 2 + which) % 2
                col = ntd if which == 0 else tdc
                S.add("vector",
                      lambda e, f_=f, sl_=slot, c_=col: e.tensor_scalar_mul(
                          scr6[:, sl_, 0:CH], ci[:, :], c_[:, f_:f_ + 1]),
                      waits=_flt([ev_ntd if (f == 0 and which == 0) else None, prev_pat]),
                      incs=[("sV", 1)])
                ev_arg = S.ev("sV")
                S.add("scalar",
                      lambda e, f_=f, w_=which, sl_=slot: e.activation(
                          pat[:, w_, f_, :], scr6[:, sl_, 0:CH], AF.Exp),
                      waits=[ev_arg], incs=[("sS", 1)])
                prev_pat = S.ev("sS")
        ev_ioA = None
        ev_ioB = None
        trp = [None, None]
        for t in range(16):
            ev_xA = S.dma("sync",
                          lambda e, t_=t: e.dma_start(
                              out=arena[:, 0:512],
                              in_=x_in.ap()[t_ * 128:(t_ + 1) * 128, 0:512]),
                          "sIO", waits=_flt([ev_ioA]))
            ev_xB = S.dma("sync",
                          lambda e, t_=t: e.dma_start(
                              out=arena[:, 512:1024],
                              in_=x_in.ap()[t_ * 128:(t_ + 1) * 128, 512:1024]),
                          "sIO", waits=_flt([ev_ioB]))
            for f in range(NF):
                pf = f % 2
                w = ([ev_xA, ev_small] if f == 0 else []) + ([ev_xB] if f == 4 else [])
                S.add("tensor",
                      lambda e, f_=f, p=pf: e.transpose(
                          psALL[:, 4 + p, 0:128], arena[:, f_ * 128:(f_ + 1) * 128], idf[:, :]),
                      waits=_flt(w + [trp[pf]]),
                      incs=[("sT", 1)])
                ev_t = S.ev("sT")
                if f == 3:
                    ev_ioA = ev_t
                if f == NF - 1:
                    ev_ioB = ev_t
                S.add("vector",
                      lambda e, f_=f, p=pf, t_=t: e.tensor_copy(
                          R[:, f_, t_ * 128:(t_ + 1) * 128], psALL[:, 4 + p, 0:128]),
                      waits=[ev_t], incs=[("sV", 1)])
                trp[pf] = S.ev("sV")
        for oc in (0, 1):
            par = oc % 2
            first_use.add(("w12", oc))
            rdy = S.dma("gpsimd",
                        lambda e, o=oc, p=par: e.dma_start(
                            out=wbufW[:, p, :, :, :],
                            in_=w12_in.ap()[o].rearrange("w p k q -> p w k q")),
                        "sGW%d" % par)
            wb = S.dma("sync",
                       lambda e, o=oc, p=par: e.dma_start(
                           out=w12_sc.ap()[o].rearrange("w p k q -> p w k q"),
                           in_=wbufW[:, p, :, :, :]),
                       "sLW%d" % par, waits=[rdy])
            pref[("w12", oc)] = (rdy, wb)
        S.barrier()

        if PH_FFN1:
            emit_ffn(0)
            ln_a0 = emit_ln(0, "attn") if PH_ATTN else None
            S.barrier()
        else:
            ln_a0 = None
        if PH_ATTN:
            emit_attn(ln_a0)
            ln_f2 = emit_ln(0, "ffn") if PH_FFN2 else None
            S.barrier()
        else:
            ln_f2 = None
        if PH_FFN2:
            emit_ffn(1, ln_f2)
            S.barrier()

        trp = [None, None]
        ev_half = [None, None]
        for t in range(16):
            for f in range(NF):
                pf = f % 2
                S.add("tensor",
                      lambda e, f_=f, p=pf, t_=t: e.transpose(
                          psALL[:, 4 + p, 0:128], R[:, f_, t_ * 128:(t_ + 1) * 128], idf[:, :]),
                      waits=_flt([trp[pf]] + (evP[4 + pf] if t == 0 and f < 2 else [])),
                      incs=[("sT", 1)])
                ev_t = S.ev("sT")
                S.add("vector",
                      lambda e, f_=f, p=pf: e.tensor_copy(
                          arena[:, f_ * 128:(f_ + 1) * 128], psALL[:, 4 + p, 0:128]),
                      waits=_flt([ev_t] + ([ev_half[f // 4]] if f % 4 == 0 else [])),
                      incs=[("sV", 1)])
                trp[pf] = S.ev("sV")
                if f == 3 or f == NF - 1:
                    h = f // 4
                    ev_half[h] = S.dma(
                        "sync",
                        lambda e, t_=t, h_=h: e.dma_start(
                            out=y_out.ap()[t_ * 128:(t_ + 1) * 128,
                                           h_ * 512:(h_ + 1) * 512],
                            in_=arena[:, h_ * 512:(h_ + 1) * 512]),
                        "sIO", waits=[trp[f % 2]])
        S.add("sync", lambda e: e.nop(), waits=[("sIO", S.val("sIO"))])

        block = ctx.enter_context(nc.Block())
        for eng in ENGS:
            getattr(block, eng)(lambda e, g=eng: S.emit(e, g, sems))

    return nc


def _tile_w(w):
    O, I = w.shape
    t = w.reshape(O // 128, 128, I // 128, 128)
    return np.ascontiguousarray(t.transpose(0, 3, 2, 1))


def _col128(v):
    return np.ascontiguousarray(np.asarray(v, np.float32).reshape(-1, 128).T)


def kernel(**inputs):
    x = np.asarray(inputs["L0_attn_x"], np.float32)
    B, T, _ = x.shape

    w12 = np.ascontiguousarray(np.stack(
        [_tile_w(np.asarray(inputs["L0_ffn_w1_w"], np.float32)),
         _tile_w(np.asarray(inputs["L0_ffn_w2_w"], np.float32))], axis=1))
    h1 = _tile_w(np.asarray(inputs["L0_ffn_h1_w"], np.float32))
    h2 = _tile_w(np.asarray(inputs["L0_ffn_h2_w"], np.float32))
    w3 = _tile_w(np.asarray(inputs["L0_ffn_w3_w"], np.float32))
    wkv = np.ascontiguousarray(np.stack(
        [_tile_w(np.asarray(inputs["L0_attn_Wk"], np.float32)),
         _tile_w(np.asarray(inputs["L0_attn_Wv"], np.float32))], axis=1))
    wrg = np.ascontiguousarray(np.stack(
        [_tile_w(np.asarray(inputs["L0_attn_Wr"], np.float32)),
         _tile_w(np.asarray(inputs["L0_attn_Wg"], np.float32))], axis=1))
    wo = np.ascontiguousarray(
        _tile_w(np.asarray(inputs["L0_attn_Wo"], np.float32))[:, None])

    s1 = _col128(inputs["L0_ffn_w1_s"])
    s2 = _col128(inputs["L0_ffn_w2_s"])
    sh1 = _col128(inputs["L0_ffn_h1_s"])
    sh2 = _col128(inputs["L0_ffn_h2_s"])
    s3 = _col128(inputs["L0_ffn_w3_s"])
    tdfm = _col128(inputs["L0_attn_time_decay"])
    tffm = _col128(inputs["L0_attn_time_first"])
    tmfm = _col128(inputs["L0_attn_time_mix"])

    idb = np.eye(128, dtype=np.float32).astype(BFNP)
    idf = np.eye(128, dtype=np.float32)
    i_loc = np.arange(CH)
    maskt = np.ascontiguousarray(
        np.tile((i_loc[:, None] <= i_loc[None, :]), (128 // CH, 1)).astype(np.uint8))
    ci = np.ascontiguousarray(
        np.broadcast_to((i_loc - (CH - 1) / 2.0).astype(np.float32),
                        (128, CH))).astype(BFNP)
    onebf = np.ones((128, 1), dtype=BFNP)
    p_i = np.arange(128)
    bdm = np.ascontiguousarray(((p_i[:, None] // 64) == (p_i[None, :] // 64)).astype(np.uint8))
    onerow = np.ones((1, 128), dtype=np.float32)
    epst = np.full((1, 1), 1e-5, dtype=np.float32)

    nc = build_nc()
    in_maps = []
    for c in range(8):
        pos = c % 4
        mvec = np.zeros(4, np.float32)
        kvec = np.zeros(4, np.float32)
        for q in range(pos):
            mvec[q] = -2048.0 * (pos - 1 - q)
            kvec[q] = 1.0
        cmbm = np.ascontiguousarray(np.broadcast_to(mvec, (128, 4)))
        cmbk = np.ascontiguousarray(np.broadcast_to(kvec, (128, 4)))
        shard = np.ascontiguousarray(x[c // 4, pos * TCORE:(pos + 1) * TCORE, :])
        in_maps.append({
            "x": shard, "w12": w12, "h1": h1, "h2": h2, "w3": w3,
            "wkv": wkv, "wrg": wrg, "wo": wo,
            "s1": s1, "s2": s2, "sh1": sh1, "sh2": sh2, "s3": s3,
            "tdfm": tdfm, "tffm": tffm, "tmfm": tmfm,
            "idb": idb, "idf": idf, "maskt": maskt, "ci": ci,
            "onebf": onebf, "onerow": onerow, "epst": epst,
            "cmbm": cmbm, "cmbk": cmbk, "bdm": bdm,
        })
    res = run_bass_kernel_spmd(nc, in_maps, core_ids=list(range(8)))
    y = np.empty((B, T, D), np.float32)
    for c in range(8):
        y[c // 4, (c % 4) * TCORE:((c % 4) + 1) * TCORE, :] = res.results[c]["y"]
    return y


# version 28
# speedup vs baseline: 1.0236x; 1.0008x over previous
import numpy as np
import ml_dtypes
from contextlib import ExitStack

import concourse.bass as bass
import concourse.mybir as mybir
from concourse.bass_utils import run_bass_kernel_spmd

F32 = mybir.dt.float32
BF16 = mybir.dt.bfloat16
U8 = mybir.dt.uint8
AF = mybir.ActivationFunctionType
ALU = mybir.AluOpType
BFNP = ml_dtypes.bfloat16

TCORE = 2048
NSTRIP = 4
STRIP = 512
NF = 8
NHC = 32
CH = 64
CPS = STRIP // CH
D = 1024

PH_FFN1 = True
PH_ATTN = True
PH_FFN2 = True

ENGS = ("tensor", "vector", "scalar", "gpsimd", "sync")
SEMS = ("sT", "sV", "sS", "sG", "sIO", "sDG", "sCC",
        "sLW0", "sLW1", "sLH0", "sLH1", "sGW0", "sGW1", "sGH0", "sGH1")
ENG_SEM = {"tensor": "sT", "vector": "sV", "scalar": "sS", "gpsimd": "sG"}


class Sched:
    def __init__(self):
        self.q = {e: [] for e in ENGS}
        self.cnt = {s: 0 for s in SEMS}

    def val(self, sem):
        return self.cnt[sem]

    def ev(self, sem):
        return (sem, self.cnt[sem])

    def add(self, eng, fn, waits=(), incs=()):
        for s, by in incs:
            self.cnt[s] += by
        w = tuple((s, v) for (s, v) in waits if v is not None and v > 0)
        self.q[eng].append((fn, w, tuple(incs)))

    def dma(self, eng, fn, sem, waits=()):
        w = list(waits) + [(sem, self.cnt[sem])]
        self.add(eng, fn, waits=w, incs=[(sem, 16)])
        return self.ev(sem)

    def barrier(self):
        for e in ("tensor", "vector", "scalar", "gpsimd"):
            s = ENG_SEM[e]
            self.add(e, lambda en: en.nop(), waits=[(s, self.cnt[s])], incs=[(s, 1)])
        vals = [(s, self.cnt[s]) for s in SEMS]
        for e in ENGS:
            self.add(e, lambda en: en.nop(), waits=vals)

    def emit(self, e, eng, sems):
        for fn, waits, incs in self.q[eng]:
            for s, v in waits:
                e.wait_ge(sems[s], v)
            ins = fn(e)
            for s, by in incs:
                ins.then_inc(sems[s], by)


def _flt(evs):
    return [e for e in evs if e is not None]


def build_nc():
    nc = bass.Bass("TRN2", target_bir_lowering=False, detect_race_conditions=False)

    x_in = nc.dram_tensor("x", [TCORE, D], F32, kind="ExternalInput")
    y_out = nc.dram_tensor("y", [TCORE, D], F32, kind="ExternalOutput")

    w12_in = nc.dram_tensor("w12", [NHC, 2, 128, NF, 128], F32, kind="ExternalInput")
    h1_in = nc.dram_tensor("h1", [NHC, 128, NHC, 128], F32, kind="ExternalInput")
    h2_in = nc.dram_tensor("h2", [NHC, 128, NHC, 128], F32, kind="ExternalInput")
    w3_in = nc.dram_tensor("w3", [NF, 128, NHC, 128], F32, kind="ExternalInput")
    wkv_in = nc.dram_tensor("wkv", [NF, 2, 128, NF, 128], F32, kind="ExternalInput")
    wrg_in = nc.dram_tensor("wrg", [NF, 2, 128, NF, 128], F32, kind="ExternalInput")
    wo_in = nc.dram_tensor("wo", [NF, 1, 128, NF, 128], F32, kind="ExternalInput")

    s1_in = nc.dram_tensor("s1", [128, NHC], F32, kind="ExternalInput")
    s2_in = nc.dram_tensor("s2", [128, NHC], F32, kind="ExternalInput")
    sh1_in = nc.dram_tensor("sh1", [128, NHC], F32, kind="ExternalInput")
    sh2_in = nc.dram_tensor("sh2", [128, NHC], F32, kind="ExternalInput")
    s3_in = nc.dram_tensor("s3", [128, NF], F32, kind="ExternalInput")
    td_in = nc.dram_tensor("tdfm", [128, NF], F32, kind="ExternalInput")
    tf_in = nc.dram_tensor("tffm", [128, NF], F32, kind="ExternalInput")
    tm_in = nc.dram_tensor("tmfm", [128, NF], F32, kind="ExternalInput")
    idb_in = nc.dram_tensor("idb", [128, 128], BF16, kind="ExternalInput")
    idf_in = nc.dram_tensor("idf", [128, 128], F32, kind="ExternalInput")
    mask_in = nc.dram_tensor("maskt", [128, CH], U8, kind="ExternalInput")
    ci_in = nc.dram_tensor("ci", [128, CH], BF16, kind="ExternalInput")
    onebf_in = nc.dram_tensor("onebf", [128, 1], BF16, kind="ExternalInput")
    onerow_in = nc.dram_tensor("onerow", [1, 128], F32, kind="ExternalInput")
    eps_in = nc.dram_tensor("epst", [1, 1], F32, kind="ExternalInput")
    cmbm_in = nc.dram_tensor("cmbm", [128, 4], F32, kind="ExternalInput")
    bdm_in = nc.dram_tensor("bdm", [128, 128], U8, kind="ExternalInput")
    cmbk_in = nc.dram_tensor("cmbk", [128, 4], F32, kind="ExternalInput")

    w12_sc = nc.dram_tensor("w12sc", [NHC, 2, 128, NF, 128], BF16)
    h1_sc = nc.dram_tensor("h1sc", [NHC, 128, NHC, 128], BF16)
    h2_sc = nc.dram_tensor("h2sc", [NHC, 128, NHC, 128], BF16)
    w3_sc = nc.dram_tensor("w3sc", [NF, 128, NHC, 128], BF16)
    wkv_sc = nc.dram_tensor("wkvsc", [NF, 2, 128, NF, 128], BF16)
    wrg_sc = nc.dram_tensor("wrgsc", [NF, 2, 128, NF, 128], BF16)
    wo_sc = nc.dram_tensor("wosc", [NF, 1, 128, NF, 128], BF16)
    ccin_d = nc.dram_tensor("ccin", [128, 1024], F32)
    ccout_d = nc.dram_tensor("ccout", [512, 1024], F32)

    S = Sched()

    with ExitStack() as ctx:
        sb = lambda name, sh, dt: ctx.enter_context(nc.sbuf_tensor(name, sh, dt))

        R = sb("R", [128, NF, TCORE], F32)
        big1 = sb("big1", [128, 16384], BF16)
        big2 = sb("big2", [128, 16384], BF16)
        ha = big1.ap().rearrange("p (a b) -> p a b", a=NHC)
        hb = big2.ap().rearrange("p (a b) -> p a b", a=NHC)
        Kv = big1.ap().rearrange("p (a b) -> p a b", a=NF)
        Vv = big2.ap().rearrange("p (a b) -> p a b", a=NF)
        xs = sb("xs", [128, NF, STRIP], BF16)
        rp = sb("rp", [128, NF, STRIP], BF16)
        gp = sb("gp", [128, NF, STRIP], BF16)
        sq2 = sb("sq2", [128, 2, STRIP], BF16)
        rb2 = sb("rb2", [128, 2, STRIP], BF16)
        wbufW = sb("wbufW", [128, 2, 2, NF, 128], BF16)
        wbufH = sb("wbufH", [128, 2, NHC, 128], BF16)
        pat = sb("pat", [128, 2, NF, CH], BF16)
        Sst = sb("Sst", [128, NF, 128], F32)
        sbf = sb("sbf", [128, 2, 128], BF16)
        kmbd = sb("kmbd", [128, 2, 2 * CH], BF16)
        vbd = sb("vbd", [128, 2, 2 * CH], BF16)
        kbd = sb("kbd", [128, 2, 2 * CH], BF16)
        vtkt = sb("vtkt", [2 * CH, 2, 256], BF16)
        qs = sb("qs", [128, 2, CH], BF16)
        kint = sb("kint", [128, 2, CH], BF16)
        ats = sb("ats", [2 * CH, 2, CH], BF16)
        bon1 = sb("bon1", [128, 1, CH], F32)
        scr6 = sb("scr6", [128, 3, STRIP], F32)
        ln_a = sb("ln_a", [1, 2, STRIP], F32)
        ktm2 = sb("ktm2", [128, 512], BF16)
        e512 = sb("e512c", [128, NF], F32)
        dpow = sb("dpow", [128, 8, NF], F32)
        arena = sb("arena", [128, 1024], F32)
        exch = arena.ap().rearrange("p (f q) -> p f q", f=NF)

        s1c = sb("s1c", [128, NHC], F32)
        s2c = sb("s2c", [128, NHC], F32)
        sh1c = sb("sh1c", [128, NHC], F32)
        sh2c = sb("sh2c", [128, NHC], F32)
        s3c = sb("s3c", [128, NF], F32)
        s3h = sb("s3h", [128, NF], F32)
        tdc = sb("tdc", [128, NF], F32)
        tfc = sb("tfc", [128, NF], F32)
        tmc = sb("tmc", [128, NF], F32)
        ntd = sb("ntd", [128, NF], F32)
        dC = sb("dCc", [128, NF], F32)
        e32 = sb("e32c", [128, NF], F32)
        e315 = sb("e315c", [128, NF], F32)
        idb = sb("idbs", [128, 128], BF16)
        idf = sb("idfs", [128, 128], F32)
        maskt = sb("maskts", [128, CH], U8)
        ci = sb("cis", [128, CH], BF16)
        onebf = sb("onebfs", [128, 1], BF16)
        onerow = sb("onerows", [1, 128], F32)
        epst = sb("epsts", [1, 1], F32)
        cmbm = sb("cmbms", [128, 4], F32)
        bdm = sb("bdms", [128, 128], U8)
        cmbk = sb("cmbks", [128, 4], F32)

        psALL = ctx.enter_context(nc.psum_tensor("psALL", [128, 6, 512], F32))
        psSC = ctx.enter_context(nc.psum_tensor("psSC", [128, 2, 1024], BF16))

        sems = {s: ctx.enter_context(nc.semaphore(s)) for s in SEMS}

        evP = [[] for _ in range(6)]
        evW = [[], []]
        evH = [[], []]
        evRB2 = [None, None]
        evSQ2 = [None, None]
        evSilu = [None, None]
        st = {"xs_free": None, "rpgp_free": None, "lna_free": None}
        tr = {k: [None, None] for k in
              ("vbd", "kmbd", "kmbd2", "kbd", "qs", "qs2", "kint", "ats",
               "vtkt", "sbf", "psSC")}
        first_use = set()
        pref = {}

        def wload(key, src_ap, scr_ap, dst_ap, free_evs, slot):
            if key in pref:
                return pref.pop(key)
            gslot = "sG" + slot[2:]
            if key not in first_use:
                first_use.add(key)
                rdy = S.dma("gpsimd",
                            lambda e, d=dst_ap, s=src_ap: e.dma_start(out=d, in_=s),
                            gslot, waits=_flt(free_evs))
                wb = S.dma("sync",
                           lambda e, d=scr_ap, s=dst_ap: e.dma_start(out=d, in_=s),
                           slot, waits=[rdy])
                return rdy, wb
            w = _flt(free_evs)
            if key in castev:
                w.append(castev.pop(key))
            return S.dma("sync",
                         lambda e, d=dst_ap, s=scr_ap: e.dma_start(out=d, in_=s),
                         slot, waits=w), None

        def emit_ln_stats(strip, mode):
            sl = slice(strip * STRIP, (strip + 1) * STRIP)
            for kc in range(NF):
                par = kc % 2
                S.add("vector",
                      lambda e, k=kc, p=par, s=sl: e.tensor_copy(rb2[:, p, :], R[:, k, s]),
                      waits=_flt([evRB2[par]]), incs=[("sV", 1)])
                ev_rb = S.ev("sV")
                S.add("scalar",
                      lambda e, k=kc, p=par, s=sl: e.activation(sq2[:, p, :], R[:, k, s], AF.Square),
                      waits=_flt([evSQ2[par]]), incs=[("sS", 1)])
                ev_sq = S.ev("sS")
                w = [ev_rb] + (evP[0] if kc == 0 else [])
                S.add("tensor",
                      lambda e, p=par, k=kc: e.matmul(
                          psALL[0:1, 0, :], onebf[:, :], rb2[:, p, :],
                          start=(k == 0), stop=(k == NF - 1)),
                      waits=_flt(w))
                w = [ev_sq] + (evP[1] if kc == 0 else [])
                S.add("tensor",
                      lambda e, p=par, k=kc: e.matmul(
                          psALL[0:1, 1, :], onebf[:, :], sq2[:, p, :],
                          start=(k == 0), stop=(k == NF - 1)),
                      waits=_flt(w), incs=[("sT", 1)])
                ev_t = S.ev("sT")
                evRB2[par] = ev_t
                evSQ2[par] = ev_t
            ev_stats = S.ev("sT")
            S.add("vector", lambda e: e.tensor_scalar_mul(ln_a[:, 0, :], psALL[0:1, 0, :], 1.0 / D),
                  waits=_flt([ev_stats, st["lna_free"]]), incs=[("sV", 1)])
            S.add("vector", lambda e: e.tensor_scalar_mul(ln_a[:, 1, :], psALL[0:1, 1, :], 1.0 / D),
                  incs=[("sV", 1)])
            S.add("vector", lambda e: e.tensor_mul(scr6[0:1, 0, :], ln_a[:, 0, :], ln_a[:, 0, :]))
            S.add("vector", lambda e: e.tensor_sub(ln_a[:, 1, :], ln_a[:, 1, :], scr6[0:1, 0, :]),
                  incs=[("sV", 1)])
            ev_var = S.ev("sV")
            S.add("scalar", lambda e: e.activation(ln_a[:, 1, :], ln_a[:, 1, :], AF.Ln,
                                                   bias=epst[:, :]),
                  waits=[ev_var])
            S.add("scalar", lambda e: e.activation(scr6[0:1, 0, :], ln_a[:, 1, :], AF.Exp,
                                                   scale=-0.5),
                  incs=[("sS", 1)])
            ev_std = S.ev("sS")
            S.add("vector", lambda e: e.scalar_tensor_tensor(
                ln_a[:, 0, :], ln_a[:, 0, :], -1.0, scr6[0:1, 0, :], ALU.mult, ALU.mult),
                waits=[ev_std], incs=[("sV", 1)])
            ev_math = S.ev("sV")

            def bc():
                S.add("tensor", lambda e: e.matmul(psALL[:, 0, :], onerow[:, :],
                                                   scr6[0:1, 0, :],
                                                   start=True, stop=True),
                      waits=_flt([ev_math]))
                S.add("tensor", lambda e: e.matmul(psALL[:, 1, :], onerow[:, :],
                                                   ln_a[:, 0, :],
                                                   start=True, stop=True),
                      incs=[("sT", 1)])
                st["lna_free"] = S.ev("sT")

            def norm():
                ev_bc = st["lna_free"]
                for kc in range(NF):
                    w = _flt(([ev_bc] if kc == 0 else []) +
                             ([st["xs_free"]] if kc == 0 else []))
                    S.add("vector",
                          lambda e, k=kc, s=sl: e.tensor_mul(scr6[:, 0, :], R[:, k, s],
                                                             psALL[:, 0, :]),
                          waits=w)
                    if mode == "ffn":
                        S.add("vector",
                              lambda e, k=kc: e.tensor_add(xs[:, k, :], scr6[:, 0, :],
                                                           psALL[:, 1, :]),
                              incs=([("sV", 1)] if kc == NF - 1 else []))
                    else:
                        S.add("vector",
                              lambda e: e.tensor_add(scr6[:, 1, :], scr6[:, 0, :],
                                                     psALL[:, 1, :]))
                        S.add("vector",
                              lambda e, k=kc: e.tensor_scalar_mul(xs[:, k, :], scr6[:, 1, :],
                                                                  tmc[:, k:k + 1]),
                              incs=([("sV", 1)] if kc == NF - 1 else []))
                ev_xs = S.ev("sV")
                evP[0] = [ev_xs]
                evP[1] = [ev_xs]
                return ev_xs

            return bc, norm

        def emit_ln(strip, mode):
            bc, norm = emit_ln_stats(strip, mode)
            bc()
            return norm()

        castev = {}

        def precast_attn_weights():
            items = ([(("wkv", oc), wkv_in.ap()[oc], wkv_sc.ap()[oc])
                      for oc in range(NF)] +
                     [(("wrg", oc), wrg_in.ap()[oc], wrg_sc.ap()[oc])
                      for oc in range(NF)] +
                     [(("wo", oc), wo_in.ap()[oc], wo_sc.ap()[oc])
                      for oc in range(NF)])
            for i, (key, src, dst) in enumerate(items):
                first_use.add(key)
                castev[key] = S.dma(
                    "gpsimd", lambda e, d=dst, s=src: e.dma_start(out=d, in_=s),
                    ("sGW0", "sGW1")[i % 2])

        def emit_ffn(app, pre_xs=None):
            ev_xs = pre_xs if pre_xs is not None else emit_ln(0, "ffn")
            for s in range(NSTRIP):
                sl = slice(s * STRIP, (s + 1) * STRIP)
                if app == 0 and s == 2:
                    precast_attn_weights()
                for oc in range(NHC):
                    par = oc % 2
                    rdy, wb = wload(("w12", oc),
                                    w12_in.ap()[oc].rearrange("w p k q -> p w k q"),
                                    w12_sc.ap()[oc].rearrange("w p k q -> p w k q"),
                                    wbufW[:, par, :, :, :], evW[par], "sLW%d" % par)
                    w = [rdy] + (([ev_xs] if oc < 2 else [])) + evP[0 + par]
                    for kc in range(NF):
                        S.add("tensor",
                              lambda e, p=par, k=kc: e.matmul(
                                  psALL[:, 0 + p, :], wbufW[:, p, 0, k, :], xs[:, k, :],
                                  start=(k == 0), stop=(k == NF - 1)),
                              waits=_flt(w) if kc == 0 else (), incs=([("sT", 1)] if kc == NF - 1 else []))
                    ev_a = S.ev("sT")
                    for kc in range(NF):
                        S.add("tensor",
                              lambda e, p=par, k=kc: e.matmul(
                                  psALL[:, 2 + p, :], wbufW[:, p, 1, k, :], xs[:, k, :],
                                  start=(k == 0), stop=(k == NF - 1)),
                              waits=_flt(evP[2 + par]) if kc == 0 else (),
                              incs=([("sT", 1)] if kc == NF - 1 else []))
                    ev_b = S.ev("sT")
                    evW[par] = [ev_b] + ([wb] if wb else [])
                    S.add("scalar",
                          lambda e, p=par, o=oc: e.activation(
                              scr6[:, 1 + p, :], psALL[:, 0 + p, :], AF.Silu,
                              scale=s1c[:, o:o + 1]),
                          waits=_flt([ev_a, evSilu[par]]), incs=[("sS", 1)])
                    ev_si = S.ev("sS")
                    evP[0 + par] = [ev_si]
                    S.add("vector",
                          lambda e, p=par, o=oc: e.scalar_tensor_tensor(
                              ha[:, o, :], psALL[:, 2 + p, :], s2c[:, o:o + 1],
                              scr6[:, 1 + p, :], ALU.mult, ALU.mult),
                          waits=[ev_b, ev_si], incs=[("sV", 1)])
                    ev_ha = S.ev("sV")
                    evP[2 + par] = [ev_ha]
                    evSilu[par] = ev_ha
                st["xs_free"] = S.ev("sT")
                ha_all = S.ev("sV")
                lnbc, lnnorm = (emit_ln_stats(s + 1, "ffn")
                                if s < NSTRIP - 1 else (None, None))
                for oc in range(NHC):
                    if oc == 4 and lnbc is not None:
                        lnbc()
                        ev_xs = lnnorm()
                    par = oc % 2
                    rdy, wb = wload(("h1", oc), h1_in.ap()[oc], h1_sc.ap()[oc],
                                    wbufH[:, par, :, :], evH[par], "sLH%d" % par)
                    w = [rdy] + ([ha_all] if oc < 2 else []) + evP[4 + par]
                    for kc in range(NHC):
                        S.add("tensor",
                              lambda e, p=par, k=kc: e.matmul(
                                  psALL[:, 4 + p, :], wbufH[:, p, k, :], ha[:, k, :],
                                  start=(k == 0), stop=(k == NHC - 1)),
                              waits=_flt(w) if kc == 0 else (),
                              incs=([("sT", 1)] if kc == NHC - 1 else []))
                    ev_m = S.ev("sT")
                    evH[par] = [ev_m] + ([wb] if wb else [])
                    S.add("scalar",
                          lambda e, p=par, o=oc: e.activation(
                              hb[:, o, :], psALL[:, 4 + p, :], AF.Gelu,
                              scale=sh1c[:, o:o + 1]),
                          waits=[ev_m], incs=[("sS", 1)])
                    evP[4 + par] = [S.ev("sS")]
                hb_all = S.ev("sS")
                for oc in range(NHC):
                    par = oc % 2
                    rdy, wb = wload(("h2", oc), h2_in.ap()[oc], h2_sc.ap()[oc],
                                    wbufH[:, par, :, :], evH[par], "sLH%d" % par)
                    w = [rdy] + ([hb_all] if oc < 2 else []) + evP[4 + par]
                    for kc in range(NHC):
                        S.add("tensor",
                              lambda e, p=par, k=kc: e.matmul(
                                  psALL[:, 4 + p, :], wbufH[:, p, k, :], hb[:, k, :],
                                  start=(k == 0), stop=(k == NHC - 1)),
                              waits=_flt(w) if kc == 0 else (),
                              incs=([("sT", 1)] if kc == NHC - 1 else []))
                    ev_m = S.ev("sT")
                    evH[par] = [ev_m] + ([wb] if wb else [])
                    S.add("scalar",
                          lambda e, p=par, o=oc: e.activation(
                              ha[:, o, :], psALL[:, 4 + p, :], AF.Gelu,
                              scale=sh2c[:, o:o + 1]),
                          waits=[ev_m], incs=[("sS", 1)])
                    evP[4 + par] = [S.ev("sS")]
                hc_all = S.ev("sS")
                for oc in range(NF):
                    par = oc % 2
                    rdy, wb = wload(("w3", oc), w3_in.ap()[oc], w3_sc.ap()[oc],
                                    wbufH[:, par, :, :], evH[par], "sLH%d" % par)
                    w = [rdy] + ([hc_all] if oc < 2 else []) + evP[4 + par]
                    for kc in range(NHC):
                        S.add("tensor",
                              lambda e, p=par, k=kc: e.matmul(
                                  psALL[:, 4 + p, :], wbufH[:, p, k, :], ha[:, k, :],
                                  start=(k == 0), stop=(k == NHC - 1)),
                              waits=_flt(w) if kc == 0 else (),
                              incs=([("sT", 1)] if kc == NHC - 1 else []))
                    ev_m = S.ev("sT")
                    evH[par] = [ev_m] + ([wb] if wb else [])
                    S.add("vector",
                          lambda e, p=par, o=oc, s=sl: e.scalar_tensor_tensor(
                              R[:, o, s], psALL[:, 4 + p, :], s3h[:, o:o + 1],
                              R[:, o, s], ALU.mult, ALU.add),
                          waits=[ev_m], incs=[("sV", 1)])
                    evP[4 + par] = [S.ev("sV")]

        pend = [None]

        def emit_scan(c, f, full, extra_w=()):
            par = (c * 8 + f) % 2
            scol = slice(c * CH, (c + 1) * CH)
            cis = slice((c % CPS) * CH, (c % CPS + 1) * CH)
            KP = Kv[:, f, scol]
            VP = Vv[:, f, scol]
            PQ = pat[:, 0, f, :]
            PK = pat[:, 1, f, :]
            vtv = vtkt[:, par, 0:128]
            ktv = vtkt[:, par, 128:256]
            S.add("vector", lambda e: e.tensor_copy(vbd[0:64, par, 0:CH], VP[0:64, :]),
                  waits=_flt([tr["vbd"][par]] + list(extra_w)))
            S.add("vector", lambda e: e.tensor_copy(vbd[64:128, par, CH:2 * CH], VP[64:128, :]),
                  incs=[("sV", 1)])
            ev_vbd = S.ev("sV")
            S.add("vector", lambda e: e.tensor_mul(qs[:, par, :], KP, PQ),
                  waits=_flt([tr["qs"][par], tr["qs2"][par]]))
            S.add("vector", lambda e: e.tensor_mul(kmbd[0:64, par, 0:CH],
                                                   KP[0:64, :], PK[0:64, :]),
                  waits=_flt([tr["kmbd"][par], tr["kmbd2"][par]]))
            S.add("vector", lambda e: e.tensor_mul(kmbd[64:128, par, CH:2 * CH],
                                                   KP[64:128, :], PK[64:128, :]),
                  incs=[("sV", 1)])
            ev_qkm = S.ev("sV")
            S.add("tensor", lambda e: e.matmul(psALL[0:2 * CH, 0 + par, 0:CH],
                                               kmbd[:, par, :], qs[:, par, :],
                                               start=True, stop=True),
                  waits=_flt([ev_qkm] + evP[0 + par]), incs=[("sT", 1)])
            ev_A = S.ev("sT")
            tr["qs"][par] = ev_A
            tr["kmbd"][par] = ev_A
            S.add("tensor", lambda e: e.transpose(psSC[0:2 * CH, par, 0:128], vbd[:, par, :],
                                                  idb[:, :]),
                  waits=_flt([ev_vbd, tr["psSC"][par]]), incs=[("sT", 1)])
            ev_vt = S.ev("sT")
            tr["vbd"][par] = ev_vt
            S.add("scalar", lambda e: e.activation(
                kbd[0:64, par, 0:CH], kmbd[0:64, par, 0:CH], AF.Copy,
                scale=e315[0:64, f:f + 1]),
                waits=_flt([ev_qkm, tr["kbd"][par]]))
            S.add("scalar", lambda e: e.activation(
                kbd[64:128, par, CH:2 * CH], kmbd[64:128, par, CH:2 * CH], AF.Copy,
                scale=e315[64:128, f:f + 1]),
                incs=[("sS", 1)])
            ev_kbd = S.ev("sS")
            tr["kmbd2"][par] = ev_kbd
            S.add("scalar", lambda e: e.copy(vtkt[:, par, 0:128], psSC[0:2 * CH, par, 0:128]),
                  waits=_flt([ev_vt, tr["vtkt"][par]]), incs=[("sS", 1)])
            ev_vte = S.ev("sS")
            S.add("scalar", lambda e: e.activation(
                kint[:, par, :], qs[:, par, :], AF.Copy, scale=e32[:, f:f + 1]),
                waits=_flt([tr["kint"][par]]), incs=[("sS", 1)])
            ev_sk = S.ev("sS")
            tr["qs2"][par] = ev_sk
            S.add("tensor", lambda e: e.transpose(psSC[0:2 * CH, par, 0:128], kbd[:, par, :],
                                                  idb[:, :]),
                  waits=[ev_kbd, ev_vte], incs=[("sT", 1)])
            ev_kt = S.ev("sT")
            tr["kbd"][par] = ev_kt
            S.add("scalar", lambda e: e.copy(vtkt[:, par, 128:256], psSC[0:2 * CH, par, 0:128]),
                  waits=_flt([ev_kt]), incs=[("sS", 1)])
            ev_tk = S.ev("sS")
            tr["psSC"][par] = ev_tk
            S.add("vector", lambda e: e.copy_predicated(sbf[:, par, :], bdm[:, :],
                                                        Sst[:, f, :]),
                  waits=_flt([tr["sbf"][par]]), incs=[("sV", 1)])
            ev_sbf = S.ev("sV")
            S.add("vector", lambda e: e.copy_predicated(ats[:, par, :], maskt[0:2 * CH, :],
                                                        psALL[0:2 * CH, 0 + par, 0:CH]),
                  waits=_flt([ev_A, tr["ats"][par]]), incs=[("sV", 1)])
            ev_ats = S.ev("sV")
            evP[0 + par] = [ev_ats]
            S.add("tensor", lambda e: e.matmul(psALL[:, 2 + par, 0:CH],
                                               vtv, ats[:, par, :],
                                               start=True, stop=False),
                  waits=_flt([ev_vte, ev_ats, ev_sbf, ev_sk] + evP[2 + par]))
            S.add("tensor", lambda e: e.matmul(psALL[:, 2 + par, 0:CH],
                                               sbf[:, par, :], kint[:, par, :],
                                               start=False, stop=True),
                  incs=[("sT", 1)])
            ev_y = S.ev("sT")
            tr["ats"][par] = ev_y
            tr["sbf"][par] = ev_y
            tr["kint"][par] = ev_y
            S.add("tensor", lambda e: e.matmul(psALL[:, 4 + par, 0:128],
                                               ktv, vtv,
                                               start=True, stop=True),
                  waits=_flt([ev_tk] + evP[4 + par]), incs=[("sT", 1)])
            ev_ds = S.ev("sT")
            tr["vtkt"][par] = ev_ds

            projT = st.get("projT")

            def tail():
                S.add("vector", lambda e: e.scalar_tensor_tensor(
                    Sst[:, f, :], Sst[:, f, :], dC[:, f:f + 1], psALL[:, 4 + par, 0:128],
                    ALU.mult, ALU.add),
                    waits=[ev_ds], incs=[("sV", 1)])
                evP[4 + par] = [S.ev("sV")]
                S.add("vector", lambda e: e.tensor_mul(bon1[:, 0, :], KP, VP))
                S.add("vector", lambda e: e.scalar_tensor_tensor(
                    bon1[:, 0, :], bon1[:, 0, :], tfc[:, f:f + 1],
                    psALL[:, 2 + par, 0:CH], ALU.mult, ALU.add),
                    waits=[ev_y])
                S.add("vector", lambda e: e.tensor_mul(bon1[:, 0, :], bon1[:, 0, :],
                                                       rp[:, f, cis]))
                S.add("vector", lambda e: e.tensor_mul(xs[:, f, cis], bon1[:, 0, :],
                                                       gp[:, f, cis]),
                      waits=_flt([projT]), incs=[("sV", 1)])
                ev_cmb = S.ev("sV")
                evP[2 + par] = [ev_cmb]

            if pend[0] is not None:
                pend[0]()
            pend[0] = tail

        def flush_scan():
            if pend[0] is not None:
                pend[0]()
                pend[0] = None

        def emit_p2b_coarse(s, kev, vev):
            vtm2 = vtkt.ap().rearrange("p a b -> p (a b)")
            for f in range(NF):
                par = f % 2
                evv = [None] * 4
                for t4 in range(4):
                    col = slice(s * STRIP + t4 * 128, s * STRIP + (t4 + 1) * 128)
                    S.add("tensor",
                          lambda e, f_=f, c_=col, p=t4 % 2: e.transpose(
                              psSC[:, p, 0:128], Vv[:, f_, c_], idb[:, :]),
                          waits=_flt([vev if (f == 0 and t4 == 0) else None,
                                      st.get("c2b_%d" % (t4 % 2)),
                                      st.get("c2k_%d" % (t4 % 2))]),
                          incs=[("sT", 1)])
                    ev_t = S.ev("sT")
                    S.add("scalar",
                          lambda e, t_=t4, p=t4 % 2: e.copy(
                              vtm2[:, t_ * 128:(t_ + 1) * 128], psSC[:, p, 0:128]),
                          waits=_flt([ev_t, st.get("c2b_vtm")]), incs=[("sS", 1)])
                    evv[t4] = S.ev("sS")
                    st["c2b_%d" % (t4 % 2)] = evv[t4]
                evk = [None] * 4
                for t4 in range(4):
                    col0 = s * STRIP + t4 * 128
                    for h in range(2):
                        q = t4 * 2 + h
                        S.add("vector",
                              lambda e, f_=f, q_=q, c0=col0, h_=h: e.scalar_tensor_tensor(
                                  kbd[:, 0, h_ * CH:(h_ + 1) * CH],
                                  Kv[:, f_, c0 + h_ * CH:c0 + (h_ + 1) * CH],
                                  dpow[:, q_, f_:f_ + 1], pat[:, 1, f_, :],
                                  ALU.mult, ALU.mult),
                              waits=_flt([kev if (f == 0 and t4 == 0 and h == 0) else None,
                                          st.get("c2b_kbd") if (h == 0) else None]),
                              incs=([("sV", 1)] if h == 1 else ()))
                    ev_kb = S.ev("sV")
                    S.add("tensor",
                          lambda e, p=t4 % 2: e.transpose(
                              psSC[:, p, 128:256], kbd[:, 0, 0:128], idb[:, :]),
                          waits=_flt([ev_kb, st.get("c2k_%d" % (t4 % 2)),
                                      evv[3] if t4 == 0 else None]),
                          incs=[("sT", 1)])
                    ev_t = S.ev("sT")
                    st["c2b_kbd"] = ev_t
                    S.add("scalar",
                          lambda e, t_=t4, p=t4 % 2: e.copy(
                              ktm2[:, t_ * 128:(t_ + 1) * 128], psSC[:, p, 128:256]),
                          waits=_flt([ev_t, st.get("c2b_ktm")]), incs=[("sS", 1)])
                    evk[t4] = S.ev("sS")
                    st["c2k_%d" % (t4 % 2)] = evk[t4]
                for t4 in range(4):
                    S.add("tensor",
                          lambda e, t_=t4, p=par: e.matmul(
                              psALL[:, 4 + p, 0:128],
                              ktm2[:, t_ * 128:(t_ + 1) * 128],
                              vtm2[:, t_ * 128:(t_ + 1) * 128],
                              start=(t_ == 0), stop=(t_ == 3)),
                          waits=_flt(([evk[3], evv[3]] + evP[4 + par]) if t4 == 0 else []),
                          incs=([("sT", 1)] if t4 == 3 else []))
                ev_ds = S.ev("sT")
                st["c2b_vtm"] = ev_ds
                st["c2b_ktm"] = ev_ds
                S.add("vector", lambda e, f_=f, p=par: e.scalar_tensor_tensor(
                    Sst[:, f_, :], Sst[:, f_, :], e512[:, f_:f_ + 1],
                    psALL[:, 4 + p, 0:128], ALU.mult, ALU.add),
                    waits=[ev_ds], incs=[("sV", 1)])
                evP[4 + par] = [S.ev("sV")]
                tr["vtkt"][0] = ev_ds
                tr["vtkt"][1] = ev_ds
                tr["psSC"][0] = S.ev("sV")
                tr["psSC"][1] = S.ev("sV")

        def emit_attn(pre_xs=None):
            ev_xs = pre_xs if pre_xs is not None else emit_ln(0, "attn")
            for s in range(NSTRIP):
                sl = slice(s * STRIP, (s + 1) * STRIP)
                for oc in range(NF):
                    par = oc % 2
                    rdy, wb = wload(("wkv", oc),
                                    wkv_in.ap()[oc].rearrange("w p k q -> p w k q"),
                                    wkv_sc.ap()[oc].rearrange("w p k q -> p w k q"),
                                    wbufW[:, par, :, :, :], evW[par], "sLW%d" % par)
                    w = [rdy] + ([ev_xs] if oc < 2 else []) + evP[4 + par]
                    for kc in range(NF):
                        S.add("tensor",
                              lambda e, p=par, k=kc: e.matmul(
                                  psALL[:, 4 + p, :], wbufW[:, p, 0, k, :], xs[:, k, :],
                                  start=(k == 0), stop=(k == NF - 1)),
                              waits=_flt(w) if kc == 0 else (),
                              incs=([("sT", 1)] if kc == NF - 1 else []))
                    ev_k = S.ev("sT")
                    for kc in range(NF):
                        S.add("tensor",
                              lambda e, p=par, k=kc: e.matmul(
                                  psALL[:, 2 + p, :], wbufW[:, p, 1, k, :], xs[:, k, :],
                                  start=(k == 0), stop=(k == NF - 1)),
                              waits=_flt(evP[2 + par]) if kc == 0 else (),
                              incs=([("sT", 1)] if kc == NF - 1 else []))
                    ev_v = S.ev("sT")
                    evW[par] = [ev_v] + ([wb] if wb else [])
                    S.add("scalar",
                          lambda e, p=par, o=oc, s=sl: e.copy(Kv[:, o, s], psALL[:, 4 + p, :]),
                          waits=[ev_k], incs=[("sS", 1)])
                    evP[4 + par] = [S.ev("sS")]
                    S.add("scalar",
                          lambda e, p=par, o=oc, s=sl: e.copy(Vv[:, o, s], psALL[:, 2 + p, :]),
                          waits=[ev_v], incs=[("sS", 1)])
                    evP[2 + par] = [S.ev("sS")]
                st["xs_free"] = S.ev("sT")
                kev = S.ev("sS")
                vev = S.ev("sS")
                lnbc, lnnorm = (emit_ln_stats(s + 1, "attn")
                                if s < NSTRIP - 1 else (None, None))
                emit_p2b_coarse(s, kev, vev)
                if lnbc is not None:
                    lnbc()
                    ev_xs = lnnorm()
            ev_sup = S.ev("sV")
            S.dma("gpsimd",
                  lambda e: e.dma_start(out=ccin_d.ap()[:, :],
                                        in_=Sst.ap().rearrange("p f q -> p (f q)")),
                  "sDG", waits=[ev_sup])
            S.add("gpsimd",
                  lambda e: e.collective_compute(
                      "AllGather", ALU.bypass,
                      replica_groups=[[0, 1, 2, 3], [4, 5, 6, 7]],
                      ins=[ccin_d.ap().opt()], outs=[ccout_d.ap().opt()]),
                  waits=[S.ev("sDG")], incs=[("sCC", 1)])
            ev_cc = S.ev("sCC")
            ev_xs0 = emit_ln(0, "attn")
            for f in range(NF):
                S.add("vector",
                      lambda e, f_=f: e.tensor_scalar_mul(
                          scr6[:, 2, f_ * 4:(f_ + 1) * 4], cmbm[:, :], tdc[:, f_:f_ + 1]),
                      incs=([("sV", 1)] if f == NF - 1 else []))
            ev_ar = S.ev("sV")
            S.add("scalar", lambda e: e.activation(scr6[:, 2, 0:32], scr6[:, 2, 0:32], AF.Exp),
                  waits=[ev_ar], incs=[("sS", 1)])
            ev_ex = S.ev("sS")
            for f in range(NF):
                S.add("vector",
                      lambda e, f_=f: e.tensor_mul(scr6[:, 2, f_ * 4:(f_ + 1) * 4],
                                                   scr6[:, 2, f_ * 4:(f_ + 1) * 4], cmbk[:, :]),
                      waits=([ev_ex] if f == 0 else ()),
                      incs=([("sV", 1)] if f == NF - 1 else []))
            ev_w = S.ev("sV")
            prev = None
            for q in range(4):
                ev_ld = S.dma("gpsimd",
                              lambda e, q_=q: e.dma_start(
                                  out=arena[:, :], in_=ccout_d.ap()[q_ * 128:(q_ + 1) * 128, :]),
                              "sDG",
                              waits=_flt([ev_cc if q == 0 else None,
                                          ev_w if q == 0 else None, prev]))
                for f in range(NF):
                    if q == 0:
                        S.add("vector",
                              lambda e, f_=f: e.tensor_scalar_mul(
                                  Sst[:, f_, :], exch[:, f_, :], scr6[:, 2, f_ * 4:f_ * 4 + 1]),
                              waits=([ev_ld] if f == 0 else ()),
                              incs=([("sV", 1)] if f == NF - 1 else []))
                    else:
                        S.add("vector",
                              lambda e, f_=f, q_=q: e.scalar_tensor_tensor(
                                  Sst[:, f_, :], exch[:, f_, :],
                                  scr6[:, 2, f_ * 4 + q_:f_ * 4 + q_ + 1],
                                  Sst[:, f_, :], ALU.mult, ALU.add),
                              waits=([ev_ld] if f == 0 else ()),
                              incs=([("sV", 1)] if f == NF - 1 else []))
                prev = S.ev("sV")
            ev_xs = ev_xs0
            for s in range(NSTRIP):
                sl = slice(s * STRIP, (s + 1) * STRIP)
                rg_free = st["rpgp_free"]
                for oc in range(NF):
                    par = oc % 2
                    rdy, wb = wload(("wrg", oc),
                                    wrg_in.ap()[oc].rearrange("w p k q -> p w k q"),
                                    wrg_sc.ap()[oc].rearrange("w p k q -> p w k q"),
                                    wbufW[:, par, :, :, :], evW[par], "sLW%d" % par)
                    w = [rdy] + ([ev_xs] if oc < 2 else []) + evP[4 + par]
                    for kc in range(NF):
                        S.add("tensor",
                              lambda e, p=par, k=kc: e.matmul(
                                  psALL[:, 4 + p, :], wbufW[:, p, 0, k, :], xs[:, k, :],
                                  start=(k == 0), stop=(k == NF - 1)),
                              waits=_flt(w) if kc == 0 else (),
                              incs=([("sT", 1)] if kc == NF - 1 else []))
                    ev_r = S.ev("sT")
                    for kc in range(NF):
                        S.add("tensor",
                              lambda e, p=par, k=kc: e.matmul(
                                  psALL[:, 2 + p, :], wbufW[:, p, 1, k, :], xs[:, k, :],
                                  start=(k == 0), stop=(k == NF - 1)),
                              waits=_flt(evP[2 + par]) if kc == 0 else (),
                              incs=([("sT", 1)] if kc == NF - 1 else []))
                    ev_g = S.ev("sT")
                    evW[par] = [ev_g] + ([wb] if wb else [])
                    S.add("scalar",
                          lambda e, p=par, o=oc: e.copy(rp[:, o, :], psALL[:, 4 + p, :]),
                          waits=_flt([ev_r, rg_free]), incs=[("sS", 1)])
                    evP[4 + par] = [S.ev("sS")]
                    S.add("scalar",
                          lambda e, p=par, o=oc: e.activation(gp[:, o, :], psALL[:, 2 + p, :],
                                                              AF.Silu),
                          waits=[ev_g], incs=[("sS", 1)])
                    evP[2 + par] = [S.ev("sS")]
                st["projT"] = S.ev("sT")
                for ci_ in range(CPS):
                    for f in range(NF):
                        emit_scan(s * CPS + ci_, f, full=True)
                flush_scan()
                aout_done = S.ev("sV")
                lnbc, lnnorm = (emit_ln_stats(s + 1, "attn")
                                if s < NSTRIP - 1 else (None, None))
                for oc in range(NF):
                    if oc == 5 and lnbc is not None:
                        lnbc()
                    par = oc % 2
                    rdy, wb = wload(("wo", oc),
                                    wo_in.ap()[oc].rearrange("w p k q -> p w k q"),
                                    wo_sc.ap()[oc].rearrange("w p k q -> p w k q"),
                                    wbufW[:, par, 0:1, :, :], evW[par], "sLW%d" % par)
                    w = [rdy] + ([aout_done] if oc < 2 else []) + evP[2 + par]
                    for kc in range(NF):
                        S.add("tensor",
                              lambda e, p=par, k=kc: e.matmul(
                                  psALL[:, 2 + p, :], wbufW[:, p, 0, k, :], xs[:, k, :],
                                  start=(k == 0), stop=(k == NF - 1)),
                              waits=_flt(w) if kc == 0 else (),
                              incs=([("sT", 1)] if kc == NF - 1 else []))
                    ev_m = S.ev("sT")
                    evW[par] = [ev_m] + ([wb] if wb else [])
                    S.add("vector",
                          lambda e, p=par, o=oc, s=sl: e.tensor_add(
                              R[:, o, s], R[:, o, s], psALL[:, 2 + p, :]),
                          waits=[ev_m], incs=[("sV", 1)])
                    evP[2 + par] = [S.ev("sV")]
                st["xs_free"] = S.ev("sT")
                st["rpgp_free"] = aout_done
                if lnnorm is not None:
                    ev_xs = lnnorm()

        loads = [
            (s1c, s1_in), (s2c, s2_in), (sh1c, sh1_in), (sh2c, sh2_in),
            (s3c, s3_in), (tdc, td_in), (tfc, tf_in), (tmc, tm_in),
            (idb, idb_in), (idf, idf_in), (maskt, mask_in), (ci, ci_in),
            (onebf, onebf_in), (onerow, onerow_in), (epst, eps_in),
            (cmbm, cmbm_in), (cmbk, cmbk_in), (bdm, bdm_in),
        ]
        for dst, src in loads:
            S.dma("sync", lambda e, d=dst, s=src: e.dma_start(out=d.ap(), in_=s.ap()),
                  "sIO")
        ev_small = S.ev("sIO")
        S.add("vector", lambda e: e.tensor_scalar_mul(ntd[:, :], tdc[:, :], -1.0),
              waits=[ev_small], incs=[("sV", 1)])
        ev_ntd = S.ev("sV")
        S.add("vector", lambda e: e.tensor_scalar_mul(s3h[:, :], s3c[:, :], 0.5))
        for t_ in (Sst, vbd, kmbd, kbd, ats, sbf):
            S.add("vector", lambda e, tt=t_: e.memset(tt.ap(), 0.0))
        S.add("vector", lambda e: e.engine_nop(), incs=[("sV", 1)])
        ev_memset = S.ev("sV")
        S.add("scalar", lambda e: e.activation(dC[:, :], tdc[:, :], AF.Exp, scale=-float(CH)),
              waits=[ev_small])
        S.add("scalar", lambda e: e.activation(e32[:, :], tdc[:, :], AF.Exp, scale=-(CH / 2.0 + 0.5)))
        S.add("scalar", lambda e: e.activation(e315[:, :], tdc[:, :], AF.Exp, scale=-(CH / 2.0 - 0.5)))
        S.add("scalar", lambda e: e.activation(e512[:, :], tdc[:, :], AF.Exp, scale=-512.0))
        for m in range(8):
            S.add("scalar", lambda e, m_=m: e.activation(dpow[:, m_, :], tdc[:, :], AF.Exp,
                                                         scale=-(float(CH) * (7 - m_) + CH / 2.0 - 0.5)),
                  incs=([("sS", 1)] if m == 7 else ()))
        prev_pat = None
        for f in range(NF):
            for which in range(2):
                slot = (f * 2 + which) % 2
                col = ntd if which == 0 else tdc
                S.add("vector",
                      lambda e, f_=f, sl_=slot, c_=col: e.tensor_scalar_mul(
                          scr6[:, sl_, 0:CH], ci[:, :], c_[:, f_:f_ + 1]),
                      waits=_flt([ev_ntd if (f == 0 and which == 0) else None, prev_pat]),
                      incs=[("sV", 1)])
                ev_arg = S.ev("sV")
                S.add("scalar",
                      lambda e, f_=f, w_=which, sl_=slot: e.activation(
                          pat[:, w_, f_, :], scr6[:, sl_, 0:CH], AF.Exp),
                      waits=[ev_arg], incs=[("sS", 1)])
                prev_pat = S.ev("sS")
        ev_ioA = None
        ev_ioB = None
        trp = [None, None]
        for t in range(16):
            ev_xA = S.dma("sync",
                          lambda e, t_=t: e.dma_start(
                              out=arena[:, 0:512],
                              in_=x_in.ap()[t_ * 128:(t_ + 1) * 128, 0:512]),
                          "sIO", waits=_flt([ev_ioA]))
            ev_xB = S.dma("sync",
                          lambda e, t_=t: e.dma_start(
                              out=arena[:, 512:1024],
                              in_=x_in.ap()[t_ * 128:(t_ + 1) * 128, 512:1024]),
                          "sIO", waits=_flt([ev_ioB]))
            for f in range(NF):
                pf = f % 2
                w = ([ev_xA, ev_small] if f == 0 else []) + ([ev_xB] if f == 4 else [])
                S.add("tensor",
                      lambda e, f_=f, p=pf: e.transpose(
                          psALL[:, 4 + p, 0:128], arena[:, f_ * 128:(f_ + 1) * 128], idf[:, :]),
                      waits=_flt(w + [trp[pf]]),
                      incs=[("sT", 1)])
                ev_t = S.ev("sT")
                if f == 3:
                    ev_ioA = ev_t
                if f == NF - 1:
                    ev_ioB = ev_t
                S.add("vector",
                      lambda e, f_=f, p=pf, t_=t: e.tensor_copy(
                          R[:, f_, t_ * 128:(t_ + 1) * 128], psALL[:, 4 + p, 0:128]),
                      waits=[ev_t], incs=[("sV", 1)])
                trp[pf] = S.ev("sV")
        for oc in (0, 1):
            par = oc % 2
            first_use.add(("w12", oc))
            rdy = S.dma("gpsimd",
                        lambda e, o=oc, p=par: e.dma_start(
                            out=wbufW[:, p, :, :, :],
                            in_=w12_in.ap()[o].rearrange("w p k q -> p w k q")),
                        "sGW%d" % par)
            wb = S.dma("sync",
                       lambda e, o=oc, p=par: e.dma_start(
                           out=w12_sc.ap()[o].rearrange("w p k q -> p w k q"),
                           in_=wbufW[:, p, :, :, :]),
                       "sLW%d" % par, waits=[rdy])
            pref[("w12", oc)] = (rdy, wb)
        S.barrier()

        if PH_FFN1:
            emit_ffn(0)
            ln_a0 = emit_ln(0, "attn") if PH_ATTN else None
            S.barrier()
        else:
            ln_a0 = None
        if PH_ATTN:
            emit_attn(ln_a0)
            ln_f2 = emit_ln(0, "ffn") if PH_FFN2 else None
            S.barrier()
        else:
            ln_f2 = None
        if PH_FFN2:
            emit_ffn(1, ln_f2)
            S.barrier()

        trp = [None, None]
        ev_half = [None, None]
        for t in range(16):
            for f in range(NF):
                pf = f % 2
                S.add("tensor",
                      lambda e, f_=f, p=pf, t_=t: e.transpose(
                          psALL[:, 4 + p, 0:128], R[:, f_, t_ * 128:(t_ + 1) * 128], idf[:, :]),
                      waits=_flt([trp[pf]] + (evP[4 + pf] if t == 0 and f < 2 else [])),
                      incs=[("sT", 1)])
                ev_t = S.ev("sT")
                S.add("vector",
                      lambda e, f_=f, p=pf: e.tensor_copy(
                          arena[:, f_ * 128:(f_ + 1) * 128], psALL[:, 4 + p, 0:128]),
                      waits=_flt([ev_t] + ([ev_half[f // 4]] if f % 4 == 0 else [])),
                      incs=[("sV", 1)])
                trp[pf] = S.ev("sV")
                if f == 3 or f == NF - 1:
                    h = f // 4
                    ev_half[h] = S.dma(
                        "sync",
                        lambda e, t_=t, h_=h: e.dma_start(
                            out=y_out.ap()[t_ * 128:(t_ + 1) * 128,
                                           h_ * 512:(h_ + 1) * 512],
                            in_=arena[:, h_ * 512:(h_ + 1) * 512]),
                        "sIO", waits=[trp[f % 2]])
        S.add("sync", lambda e: e.nop(), waits=[("sIO", S.val("sIO"))])

        block = ctx.enter_context(nc.Block())
        for eng in ENGS:
            getattr(block, eng)(lambda e, g=eng: S.emit(e, g, sems))

    return nc


def _tile_w(w):
    O, I = w.shape
    t = w.reshape(O // 128, 128, I // 128, 128)
    return np.ascontiguousarray(t.transpose(0, 3, 2, 1))


def _col128(v):
    return np.ascontiguousarray(np.asarray(v, np.float32).reshape(-1, 128).T)


def kernel(**inputs):
    x = np.asarray(inputs["L0_attn_x"], np.float32)
    B, T, _ = x.shape

    w12 = np.ascontiguousarray(np.stack(
        [_tile_w(np.asarray(inputs["L0_ffn_w1_w"], np.float32)),
         _tile_w(np.asarray(inputs["L0_ffn_w2_w"], np.float32))], axis=1))
    h1 = _tile_w(np.asarray(inputs["L0_ffn_h1_w"], np.float32))
    h2 = _tile_w(np.asarray(inputs["L0_ffn_h2_w"], np.float32))
    w3 = _tile_w(np.asarray(inputs["L0_ffn_w3_w"], np.float32))
    wkv = np.ascontiguousarray(np.stack(
        [_tile_w(np.asarray(inputs["L0_attn_Wk"], np.float32)),
         _tile_w(np.asarray(inputs["L0_attn_Wv"], np.float32))], axis=1))
    wrg = np.ascontiguousarray(np.stack(
        [_tile_w(np.asarray(inputs["L0_attn_Wr"], np.float32)),
         _tile_w(np.asarray(inputs["L0_attn_Wg"], np.float32))], axis=1))
    wo = np.ascontiguousarray(
        _tile_w(np.asarray(inputs["L0_attn_Wo"], np.float32))[:, None])

    s1 = _col128(inputs["L0_ffn_w1_s"])
    s2 = _col128(inputs["L0_ffn_w2_s"])
    sh1 = _col128(inputs["L0_ffn_h1_s"])
    sh2 = _col128(inputs["L0_ffn_h2_s"])
    s3 = _col128(inputs["L0_ffn_w3_s"])
    tdfm = _col128(inputs["L0_attn_time_decay"])
    tffm = _col128(inputs["L0_attn_time_first"])
    tmfm = _col128(inputs["L0_attn_time_mix"])

    idb = np.eye(128, dtype=np.float32).astype(BFNP)
    idf = np.eye(128, dtype=np.float32)
    i_loc = np.arange(CH)
    maskt = np.ascontiguousarray(
        np.tile((i_loc[:, None] <= i_loc[None, :]), (128 // CH, 1)).astype(np.uint8))
    ci = np.ascontiguousarray(
        np.broadcast_to((i_loc - (CH - 1) / 2.0).astype(np.float32),
                        (128, CH))).astype(BFNP)
    onebf = np.ones((128, 1), dtype=BFNP)
    p_i = np.arange(128)
    bdm = np.ascontiguousarray(((p_i[:, None] // 64) == (p_i[None, :] // 64)).astype(np.uint8))
    onerow = np.ones((1, 128), dtype=np.float32)
    epst = np.full((1, 1), 1e-5, dtype=np.float32)

    nc = build_nc()
    in_maps = []
    for c in range(8):
        pos = c % 4
        mvec = np.zeros(4, np.float32)
        kvec = np.zeros(4, np.float32)
        for q in range(pos):
            mvec[q] = -2048.0 * (pos - 1 - q)
            kvec[q] = 1.0
        cmbm = np.ascontiguousarray(np.broadcast_to(mvec, (128, 4)))
        cmbk = np.ascontiguousarray(np.broadcast_to(kvec, (128, 4)))
        shard = np.ascontiguousarray(x[c // 4, pos * TCORE:(pos + 1) * TCORE, :])
        in_maps.append({
            "x": shard, "w12": w12, "h1": h1, "h2": h2, "w3": w3,
            "wkv": wkv, "wrg": wrg, "wo": wo,
            "s1": s1, "s2": s2, "sh1": sh1, "sh2": sh2, "s3": s3,
            "tdfm": tdfm, "tffm": tffm, "tmfm": tmfm,
            "idb": idb, "idf": idf, "maskt": maskt, "ci": ci,
            "onebf": onebf, "onerow": onerow, "epst": epst,
            "cmbm": cmbm, "cmbk": cmbk, "bdm": bdm,
        })
    res = run_bass_kernel_spmd(nc, in_maps, core_ids=list(range(8)))
    y = np.empty((B, T, D), np.float32)
    for c in range(8):
        y[c // 4, (c % 4) * TCORE:((c % 4) + 1) * TCORE, :] = res.results[c]["y"]
    return y
